# version 06
# baseline (speedup 1.0000x reference)
import math
from contextlib import ExitStack

import numpy as np
import ml_dtypes

import concourse.bass as bass
import concourse.mybir as mybir
from concourse.bass_utils import run_bass_kernel_spmd

F32 = mybir.dt.float32
BF16 = mybir.dt.bfloat16
AF = mybir.ActivationFunctionType
ALU = mybir.AluOpType
BF = ml_dtypes.bfloat16

D = 128
PD = 128


class Cfg:
    def __init__(self, E, H, KVH, S, R, T):
        self.E, self.H, self.KVH, self.S, self.R, self.T = E, H, KVH, S, R, T
        self.REP = H // KVH
        self.FC = E // PD
        self.KC = S // PD
        self.QB = min(512, T)
        self.NQB = T // self.QB
        self.PW = 2 * self.QB
        self.KVDIM = KVH * D
        self.SBT = min(1024, S)
        self.NSB = S // self.SBT
        self.NPAIR = self.REP // 2
        assert self.REP == 4 and E % PD == 0 and S % PD == 0 and T % self.QB == 0
        self.scale = 1.0 / math.sqrt(float(D))
        self.expbias = -20.0


FULL = Cfg(E=4096, H=32, KVH=8, S=8192, R=8, T=1024)

TICK = {"vector": "dve", "gpsimd": "pool", "scalar": "act", "tensor": "pe"}
CHAINED = {"vector", "gpsimd", "scalar"}


class Prog:
    ENGS = ("sync", "scalar", "vector", "tensor", "gpsimd")

    def __init__(self, nc, stack):
        self.nc = nc
        self.stack = stack
        self.q = {k: [] for k in self.ENGS}
        self.handles = {}
        self.cnt = {}
        self.lastw = {}
        self.enabled = True

    def cut(self, label, stop_after):
        if stop_after is not None and label == stop_after:
            self.enabled = False

    def sem(self, name):
        if name not in self.handles:
            self.handles[name] = self.stack.enter_context(self.nc.semaphore(name))
            self.cnt.setdefault(name, 0)
        return self.handles[name]

    def wait(self, eng, name, val):
        if val is None or val <= 0:
            return
        if self.lastw.get((eng, name), 0) >= val:
            return
        assert val <= self.cnt.get(name, 0), (
            f"wait({eng},{name},{val}) > scheduled {self.cnt.get(name, 0)}")
        self.lastw[(eng, name)] = val
        h = self.sem(name)
        if self.enabled:
            self.q[eng].append(lambda e, h=h, v=val: e.wait_ge(h, v))

    def do(self, eng, fn, waits=(), sem=None, inc=1):
        if eng in CHAINED:
            t = TICK[eng]
            if sem is None:
                sem = t
            if sem == t and self.cnt.get(t, 0) > 0:
                self.wait(eng, t, self.cnt[t])
        for wv in waits:
            if wv is None or wv[1] is None:
                continue
            self.wait(eng, wv[0], wv[1])
        if sem is None:
            if self.enabled:
                self.q[eng].append(fn)
            return None
        h = self.sem(sem)
        self.cnt[sem] += inc
        tok = self.cnt[sem]
        if self.enabled:
            self.q[eng].append(lambda e, h=h, fn=fn, inc=inc: fn(e).then_inc(h, inc))
        return tok

    def barrier(self):
        return [(s, self.cnt[s]) for s in ("pe", "act", "dve", "pool")
                if self.cnt.get(s, 0) > 0]

    def finalize(self, block):
        for eng in self.ENGS:
            fns = self.q[eng]

            def runner(e, fns=fns):
                for f in fns:
                    f(e)

            getattr(block, eng)(runner)


def build(cfg: Cfg, stop_after=None, dump=None):
    nc = bass.Bass("TRN2", target_bir_lowering=False)
    E, H, KVH, S, T = cfg.E, cfg.H, cfg.KVH, cfg.S, cfg.T
    FC, KC, QB, NQB, PW = cfg.FC, cfg.KC, cfg.QB, cfg.NQB, cfg.PW
    KVDIM, SBT, NSB, REP, NPAIR = cfg.KVDIM, cfg.SBT, cfg.NSB, cfg.REP, cfg.NPAIR

    x_in = nc.dram_tensor("x", [S, E], F32, kind="ExternalInput").ap()
    wq_in = nc.dram_tensor("wq", [E, E], F32, kind="ExternalInput").ap()
    wk_in = nc.dram_tensor("wk", [E, KVDIM], F32, kind="ExternalInput").ap()
    wv_in = nc.dram_tensor("wv", [E, KVDIM], F32, kind="ExternalInput").ap()
    wpos_in = nc.dram_tensor("wpos", [E, E], F32, kind="ExternalInput").ap()
    wneg_in = nc.dram_tensor("wneg", [E, E], F32, kind="ExternalInput").ap()
    bqT_in = nc.dram_tensor("bqT", [PD, FC], F32, kind="ExternalInput").ap()
    bkT_in = nc.dram_tensor("bkT", [PD, KVH], F32, kind="ExternalInput").ap()
    bv_in = nc.dram_tensor("bv", [1, KVDIM], F32, kind="ExternalInput").ap()
    cosq_in = nc.dram_tensor("cosq", [PD, T], F32, kind="ExternalInput").ap()
    sinq_in = nc.dram_tensor("sinq", [PD, T], F32, kind="ExternalInput").ap()
    cosk_in = nc.dram_tensor("cosk", [PD, S], F32, kind="ExternalInput").ap()
    sink_in = nc.dram_tensor("sink", [PD, S], F32, kind="ExternalInput").ap()
    perm_in = nc.dram_tensor("perm", [PD, PD], F32, kind="ExternalInput").ap()
    ident_in = nc.dram_tensor("ident", [PD, PD], BF16, kind="ExternalInput").ap()
    out_t = nc.dram_tensor("out_t", [E, T], F32, kind="ExternalOutput").ap()

    xT_dram = nc.dram_tensor("xT_dram", [E, S], BF16)
    kT_dram = nc.dram_tensor("kT_dram", [KVDIM, S], BF16)
    v_dram = nc.dram_tensor("v_dram", [S, KVDIM], BF16)
    xT_r = xT_dram.ap().rearrange("(fc p) t -> p fc t", p=PD)
    v_w = v_dram.ap().rearrange("(b p) kv -> p b kv", p=PD)

    with ExitStack() as top:
        P = Prog(nc, top)
        e = top.enter_context

        consts = e(nc.sbuf_tensor("consts", [PD, 8], F32))
        CONSTS = {}
        for v in (0.0, 1.0, cfg.scale, cfg.expbias, 1e-5, -0.5):
            v = float(v)
            if v not in CONSTS:
                i = len(CONSTS)
                CONSTS[v] = i
                nc.const_aps.register(F32, v, consts[:, i:i + 1])

        ones1f = e(nc.sbuf_tensor("ones1f", [1, PD], F32))
        ones128b = e(nc.sbuf_tensor("ones128b", [PD, 1], BF16))
        ones32b = e(nc.sbuf_tensor("ones32b", [32, 1], BF16))
        identb = e(nc.sbuf_tensor("identb", [PD, PD], BF16))
        bqT = e(nc.sbuf_tensor("bqT_s", [PD, FC], F32))
        bkT = e(nc.sbuf_tensor("bkT_s", [PD, KVH], F32))
        bv_row = e(nc.sbuf_tensor("bv_row", [1, KVDIM], F32))
        bvb = e(nc.sbuf_tensor("bvb", [PD, KVDIM], F32))
        permf = e(nc.sbuf_tensor("permf", [PD, PD], F32))

        for dst, src in ((identb, ident_in), (bqT, bqT_in), (bkT, bkT_in),
                         (bv_row, bv_in), (permf, perm_in)):
            P.do("sync", lambda s, d=dst, sp=src: s.dma_start(out=d[:], in_=sp),
                 sem="dtab", inc=16)
        NTAB = P.cnt["dtab"]

        for v, i in CONSTS.items():
            P.do("vector", lambda en, ap=consts[:, i:i + 1], v=v: en.memset(ap, v))
        P.do("vector", lambda en: en.memset(ones1f[:], 1.0))
        P.do("vector", lambda en: en.memset(ones128b[:], 1.0))
        tok_setup = P.do("vector", lambda en: en.memset(ones32b[:], 1.0))


        with nc.psum_tensor("ps_bv", [PD, 512], F32) as ps_bv:
            Wv = min(512, KVDIM)
            tok_prev = None
            for i in range(KVDIM // Wv):
                sl = slice(i * Wv, (i + 1) * Wv)
                tok_mm = P.do(
                    "tensor",
                    lambda t, sl=sl: t.matmul(ps_bv[:, 0:Wv], ones1f[:],
                                              bv_row[:, sl], start=True, stop=True),
                    waits=[("dtab", NTAB), ("dve", tok_setup),
                           ("dve", tok_prev) if tok_prev else None],
                    sem="pe")
                tok_prev = P.do(
                    "vector",
                    lambda v, sl=sl: v.tensor_copy(bvb[:, sl], ps_bv[:, 0:Wv]),
                    waits=[("pe", tok_mm)])
            tok_bvb = tok_prev

        P.cut("setup", stop_after)
        NTB = S // PD
        TGW = 4 * PD
        gdA = P.barrier()
        with ExitStack() as pa:
            ea = pa.enter_context
            xraw = [ea(nc.sbuf_tensor(f"xraw{i}", [PD, E], BF16)) for i in range(2)]
            stage = [ea(nc.sbuf_tensor(f"xstg{i}", [PD, FC, TGW], BF16)) for i in range(2)]
            ps_tr = [ea(nc.psum_tensor(f"ps_trA{i}", [PD, 1024], BF16)) for i in range(2)]

            tok_xr_use = {0: None, 1: None}
            tok_stg_st = {0: None, 1: None}
            tok_ptr_rd = {0: None, 1: None}

            for tg in range(NTB // 4):
                gpar = tg % 2
                last_ev = None
                for ti in range(4):
                    tb = tg * 4 + ti
                    par = tb % 2
                    tok_ld = P.do(
                        "gpsimd",
                        lambda g, tb=tb, par=par: g.dma_start(
                            out=xraw[par][:], in_=x_in[tb * PD:(tb + 1) * PD, :]),
                        waits=[("pe", tok_xr_use[par]) if tok_xr_use[par] else None],
                        sem=f"dx{par}", inc=16)
                    for fc in range(FC):
                        pp = fc % 2
                        tok_tr = P.do(
                            "tensor",
                            lambda t, par=par, fc=fc, pp=pp: t.transpose(
                                ps_tr[pp][:, 0:PD],
                                xraw[par][:, fc * PD:(fc + 1) * PD], identb[:]),
                            waits=[(f"dx{par}", tok_ld), ("dtab", NTAB),
                                   ("act", tok_ptr_rd[pp]) if tok_ptr_rd[pp] else None]
                                  + (gdA if tb == 0 else []),
                            sem="pe")
                        tok_ev = P.do(
                            "scalar",
                            lambda s, gpar=gpar, fc=fc, pp=pp, ti=ti: s.copy(
                                stage[gpar][:, fc, ti * PD:(ti + 1) * PD],
                                ps_tr[pp][:, 0:PD]),
                            waits=[("pe", tok_tr),
                                   (f"dso{gpar}", tok_stg_st[gpar])
                                   if tok_stg_st[gpar] else None])
                        tok_ptr_rd[pp] = tok_ev
                        last_ev = tok_ev
                    tok_xr_use[par] = P.cnt["pe"]
                nfc = max(1, FC // 4)
                for j in range(0, FC, nfc):
                    tok_st = P.do(
                        "sync",
                        lambda s, tg=tg, gpar=gpar, j=j, nfc=nfc: s.dma_start(
                            out=xT_r[:, j:j + nfc, tg * TGW:(tg + 1) * TGW],
                            in_=stage[gpar][:, j:j + nfc, :]),
                        waits=[("act", last_ev)], sem=f"dso{gpar}", inc=16)
                tok_stg_st[gpar] = tok_st
            NDSO0 = P.cnt.get("dso0", 0)
            NDSO1 = P.cnt.get("dso1", 0)

        P.cut("A", stop_after)
        def proj_pass(kind):
            gdB = P.barrier()
            with ExitStack() as pb:
                eb = pb.enter_context
                if kind == "K":
                    wb = eb(nc.sbuf_tensor("wkb", [PD, FC, KVDIM], BF16))
                    w_src = wk_in.rearrange("(fc p) kv -> p fc kv", p=PD)
                    cosk = eb(nc.sbuf_tensor("cosk_s", [PD, SBT], F32))
                    sink = eb(nc.sbuf_tensor("sink_s", [PD, SBT], F32))
                    kraw = [eb(nc.sbuf_tensor(f"kraw{i}", [PD, 512], F32)) for i in range(2)]
                    rsc = [eb(nc.sbuf_tensor(f"rsc{i}", [PD, 512], F32)) for i in range(2)]
                    ps_sw = [eb(nc.psum_tensor(f"ps_swK{i}", [PD, 512], F32))
                             for i in range(2)]
                    kst = [eb(nc.sbuf_tensor(f"kst{i}", [PD, SBT], BF16)) for i in range(2)]

                else:
                    wb = eb(nc.sbuf_tensor("wvb", [PD, FC, KVDIM], BF16))
                    w_src = wv_in.rearrange("(fc p) kv -> p fc kv", p=PD)
                    TB_G = min(8, SBT // PD)
                    vst = [eb(nc.sbuf_tensor(f"vst{i}", [PD, TB_G, KVDIM], BF16))
                           for i in range(2)]
                xsup = eb(nc.sbuf_tensor(f"xsup{kind}", [PD, FC, SBT], BF16))
                psb = [eb(nc.psum_tensor(f"psB{kind}{i}", [PD, 512], F32))
                       for i in range(4)]
                ntab_now = P.cnt["dtab"]

                wsem = f"dw{kind}"
                step = max(1, FC // 4)
                for fc2 in range(0, FC, step):
                    P.do("gpsimd", lambda g, fc2=fc2: g.dma_start(
                        out=wb[:, fc2:fc2 + step, :], in_=w_src[:, fc2:fc2 + step, :]),
                        waits=gdB + [("dks0", P.cnt.get("dks0", 0)),
                                     ("dks1", P.cnt.get("dks1", 0))],
                        sem=wsem, inc=16)
                nw = P.cnt[wsem]

                psb_free = [None] * 4
                tok_stage_st = {0: None, 1: None}
                st_sem = "dks" if kind == "K" else "dvs"
                xsup_pe_done = None
                kraw_rd = {0: None, 1: None}
                sw_free = {0: None, 1: None}
                ck_rope_done = None

                for sb in range(NSB):
                    nl = max(1, FC // 8)
                    for j in range(0, FC, nl):
                        P.do("sync", lambda s, sb=sb, j=j, nl=nl: s.dma_start(
                            out=xsup[:, j:j + nl, :],
                            in_=xT_r[:, j:j + nl, sb * SBT:(sb + 1) * SBT]),
                            waits=gdB + [("dso0", NDSO0), ("dso1", NDSO1),
                                   ("pe", xsup_pe_done) if xsup_pe_done else None],
                            sem=f"dxs{kind}", inc=16)
                    tok_xs = P.cnt[f"dxs{kind}"]

                    if kind == "K":
                        P.do("sync", lambda s, sb=sb: s.dma_start(
                            out=cosk[:], in_=cosk_in[:, sb * SBT:(sb + 1) * SBT]),
                            waits=gdB + [("dso0", NDSO0), ("dso1", NDSO1),
                                         ("dve", ck_rope_done) if ck_rope_done else None],
                            sem="dck", inc=16)
                        P.do("sync", lambda s, sb=sb: s.dma_start(
                            out=sink[:], in_=sink_in[:, sb * SBT:(sb + 1) * SBT]),
                            sem="dck", inc=16)
                        tok_ck = P.cnt["dck"]
                        TQK = SBT // 512
                        for g in range(KVH):
                            spar = (sb * KVH + g) % 2
                            last_rope = None
                            for tqp in range(0, max(2, TQK), 2):
                                bks = [tqp % 4, (tqp % 4) + 1][:min(2, TQK)]
                                for fc in range(FC):
                                    for jj, bk in enumerate(bks):
                                        P.do("tensor", lambda t, g=g, fc=fc, tq=tqp + jj, bk=bk: t.matmul(
                                            psb[bk][:],
                                            wb[:, fc, g * PD:(g + 1) * PD],
                                            xsup[:, fc, tq * 512:(tq + 1) * 512],
                                            start=(fc == 0), stop=(fc == FC - 1)),
                                            waits=[("dxs" + kind, tok_xs), (wsem, nw),
                                                   psb_free[bk] if fc == 0 else None],
                                            sem="pe")
                                tok_mm = P.cnt["pe"]
                                for jj, bk in enumerate(bks):
                                    tq = tqp + jj
                                    rp = tq % 2
                                    tok_ev = P.do("scalar", lambda s, bk=bk, g=g, rp=rp: s.add(
                                        kraw[rp][:], psb[bk][:], bkT[:, g:g + 1]),
                                        waits=[("pe", tok_mm), ("dtab", ntab_now)]
                                              + (kraw_rd[rp] or []))
                                    psb_free[bk] = ("act", tok_ev)
                                    cs = slice(tq * 512, (tq + 1) * 512)
                                    dst = kst[spar][:, tq * 512:(tq + 1) * 512]
                                    sp = tq % 2
                                    tok_sw = P.do("tensor", lambda t, rp=rp, sp=sp: t.matmul(
                                        ps_sw[sp][:], permf[:], kraw[rp][:],
                                        start=True, stop=True, skip_group_check=True),
                                        waits=[("act", tok_ev), ("dtab", NTAB),
                                               sw_free[sp]],
                                        sem="pe")
                                    w0 = [("act", tok_ev), ("dck", tok_ck),
                                          (st_sem + str(spar), tok_stage_st[spar])
                                          if (tok_stage_st[spar] and tq == 0) else None]
                                    tp1 = P.do("vector", lambda v, rp=rp, cs=cs: v.tensor_mul(
                                        rsc[0][:], kraw[rp][:], cosk[:, cs]),
                                        waits=w0)
                                    tp2 = P.do("vector", lambda v, sp=sp, cs=cs: v.tensor_mul(
                                        rsc[1][:], ps_sw[sp][:], sink[:, cs]),
                                        waits=[("pe", tok_sw)])
                                    sw_free[sp] = ("dve", tp2)
                                    to1 = P.do("vector", lambda v, dst=dst: v.tensor_add(
                                        dst[:], rsc[0][:], rsc[1][:]))
                                    kraw_rd[rp] = [("pe", tok_sw), ("dve", tp1)]
                                    last_rope = to1
                                    ck_rope_done = to1
                            tok_st = P.do("sync", lambda s, g=g, sb=sb, spar=spar: s.dma_start(
                                out=kT_dram.ap()[g * PD:(g + 1) * PD,
                                                 sb * SBT:(sb + 1) * SBT],
                                in_=kst[spar][:]),
                                waits=[("dve", last_rope)],
                                sem=st_sem + str(spar), inc=16)
                            tok_stage_st[spar] = tok_st
                    else:
                        NTBS = SBT // PD
                        TB_G = min(8, NTBS)
                        NDV = max(1, KVDIM // 512)
                        Wv = min(512, KVDIM)
                        for tb8 in range(0, NTBS, TB_G):
                            par = (sb * (NTBS // TB_G) + tb8 // TB_G) % 2
                            last_vev = None
                            for tb in range(tb8, tb8 + TB_G):
                                banks = [(tb % 2) * 2 + i for i in range(NDV)]
                                for fc in range(FC):
                                    for dvh in range(NDV):
                                        bk = banks[dvh]
                                        P.do("tensor", lambda t, tb=tb, fc=fc, dvh=dvh, bk=bk: t.matmul(
                                            psb[bk][:, 0:Wv],
                                            xsup[:, fc, tb * PD:(tb + 1) * PD],
                                            wb[:, fc, dvh * 512:dvh * 512 + Wv],
                                            start=(fc == 0), stop=(fc == FC - 1)),
                                            waits=[("dxs" + kind, tok_xs), (wsem, nw),
                                                   psb_free[bk] if fc == 0 else None,
                                                   (st_sem + str(par), tok_stage_st[par])
                                                   if (fc == 0 and tb == tb8 and tok_stage_st[par]) else None],
                                            sem="pe")
                                tok_mm = P.cnt["pe"]
                                for dvh in range(NDV):
                                    bk = banks[dvh]
                                    tok_ev = P.do("vector", lambda v, tb=tb, tb8=tb8, dvh=dvh, bk=bk, par=par: v.tensor_add(
                                        vst[par][:, tb - tb8, dvh * 512:dvh * 512 + Wv],
                                        psb[bk][:, 0:Wv],
                                        bvb[:, dvh * 512:dvh * 512 + Wv]),
                                        waits=[("pe", tok_mm), ("dve", tok_bvb)])
                                    psb_free[bk] = ("dve", tok_ev)
                                    last_vev = tok_ev
                            tok_st = P.do("sync", lambda s, sb=sb, tb8=tb8, par=par: s.dma_start(
                                out=v_w[:, sb * NTBS + tb8:sb * NTBS + tb8 + TB_G, :],
                                in_=vst[par][:]),
                                waits=[("dve", last_vev)],
                                sem=st_sem + str(par), inc=16)
                            tok_stage_st[par] = tok_st
                    xsup_pe_done = P.cnt["pe"]

        proj_pass("K")
        proj_pass("V")
        NKS0, NKS1 = P.cnt.get("dks0", 0), P.cnt.get("dks1", 0)
        NVS0, NVS1 = P.cnt.get("dvs0", 0), P.cnt.get("dvs1", 0)

        if dump in ("kT", "v", "xT"):
            if dump == "kT":
                o = out_t.rearrange("(a b) t -> a (b t)", b=8192 // T)
                P.do("gpsimd", lambda g, o=o: g.dma_start(
                    out=o, in_=kT_dram.ap()[0:KVDIM // 2, :]),
                    waits=[("dks0", P.cnt.get("dks0", 0)), ("dks1", P.cnt.get("dks1", 0))],
                    sem="ddump", inc=16)
            elif dump == "v":
                P.do("gpsimd", lambda g: g.dma_start(
                    out=out_t, in_=v_dram.ap()[0:E, :]),
                    waits=[("dvs0", P.cnt.get("dvs0", 0)), ("dvs1", P.cnt.get("dvs1", 0))],
                    sem="ddump", inc=16)
            else:
                P.do("gpsimd", lambda g: g.dma_start(
                    out=out_t, in_=xT_dram.ap()[:, 0:T]),
                    waits=[("dso0", NDSO0), ("dso1", NDSO1)],
                    sem="ddump", inc=16)
            P.do("gpsimd", lambda g: g.nop(), waits=[("ddump", 16)])
            P.cut("DUMP", "DUMP")
        P.cut("B", stop_after)
        sT = e(nc.sbuf_tensor("sT", [PD, FC, T], BF16))
        gdT = P.barrier()
        nl = max(1, FC // 4)
        for j in range(0, FC, nl):
            P.do("sync", lambda s, j=j, nl=nl: s.dma_start(
                out=sT[:, j:j + nl, :], in_=xT_r[:, j:j + nl, 0:T]),
                waits=gdT + [("dso0", NDSO0), ("dso1", NDSO1),
                             ("dks0", NKS0), ("dks1", NKS1),
                             ("dvs0", NVS0), ("dvs1", NVS1)],
                sem="dsT", inc=16)
        NDST = P.cnt["dsT"]

        pcd = ExitStack()
        qT = pcd.enter_context(nc.sbuf_tensor("qT", [PD, H, T], BF16))
        cosq = pcd.enter_context(nc.sbuf_tensor("cosq_s", [PD, T], F32))
        sinq = pcd.enter_context(nc.sbuf_tensor("sinq_s", [PD, T], F32))
        gdC = P.barrier()
        P.do("sync", lambda s: s.dma_start(out=cosq[:], in_=cosq_in),
             waits=gdC + [("dks0", NKS0), ("dks1", NKS1),
                          ("dvs0", NVS0), ("dvs1", NVS1)],
             sem="dcq", inc=16)
        P.do("sync", lambda s: s.dma_start(out=sinq[:], in_=sinq_in),
             sem="dcq", inc=16)
        NCQ = P.cnt["dcq"]
        with ExitStack() as pc:
            ec = pc.enter_context
            HB = 2
            wqb = [ec(nc.sbuf_tensor(f"wqb{i}", [PD, FC, HB * PD], BF16)) for i in range(2)]
            qraw = [ec(nc.sbuf_tensor(f"qraw{i}", [PD, T], F32)) for i in range(2)]
            rscq = [ec(nc.sbuf_tensor(f"rscq{i}", [PD, T], F32)) for i in range(2)]
            psc = [ec(nc.psum_tensor(f"psC{i}", [PD, 512], F32)) for i in range(4)]
            ps_swq = [ec(nc.psum_tensor(f"ps_swQ{i}", [PD, 512], F32)) for i in range(2)]
            wq_r = wq_in.rearrange("(fc p) o -> p fc o", p=PD)

            tok_wq = {0: None, 1: None}
            tok_wq_use = {0: None, 1: None}
            psc_free = [None] * 4
            qraw_free = {0: None, 1: None}
            swq_free = {0: None, 1: None}
            qsw_last = {0: None, 1: None}

            for blki in range(H // HB):
                par = blki % 2
                nfc = max(1, FC // 4)
                for j in range(0, FC, nfc):
                    tok_wq[par] = P.do("gpsimd", lambda g, blki=blki, par=par, j=j, nfc=nfc: g.dma_start(
                        out=wqb[par][:, j:j + nfc, :],
                        in_=wq_r[:, j:j + nfc, blki * HB * PD:(blki + 1) * HB * PD]),
                        waits=[("pe", tok_wq_use[par]) if tok_wq_use[par] else None]
                              + (gdC + [("dks0", NKS0), ("dks1", NKS1),
                                        ("dvs0", NVS0), ("dvs1", NVS1)]
                                 if blki < 2 else []),
                        sem=f"dwq{par}", inc=16)
                for hh in range(HB):
                    h = blki * HB + hh
                    hp = h % 2
                    tok_ev = None
                    for qb in range(NQB):
                        bk = (h % 2) * 2 + (qb % 2)
                        for fc in range(FC):
                            P.do("tensor", lambda t, fc=fc, hh=hh, qb=qb, par=par, bk=bk: t.matmul(
                                psc[bk][:, 0:QB],
                                wqb[par][:, fc, hh * PD:(hh + 1) * PD],
                                sT[:, fc, qb * QB:(qb + 1) * QB],
                                start=(fc == 0), stop=(fc == FC - 1)),
                                waits=[(f"dwq{par}", tok_wq[par]), ("dsT", NDST),
                                       psc_free[bk] if fc == 0 else None,
                                       ("dve", qraw_free[hp])
                                       if (fc == 0 and qb == 0 and qraw_free[hp]) else None,
                                       qsw_last[hp] if (fc == 0 and qb == 0) else None],
                                sem="pe")
                        tok_mm = P.cnt["pe"]
                        tok_ev = P.do("scalar", lambda s, h=h, qb=qb, hp=hp, bk=bk: s.add(
                            qraw[hp][:, qb * QB:(qb + 1) * QB], psc[bk][:, 0:QB],
                            bqT[:, h:h + 1]),
                            waits=[("pe", tok_mm)])
                        psc_free[bk] = ("act", tok_ev)
                    dst = qT[:, h, :]
                    tok_sw = {}
                    for qb in range(NQB):
                        sp = qb % 2
                        tok_sw[qb] = P.do("tensor", lambda t, hp=hp, qb=qb, sp=sp: t.matmul(
                            ps_swq[sp][:, 0:QB], permf[:],
                            qraw[hp][:, qb * QB:(qb + 1) * QB],
                            start=True, stop=True, skip_group_check=True),
                            waits=[("act", tok_ev), ("dtab", NTAB),
                                   swq_free[sp]],
                            sem="pe")
                    P.do("vector", lambda v, hp=hp: v.tensor_mul(
                        rscq[0][:], qraw[hp][:], cosq[:]),
                        waits=[("act", tok_ev), ("dcq", NCQ)])
                    for qb in range(NQB):
                        sp = qb % 2
                        qsl = slice(qb * QB, (qb + 1) * QB)
                        t2 = P.do("vector", lambda v, sp=sp, qsl=qsl: v.tensor_mul(
                            rscq[1][:, qsl], ps_swq[sp][:, 0:QB], sinq[:, qsl]),
                            waits=[("pe", tok_sw[qb])])
                        swq_free[sp] = ("dve", t2)
                    tm4 = P.do("vector", lambda v, dst=dst: v.tensor_add(
                        dst[:], rscq[0][:], rscq[1][:]))
                    qraw_free[hp] = tm4
                    qsw_last[hp] = ("pe", tok_sw[NQB - 1])
                tok_wq_use[par] = P.cnt["pe"]
            tok_qT_dve = P.cnt["dve"]

        if dump == "qT":
            o = out_t.rearrange("(h p) t -> p h t", p=PD)
            P.do("gpsimd", lambda g, o=o: g.dma_start(out=o, in_=qT[:]),
                 waits=[("dve", P.cnt["dve"])], sem="ddump", inc=16)
            P.do("gpsimd", lambda g: g.nop(), waits=[("ddump", 16)])
            P.cut("DUMP", "DUMP")
        P.cut("C", stop_after)
        gdD = P.barrier()
        with ExitStack() as pd:
            ed = pd.enter_context
            kS = ed(nc.sbuf_tensor("kS", [PD, S], BF16))
            vS = ed(nc.sbuf_tensor("vS", [PD, KC, D], BF16))
            pT = [ed(nc.sbuf_tensor(f"pT{i}", [PD, PW], BF16)) for i in range(3)]
            acc = [ed(nc.sbuf_tensor(f"acc{i}", [PD, PW], F32)) for i in range(2)]
            accb = ed(nc.sbuf_tensor("accb", [PD, PW], BF16))
            rl = ed(nc.sbuf_tensor("rl", [1, QB], F32))
            bcN = ed(nc.sbuf_tensor("bcN", [PD, QB], F32))
            tmpf = ed(nc.sbuf_tensor("tmpf", [PD, QB], F32))
            tmpb = ed(nc.sbuf_tensor("tmpb", [PD, QB], BF16))
            expb_t = ed(nc.sbuf_tensor("expb_t", [PD, 1], F32))
            exps_t = ed(nc.sbuf_tensor("exps_t", [PD, 1], F32))
            sc = [ed(nc.psum_tensor(f"scD{i}", [PD, PW], F32)) for i in range(2)]
            oPs2 = [ed(nc.psum_tensor(f"oPs{i}", [PD, 512], F32)) for i in range(2)]
            lpsb = ed(nc.psum_tensor("lps", [PD, 512], F32))
            bcpb = ed(nc.psum_tensor("bcp", [PD, 512], F32))

            P.do("vector", lambda v: v.memset(expb_t[:], cfg.expbias))
            tok_expc = P.do("vector", lambda v: v.memset(exps_t[:], cfg.scale))
            tok_kv_pe_done = None
            sc_free = {0: None, 1: None}
            pT_free = [None, None, None]
            oPs_free = None
            prev_tree = {"accv": None, "accp": None}

            for g in range(KVH):
                P.do("sync", lambda s, g=g: s.dma_start(
                    out=kS[:], in_=kT_dram.ap()[g * PD:(g + 1) * PD, :]),
                    waits=gdD + [("dks0", NKS0), ("dks1", NKS1),
                           ("dvs0", NVS0), ("dvs1", NVS1),
                           ("pe", tok_kv_pe_done) if tok_kv_pe_done else None],
                    sem="dkv", inc=16)
                nkcl = max(1, KC // 8)
                for j in range(0, KC, nkcl):
                    P.do("sync", lambda s, g=g, j=j, nkcl=nkcl: s.dma_start(
                        out=vS[:, j:j + nkcl, :],
                        in_=v_w[:, j:j + nkcl, g * PD:(g + 1) * PD]),
                        sem="dkv", inc=16)
                tok_kv = P.cnt["dkv"]
                if g == 0:
                    P.do("sync", lambda s: s.nop(), waits=[("dkv", tok_kv)])
                    if dump == "vS":
                        o = out_t.rearrange("(a p) t -> p a t", p=PD)
                        vv = vS.ap().rearrange("p (c8 c) d -> p c8 (c d)", c8=8)
                        P.do("gpsimd", lambda g2, o=o, vv=vv: g2.dma_start(
                            out=o[:, 0:8, :], in_=vv),
                            waits=[("dkv", tok_kv)], sem="ddump", inc=16)
                        P.do("gpsimd", lambda g2: g2.nop(), waits=[("ddump", 16)])
                        P.cut("DUMP", "DUMP")
                    P.cut("D0", stop_after)

                for qb in range(NQB):
                    qsl = slice(qb * QB, (qb + 1) * QB)
                    for pairi in range(NPAIR):
                        h0 = g * REP + pairi * 2
                        h1 = h0 + 1
                        first_pair = (g == 0 and qb == 0 and pairi == 0)
                        tok_z0 = P.do("vector", lambda en: en.memset(acc[0][:], 0.0),
                                      waits=[("dve", prev_tree["accv"])
                                             if prev_tree["accv"] else None]
                                            + (gdD if first_pair else []))
                        tok_z1 = P.do("gpsimd", lambda en: en.memset(acc[1][:], 0.0),
                                      waits=[("dve", prev_tree["accp"])
                                             if prev_tree["accp"] else None]
                                            + (gdD if first_pair else []))
                        if g == 0 and qb == 0 and pairi == 0:
                            P.cut("DM0", stop_after)
                        tok_exp = {}
                        tok_accop = {}
                        for kc in range(KC):
                            if g == 0 and qb == 0 and pairi == 0 and kc in (1, 2, 4, 16):
                                P.cut(f"DK{kc}", stop_after)
                            if (g == 0 and qb == 0 and pairi == 0 and kc == 16
                                    and stop_after in ("DNOPV", "DQK", "DQE")):
                                P.cut(stop_after, stop_after)
                            par = kc % 2
                            pi = kc % 3
                            P.do("tensor", lambda t, kc=kc, h0=h0, par=par, qsl=qsl: t.matmul(
                                sc[par][:, 0:QB], kS[:, kc * PD:(kc + 1) * PD],
                                qT[:, h0, qsl], start=True, stop=True,
                                skip_group_check=True),
                                waits=[("dkv", tok_kv), ("dve", tok_qT_dve),
                                       sc_free[par]],
                                sem="pe")
                            tok_qk = P.do("tensor", lambda t, kc=kc, h1=h1, par=par, qsl=qsl: t.matmul(
                                sc[par][:, QB:PW], kS[:, kc * PD:(kc + 1) * PD],
                                qT[:, h1, qsl], start=True, stop=True,
                                skip_group_check=True),
                                sem="pe")
                            if kc >= 1 and stop_after not in ("DNOPV", "DQK", "DQE"):
                                kcp = kc - 1
                                pip = kcp % 3
                                P.do("tensor", lambda t, kcp=kcp, pip=pip: t.matmul(
                                    oPs2[0][:, 0:QB], vS[:, kcp, :], pT[pip][:, 0:QB],
                                    start=(kcp == 0), stop=False,
                                    skip_group_check=True),
                                    waits=[("act", tok_exp[kcp]),
                                           oPs_free if kcp == 0 else None],
                                    sem="pe")
                                P.do("tensor", lambda t, kcp=kcp, pip=pip: t.matmul(
                                    oPs2[1][:, 0:QB], vS[:, kcp, :], pT[pip][:, QB:PW],
                                    start=(kcp == 0), stop=False,
                                    skip_group_check=True),
                                    sem="pe")
                            if g == 0 and qb == 0 and pairi == 0 and kc == 0:
                                if stop_after == "DQ1W":
                                    P.do("sync", lambda s: s.nop(),
                                         waits=[("pe", tok_qk)])
                                    P.cut("DQ1W", stop_after)
                                P.cut("DQ1", stop_after)
                            if stop_after == "DQK":
                                sc_free[par] = ("pe", tok_qk)
                                continue
                            tok_exp[kc] = P.do("scalar", lambda s, par=par, pi=pi: s.activation(
                                pT[pi][:], sc[par][:], AF.Exp,
                                bias=expb_t[:], scale=exps_t[:]),
                                waits=[("pe", tok_qk), pT_free[pi],
                                       ("dve", tok_expc)])
                            sc_free[par] = ("act", tok_exp[kc])
                            if g == 0 and qb == 0 and pairi == 0 and kc == 0:
                                P.cut("DE1", stop_after)
                            if stop_after == "DQE":
                                pT_free[pi] = ("act", tok_exp[kc])
                                continue
                            a = kc % 2
                            eng = "vector" if a == 0 else "gpsimd"
                            sname = "dve" if a == 0 else "pool"
                            tok_accop[kc] = P.do(eng, lambda en, a=a, pi=pi: en.tensor_add(
                                acc[a][:], acc[a][:], pT[pi][:]),
                                waits=[("act", tok_exp[kc]),
                                       ("dve", tok_z0) if kc == 0 else None,
                                       ("pool", tok_z1) if kc == 1 else None])
                            pT_free[pi] = (sname, tok_accop[kc])
                        if g == 0 and qb == 0 and pairi == 0:
                            P.cut("D1", stop_after)
                        kcp = KC - 1
                        pip = kcp % 3
                        P.do("tensor", lambda t, kcp=kcp, pip=pip: t.matmul(
                            oPs2[0][:, 0:QB], vS[:, kcp, :], pT[pip][:, 0:QB],
                            start=(kcp == 0), stop=True, skip_group_check=True),
                            waits=[("act", tok_exp.get(kcp)),
                                   oPs_free if kcp == 0 else None],
                            sem="pe")
                        tok_pvl = P.do("tensor", lambda t, kcp=kcp, pip=pip: t.matmul(
                            oPs2[1][:, 0:QB], vS[:, kcp, :], pT[pip][:, QB:PW],
                            start=(kcp == 0), stop=True, skip_group_check=True),
                            sem="pe")

                        tok_m = P.do("vector", lambda v: v.tensor_add(
                            acc[0][:], acc[0][:], acc[1][:]),
                            waits=[("pool", tok_accop.get(KC - 1))])
                        tok_t3 = P.do("vector", lambda v: v.tensor_copy(
                            accb[:], acc[0][:]))
                        prev_tree["accv"] = tok_t3
                        prev_tree["accp"] = tok_m
                        last_cp = None
                        for hh, h in enumerate((h0, h1)):
                            hsl = slice(hh * QB, (hh + 1) * QB)
                            tok_lp = P.do("tensor", lambda t, hsl=hsl: t.matmul(
                                lpsb[0:1, 0:QB], ones128b[:], accb[:, hsl],
                                start=True, stop=True, skip_group_check=True),
                                waits=[("dve", tok_t3)], sem="pe")
                            tok_rl = P.do("vector", lambda v: v.reciprocal(
                                rl[:], lpsb[0:1, 0:QB]),
                                waits=[("pe", tok_lp)])
                            tok_bc = P.do("tensor", lambda t: t.matmul(
                                bcpb[:, 0:QB], ones1f[:], rl[:],
                                start=True, stop=True, skip_group_check=True),
                                waits=[("dve", tok_rl)], sem="pe")
                            tok_bcc = P.do("vector", lambda v: v.tensor_copy(
                                bcN[:], bcpb[:, 0:QB]),
                                waits=[("pe", tok_bc)])
                            P.do("vector", lambda v, hh=hh: v.tensor_mul(
                                tmpf[:], oPs2[hh][:, 0:QB], bcN[:]),
                                waits=[("pe", tok_pvl)])
                            P.do("vector", lambda v, h=h, qsl=qsl: v.tensor_add(
                                tmpb[:], tmpf[:], sT[:, h, qsl]))
                            last_cp = P.do("vector", lambda v, h=h, qsl=qsl: v.tensor_copy(
                                sT[:, h, qsl], tmpb[:]))
                        oPs_free = ("dve", last_cp)
                        if g == 0 and qb == 0 and pairi == 0 and dump == "lpath":
                            t_e1 = P.do("vector", lambda v: v.tensor_copy(
                                tmpf[:], oPs2[0][:, 0:QB]))
                            P.do("gpsimd", lambda g2: g2.dma_start(
                                out=out_t[0:PD, 0:QB], in_=tmpf[:]),
                                waits=[("dve", t_e1)], sem="ddump", inc=16)
                            P.do("gpsimd", lambda g2: g2.dma_start(
                                out=out_t[PD:2 * PD, 0:PW], in_=accb[:]),
                                waits=[("dve", t_e1)], sem="ddump", inc=16)
                            P.do("gpsimd", lambda g2: g2.dma_start(
                                out=out_t[256:257, 0:QB], in_=rl[:]),
                                waits=[("dve", t_e1)], sem="ddump", inc=16)
                            P.do("gpsimd", lambda g2: g2.dma_start(
                                out=out_t[384:384 + PD, 0:QB], in_=bcN[:]),
                                waits=[("dve", t_e1)], sem="ddump", inc=16)
                            P.do("gpsimd", lambda g2: g2.nop(), waits=[("ddump", 64)])
                            P.cut("DUMP", "DUMP")
                        if g == 0 and qb == 0 and pairi == 0:
                            P.cut("D2", stop_after)
                tok_kv_pe_done = P.cnt["pe"]
                if g == 0 and dump == "vSbusy":
                    o = out_t.rearrange("(a p) t -> p a t", p=PD)
                    vv = vS.ap().rearrange("p (c8 c) d -> p c8 (c d)", c8=8)
                    P.do("gpsimd", lambda g2, o=o, vv=vv: g2.dma_start(
                        out=o[:, 0:8, :], in_=vv),
                        waits=[("pe", tok_kv_pe_done), ("dve", P.cnt["dve"])],
                        sem="ddump", inc=16)
                    P.do("gpsimd", lambda g2: g2.nop(), waits=[("ddump", 16)])
                    P.cut("DUMP", "DUMP")
        pcd.close()

        if dump == "sT":
            o = out_t.rearrange("(h p) t -> p h t", p=PD)
            P.do("gpsimd", lambda g, o=o: g.dma_start(out=o, in_=sT[:]),
                 waits=[("dve", P.cnt["dve"])], sem="ddump", inc=16)
            P.do("gpsimd", lambda g: g.nop(), waits=[("ddump", 16)])
            P.cut("DUMP", "DUMP")
        P.cut("D", stop_after)
        lnt = e(nc.sbuf_tensor("lntmp", [PD, QB], F32))

        def emit_ln(src_chunks, apply_out, tag, entry_list):
            with ExitStack() as pl:
                el = pl.enter_context
                sq = [el(nc.sbuf_tensor(f"sq{tag}{i}", [PD, QB], BF16)) for i in range(2)]
                rows = el(nc.sbuf_tensor(f"rows{tag}", [1, 6 * QB], F32))
                a_sb = el(nc.sbuf_tensor(f"asb{tag}", [PD, QB], F32))
                b_sb = el(nc.sbuf_tensor(f"bsb{tag}", [PD, QB], F32))
                eps_t = el(nc.sbuf_tensor(f"eps{tag}", [1, 1], F32))
                nh_t = el(nc.sbuf_tensor(f"nh{tag}", [1, 1], F32))
                ps1 = el(nc.psum_tensor(f"ps1{tag}", [PD, 512], F32))
                ps2 = el(nc.psum_tensor(f"ps2{tag}", [PD, 512], F32))
                psb2 = el(nc.psum_tensor(f"psbb{tag}", [PD, 512], F32))

                mrow = rows[:, 0:QB]
                msrow = rows[:, QB:2 * QB]
                m2row = rows[:, 2 * QB:3 * QB]
                varrow = rows[:, 3 * QB:4 * QB]
                arow = rows[:, 4 * QB:5 * QB]
                brow = rows[:, 5 * QB:6 * QB]
                sq_free = {0: None, 1: None}
                P.do("vector", lambda v: v.memset(eps_t[:], 1e-5))
                tok_lnc = P.do("vector", lambda v: v.memset(nh_t[:], -0.5))

                for qb in range(NQB):
                    qsl = slice(qb * QB, (qb + 1) * QB)
                    tok_s1 = tok_s2 = None
                    for fc in range(FC):
                        sp = fc % 2
                        src = src_chunks(fc)
                        tok_sq = P.do("vector", lambda v, src=src, sp=sp, qsl=qsl: v.tensor_mul(
                            sq[sp][:], src[:, qsl], src[:, qsl]),
                            waits=entry_list + [sq_free[sp]])
                        tok_s1 = P.do("tensor", lambda t, src=src, qsl=qsl, fc=fc: t.matmul(
                            ps1[0:1, 0:QB], ones128b[:], src[:, qsl],
                            start=(fc == 0), stop=(fc == FC - 1),
                            skip_group_check=True),
                            waits=entry_list + [("dve", tok_setup)], sem="pe")
                        tok_s2 = P.do("tensor", lambda t, sp=sp, fc=fc: t.matmul(
                            ps2[0:1, 0:QB], ones128b[:], sq[sp][:],
                            start=(fc == 0), stop=(fc == FC - 1),
                            skip_group_check=True),
                            waits=[("dve", tok_sq)], sem="pe")
                        sq_free[sp] = ("pe", tok_s2)
                    inv = 1.0 / float(E)
                    P.do("vector", lambda v, inv=inv: v.tensor_scalar_mul(
                        mrow, ps1[0:1, 0:QB], inv), waits=[("pe", tok_s1)])
                    P.do("vector", lambda v, inv=inv: v.tensor_scalar_mul(
                        msrow, ps2[0:1, 0:QB], inv), waits=[("pe", tok_s2)])
                    P.do("vector", lambda v: v.tensor_mul(m2row, mrow, mrow))
                    t_var = P.do("vector", lambda v: v.tensor_sub(varrow, msrow, m2row))
                    t_ln = P.do("scalar", lambda s: s.activation(
                        arow, varrow, AF.Ln, bias=eps_t[0:1, :], scale=1.0),
                        waits=[("dve", t_var), ("dve", tok_lnc)])
                    t_rs = P.do("scalar", lambda s: s.activation(
                        arow, arow, AF.Exp, bias=0.0, scale=nh_t[0:1, :]))
                    t_mr = P.do("vector", lambda v: v.tensor_mul(brow, mrow, arow),
                                waits=[("act", t_rs)])
                    t_nb = P.do("vector", lambda v: v.tensor_scalar_mul(brow, brow, -1.0))
                    t_bca = P.do("tensor", lambda t: t.matmul(
                        psb2[:, 0:QB], ones1f[:], arow, start=True, stop=True,
                        skip_group_check=True),
                        waits=[("dve", t_nb)], sem="pe")
                    t_ca = P.do("vector", lambda v: v.tensor_copy(a_sb[:], psb2[:, 0:QB]),
                                waits=[("pe", t_bca)])
                    t_bcb = P.do("tensor", lambda t: t.matmul(
                        psb2[:, 0:QB], ones1f[:], brow, start=True, stop=True,
                        skip_group_check=True),
                        waits=[("dve", t_ca)], sem="pe")
                    t_cb = P.do("vector", lambda v: v.tensor_copy(b_sb[:], psb2[:, 0:QB]),
                                waits=[("pe", t_bcb)])
                    for fc in range(FC):
                        apply_out(fc, qb, a_sb, b_sb, [("dve", t_cb)])

        def ln1_apply(fc, qb, a_sb, b_sb, waits):
            qsl = slice(qb * QB, (qb + 1) * QB)
            P.do("vector", lambda v, fc=fc, qsl=qsl: v.tensor_mul(
                lnt[:], sT[:, fc, qsl], a_sb[:]), waits=waits)
            P.do("vector", lambda v, fc=fc, qsl=qsl: v.tensor_add(
                sT[:, fc, qsl], lnt[:], b_sb[:]))

        emit_ln(lambda fc: sT[:, fc, :], ln1_apply, "1", P.barrier())
        tok_ln1 = P.cnt["dve"]

        if dump == "sTe":
            o = out_t.rearrange("(h p) t -> p h t", p=PD)
            P.do("gpsimd", lambda g, o=o: g.dma_start(out=o, in_=sT[:]),
                 waits=[("dve", P.cnt["dve"])], sem="ddump", inc=16)
            P.do("gpsimd", lambda g: g.nop(), waits=[("ddump", 16)])
            P.cut("DUMP", "DUMP")
        P.cut("E", stop_after)
        v2T = e(nc.sbuf_tensor("v2T", [PD, FC, T], BF16))
        gdF = P.barrier()
        with ExitStack() as pf:
            ef = pf.enter_context
            wpb = [ef(nc.sbuf_tensor(f"wpb{i}", [PD, E], BF16)) for i in range(2)]
            wnb = [ef(nc.sbuf_tensor(f"wnb{i}", [PD, E], BF16)) for i in range(2)]
            bin2 = ef(nc.sbuf_tensor("bin2", [PD, E], BF16))
            wd = ef(nc.sbuf_tensor("wd", [PD, E], BF16))
            wdT = [ef(nc.sbuf_tensor(f"wdT{i}", [PD, FC, PD], BF16)) for i in range(2)]
            psf = [ef(nc.psum_tensor(f"psF{i}", [PD, 512], F32)) for i in range(2)]
            pst = [ef(nc.psum_tensor(f"psFT{i}", [PD, 1024], BF16)) for i in range(2)]

            tok_w_use = {0: None, 1: None}
            tok_wd_use_any = None
            tok_wdT_use = {0: None, 1: None}
            tok_pst_rd = {0: None, 1: None}
            psf_free = {0: None, 1: None}

            for ob in range(FC):
                par = ob % 2
                t_lp = P.do("gpsimd", lambda g, ob=ob, par=par: g.dma_start(
                    out=wpb[par][:], in_=wpos_in[ob * PD:(ob + 1) * PD, :]),
                    waits=[("pool", tok_w_use[par]) if tok_w_use[par] else None]
                          + (gdF if ob < 2 else []),
                    sem=f"dfp{par}", inc=16)
                t_lnn = P.do("gpsimd", lambda g, ob=ob, par=par: g.dma_start(
                    out=wnb[par][:], in_=wneg_in[ob * PD:(ob + 1) * PD, :]),
                    sem=f"dfn{par}", inc=16)
                P.do("gpsimd", lambda g, par=par: g.tensor_single_scalar(
                    bin2[:], wnb[par][:], 0.0, ALU.is_gt),
                    waits=[(f"dfn{par}", t_lnn)])
                P.do("gpsimd", lambda g, par=par: g.tensor_single_scalar(
                    wd[:], wpb[par][:], 0.0, ALU.is_gt),
                    waits=[(f"dfp{par}", t_lp),
                           ("pe", tok_wd_use_any) if tok_wd_use_any else None])
                t_wd = P.do("gpsimd", lambda g: g.tensor_sub(
                    wd[:], wd[:], bin2[:]))
                tok_w_use[par] = t_wd
                last_ev = None
                for fc in range(FC):
                    pp = fc % 2
                    t_tr = P.do("tensor", lambda t, fc=fc, pp=pp: t.transpose(
                        pst[pp][:, 0:PD], wd[:, fc * PD:(fc + 1) * PD], identb[:]),
                        waits=[("pool", t_wd),
                               ("dve", tok_pst_rd[pp]) if tok_pst_rd[pp] else None,
                               ("pe", tok_wdT_use[par])
                               if (fc == 0 and tok_wdT_use[par]) else None],
                        sem="pe")
                    t_evt = P.do("vector", lambda v, par=par, fc=fc, pp=pp: v.tensor_copy(
                        wdT[par][:, fc, :], pst[pp][:, 0:PD]),
                        waits=[("pe", t_tr)])
                    tok_pst_rd[pp] = t_evt
                    last_ev = t_evt
                tok_wd_use_any = P.cnt["pe"]
                for fc in range(FC):
                    for qb in range(NQB):
                        P.do("tensor", lambda t, par=par, fc=fc, qb=qb: t.matmul(
                            psf[qb % 2][:, 0:QB], wdT[par][:, fc, :],
                            sT[:, fc, qb * QB:(qb + 1) * QB],
                            start=(fc == 0), stop=(fc == FC - 1),
                            skip_group_check=True),
                            waits=[("dve", last_ev), ("dve", tok_ln1),
                                   psf_free[qb % 2] if fc == 0 else None],
                            sem="pe")
                tok_mm = P.cnt["pe"]
                tok_wdT_use[par] = tok_mm
                for qb in range(NQB):
                    qsl = slice(qb * QB, (qb + 1) * QB)
                    t_ev = P.do("vector", lambda v, ob=ob, qb=qb, qsl=qsl: v.tensor_add(
                        v2T[:, ob, qsl], psf[qb % 2][:, 0:QB], sT[:, ob, qsl]),
                        waits=[("pe", tok_mm)])
                    psf_free[qb % 2] = ("dve", t_ev)

        if dump == "v2T":
            o = out_t.rearrange("(h p) t -> p h t", p=PD)
            P.do("gpsimd", lambda g, o=o: g.dma_start(out=o, in_=v2T[:]),
                 waits=[("dve", P.cnt["dve"])], sem="ddump", inc=16)
            P.do("gpsimd", lambda g: g.nop(), waits=[("ddump", 16)])
            P.cut("DUMP", "DUMP")
        P.cut("F", stop_after)
        outstage = [e(nc.sbuf_tensor(f"outs{i}", [PD, QB], F32)) for i in range(2)]
        out_st = {0: None, 1: None}
        _ln2_n = [0]

        def ln2_apply(fc, qb, a_sb, b_sb, waits):
            par = _ln2_n[0] % 2
            _ln2_n[0] += 1
            qsl = slice(qb * QB, (qb + 1) * QB)
            w = list(waits)
            if out_st[par] is not None:
                w.append((f"dob{par}", out_st[par]))
            P.do("vector", lambda v, fc=fc, qsl=qsl: v.tensor_mul(
                lnt[:], v2T[:, fc, qsl], a_sb[:]), waits=w)
            t2 = P.do("vector", lambda v, par=par: v.tensor_add(
                outstage[par][:], lnt[:], b_sb[:]))
            out_st[par] = P.do("sync", lambda s, fc=fc, qsl=qsl, par=par: s.dma_start(
                out=out_t[fc * PD:(fc + 1) * PD, qsl], in_=outstage[par][:]),
                waits=[("dve", t2)], sem=f"dob{par}", inc=16)

        emit_ln(lambda fc: v2T[:, fc, :], ln2_apply, "2", P.barrier())

        P.do("sync", lambda s: s.nop(),
             waits=[("dob0", P.cnt.get("dob0", 0)), ("dob1", P.cnt.get("dob1", 0))])

        blk = e(nc.Block())
        P.finalize(blk)
    return nc



def _rope_tables(positions):
    j = np.arange(0, D, 2, dtype=np.float32) / np.float32(D)
    inv_freq = (np.float32(1.0) / (np.float32(10000.0) ** j)).astype(np.float32)
    ang = positions.astype(np.float32)[:, None] * inv_freq[None, :]
    cos = np.cos(ang).T.astype(np.float32)
    sin = np.sin(ang).T.astype(np.float32)
    c2 = np.concatenate([cos, cos], axis=0)
    s2m = np.concatenate([-sin, sin], axis=0)
    return np.ascontiguousarray(c2), np.ascontiguousarray(s2m)


def make_inputs(cfg, core, x, wq, bq, wk, bk, wv, bv, w_pos, w_neg):
    S, T = cfg.S, cfg.T
    t0 = core * T
    perm = np.concatenate([np.arange(t0, t0 + T), np.arange(0, t0),
                           np.arange(t0 + T, S)])
    xr = np.ascontiguousarray(x[perm])
    cosk, sink = _rope_tables(perm.astype(np.float32))
    return {
        "x": xr,
        "wq": wq, "wk": wk, "wv": wv,
        "wpos": w_pos, "wneg": w_neg,
        "bqT": np.ascontiguousarray(bq.reshape(cfg.FC, PD).T),
        "bkT": np.ascontiguousarray(bk.reshape(cfg.KVH, PD).T),
        "bv": np.ascontiguousarray(bv.reshape(1, cfg.KVDIM)),
        "cosq": np.ascontiguousarray(cosk[:, 0:T]),
        "sinq": np.ascontiguousarray(sink[:, 0:T]),
        "cosk": cosk, "sink": sink,
        "ident": np.eye(PD, dtype=BF),
        "perm": np.ascontiguousarray(
            np.eye(PD, dtype=np.float32)[:, list(range(64, PD)) + list(range(64))]),
    }


def kernel(L0_attn_x, L0_attn_wq, L0_attn_bq, L0_attn_wk, L0_attn_bk,
           L0_attn_wv, L0_attn_bv, L0_ffn_w_pos, L0_ffn_w_neg):
    cfg = FULL
    x = np.ascontiguousarray(np.asarray(L0_attn_x, dtype=np.float32).reshape(cfg.S, cfg.E))
    wq = np.ascontiguousarray(np.asarray(L0_attn_wq, dtype=np.float32))
    bq = np.asarray(L0_attn_bq, dtype=np.float32)
    wk = np.ascontiguousarray(np.asarray(L0_attn_wk, dtype=np.float32))
    bk = np.asarray(L0_attn_bk, dtype=np.float32)
    wv = np.ascontiguousarray(np.asarray(L0_attn_wv, dtype=np.float32))
    bv = np.asarray(L0_attn_bv, dtype=np.float32)
    wp = np.ascontiguousarray(np.asarray(L0_ffn_w_pos, dtype=np.float32))
    wn = np.ascontiguousarray(np.asarray(L0_ffn_w_neg, dtype=np.float32))

    nc = build(cfg)
    in_maps = [make_inputs(cfg, c, x, wq, bq, wk, bk, wv, bv, wp, wn)
               for c in range(cfg.R)]
    res = run_bass_kernel_spmd(nc, in_maps, core_ids=list(range(cfg.R)))
    out = np.concatenate(
        [np.asarray(res.results[c]["out_t"]).T for c in range(cfg.R)], axis=0)
    return out.reshape(1, cfg.S, cfg.E).astype(np.float32)


# version 07
# speedup vs baseline: 1.0229x; 1.0229x over previous
import math
from contextlib import ExitStack

import numpy as np
import ml_dtypes

import concourse.bass as bass
import concourse.mybir as mybir
from concourse.bass_utils import run_bass_kernel_spmd

F32 = mybir.dt.float32
BF16 = mybir.dt.bfloat16
AF = mybir.ActivationFunctionType
ALU = mybir.AluOpType
BF = ml_dtypes.bfloat16

D = 128
PD = 128


class Cfg:
    def __init__(self, E, H, KVH, S, R, T):
        self.E, self.H, self.KVH, self.S, self.R, self.T = E, H, KVH, S, R, T
        self.REP = H // KVH
        self.FC = E // PD
        self.KC = S // PD
        self.QB = min(512, T)
        self.NQB = T // self.QB
        self.PW = 2 * self.QB
        self.KVDIM = KVH * D
        self.SBT = min(1024, S)
        self.NSB = S // self.SBT
        self.NPAIR = self.REP // 2
        assert self.REP == 4 and E % PD == 0 and S % PD == 0 and T % self.QB == 0
        self.scale = 1.0 / math.sqrt(float(D))
        self.expbias = -20.0


FULL = Cfg(E=4096, H=32, KVH=8, S=8192, R=8, T=1024)

TICK = {"vector": "dve", "gpsimd": "pool", "scalar": "act", "tensor": "pe"}
CHAINED = {"vector", "gpsimd", "scalar"}


class Prog:
    ENGS = ("sync", "scalar", "vector", "tensor", "gpsimd")

    def __init__(self, nc, stack):
        self.nc = nc
        self.stack = stack
        self.q = {k: [] for k in self.ENGS}
        self.handles = {}
        self.cnt = {}
        self.lastw = {}
        self.enabled = True

    def cut(self, label, stop_after):
        if stop_after is not None and label == stop_after:
            self.enabled = False

    def sem(self, name):
        if name not in self.handles:
            self.handles[name] = self.stack.enter_context(self.nc.semaphore(name))
            self.cnt.setdefault(name, 0)
        return self.handles[name]

    def wait(self, eng, name, val):
        if val is None or val <= 0:
            return
        if self.lastw.get((eng, name), 0) >= val:
            return
        assert val <= self.cnt.get(name, 0), (
            f"wait({eng},{name},{val}) > scheduled {self.cnt.get(name, 0)}")
        self.lastw[(eng, name)] = val
        h = self.sem(name)
        if self.enabled:
            self.q[eng].append(lambda e, h=h, v=val: e.wait_ge(h, v))

    def do(self, eng, fn, waits=(), sem=None, inc=1):
        if eng in CHAINED:
            t = TICK[eng]
            if sem is None:
                sem = t
            if sem == t and self.cnt.get(t, 0) > 0:
                self.wait(eng, t, self.cnt[t])
        for wv in waits:
            if wv is None or wv[1] is None:
                continue
            self.wait(eng, wv[0], wv[1])
        if sem is None:
            if self.enabled:
                self.q[eng].append(fn)
            return None
        h = self.sem(sem)
        self.cnt[sem] += inc
        tok = self.cnt[sem]
        if self.enabled:
            self.q[eng].append(lambda e, h=h, fn=fn, inc=inc: fn(e).then_inc(h, inc))
        return tok

    def barrier(self):
        return [(s, self.cnt[s]) for s in ("pe", "act", "dve", "pool")
                if self.cnt.get(s, 0) > 0]

    def finalize(self, block):
        for eng in self.ENGS:
            fns = self.q[eng]

            def runner(e, fns=fns):
                for f in fns:
                    f(e)

            getattr(block, eng)(runner)


def build(cfg: Cfg, stop_after=None, dump=None):
    nc = bass.Bass("TRN2", target_bir_lowering=False)
    E, H, KVH, S, T = cfg.E, cfg.H, cfg.KVH, cfg.S, cfg.T
    FC, KC, QB, NQB, PW = cfg.FC, cfg.KC, cfg.QB, cfg.NQB, cfg.PW
    KVDIM, SBT, NSB, REP, NPAIR = cfg.KVDIM, cfg.SBT, cfg.NSB, cfg.REP, cfg.NPAIR

    x_in = nc.dram_tensor("x", [S, E], F32, kind="ExternalInput").ap()
    wq_in = nc.dram_tensor("wq", [E, E], F32, kind="ExternalInput").ap()
    wk_in = nc.dram_tensor("wk", [E, KVDIM], F32, kind="ExternalInput").ap()
    wv_in = nc.dram_tensor("wv", [E, KVDIM], F32, kind="ExternalInput").ap()
    wpos_in = nc.dram_tensor("wpos", [E, E], F32, kind="ExternalInput").ap()
    wneg_in = nc.dram_tensor("wneg", [E, E], F32, kind="ExternalInput").ap()
    bqT_in = nc.dram_tensor("bqT", [PD, FC], F32, kind="ExternalInput").ap()
    bkT_in = nc.dram_tensor("bkT", [PD, KVH], F32, kind="ExternalInput").ap()
    bv_in = nc.dram_tensor("bv", [1, KVDIM], F32, kind="ExternalInput").ap()
    cosq_in = nc.dram_tensor("cosq", [PD, T], F32, kind="ExternalInput").ap()
    sinq_in = nc.dram_tensor("sinq", [PD, T], F32, kind="ExternalInput").ap()
    cosk_in = nc.dram_tensor("cosk", [PD, S], F32, kind="ExternalInput").ap()
    sink_in = nc.dram_tensor("sink", [PD, S], F32, kind="ExternalInput").ap()
    perm_in = nc.dram_tensor("perm", [PD, PD], F32, kind="ExternalInput").ap()
    ident_in = nc.dram_tensor("ident", [PD, PD], BF16, kind="ExternalInput").ap()
    out_t = nc.dram_tensor("out_t", [E, T], F32, kind="ExternalOutput").ap()

    xT_dram = nc.dram_tensor("xT_dram", [E, S], BF16)
    kT_dram = nc.dram_tensor("kT_dram", [KVDIM, S], BF16)
    v_dram = nc.dram_tensor("v_dram", [S, KVDIM], BF16)
    xT_r = xT_dram.ap().rearrange("(fc p) t -> p fc t", p=PD)
    v_w = v_dram.ap().rearrange("(b p) kv -> p b kv", p=PD)

    with ExitStack() as top:
        P = Prog(nc, top)
        e = top.enter_context

        consts = e(nc.sbuf_tensor("consts", [PD, 8], F32))
        CONSTS = {}
        for v in (0.0, 1.0, cfg.scale, cfg.expbias, 1e-5, -0.5):
            v = float(v)
            if v not in CONSTS:
                i = len(CONSTS)
                CONSTS[v] = i
                nc.const_aps.register(F32, v, consts[:, i:i + 1])

        ones1f = e(nc.sbuf_tensor("ones1f", [1, PD], F32))
        ones128b = e(nc.sbuf_tensor("ones128b", [PD, 1], BF16))
        ones32b = e(nc.sbuf_tensor("ones32b", [32, 1], BF16))
        identb = e(nc.sbuf_tensor("identb", [PD, PD], BF16))
        bqT = e(nc.sbuf_tensor("bqT_s", [PD, FC], F32))
        bkT = e(nc.sbuf_tensor("bkT_s", [PD, KVH], F32))
        bv_row = e(nc.sbuf_tensor("bv_row", [1, KVDIM], F32))
        bvb = e(nc.sbuf_tensor("bvb", [PD, KVDIM], F32))
        permf = e(nc.sbuf_tensor("permf", [PD, PD], F32))

        for dst, src in ((identb, ident_in), (bqT, bqT_in), (bkT, bkT_in),
                         (bv_row, bv_in), (permf, perm_in)):
            P.do("sync", lambda s, d=dst, sp=src: s.dma_start(out=d[:], in_=sp),
                 sem="dtab", inc=16)
        NTAB = P.cnt["dtab"]

        for v, i in CONSTS.items():
            P.do("vector", lambda en, ap=consts[:, i:i + 1], v=v: en.memset(ap, v))
        P.do("vector", lambda en: en.memset(ones1f[:], 1.0))
        P.do("vector", lambda en: en.memset(ones128b[:], 1.0))
        tok_setup = P.do("vector", lambda en: en.memset(ones32b[:], 1.0))


        with nc.psum_tensor("ps_bv", [PD, 512], F32) as ps_bv:
            Wv = min(512, KVDIM)
            tok_prev = None
            for i in range(KVDIM // Wv):
                sl = slice(i * Wv, (i + 1) * Wv)
                tok_mm = P.do(
                    "tensor",
                    lambda t, sl=sl: t.matmul(ps_bv[:, 0:Wv], ones1f[:],
                                              bv_row[:, sl], start=True, stop=True),
                    waits=[("dtab", NTAB), ("dve", tok_setup),
                           ("dve", tok_prev) if tok_prev else None],
                    sem="pe")
                tok_prev = P.do(
                    "vector",
                    lambda v, sl=sl: v.tensor_copy(bvb[:, sl], ps_bv[:, 0:Wv]),
                    waits=[("pe", tok_mm)])
            tok_bvb = tok_prev

        P.cut("setup", stop_after)
        NTB = S // PD
        TGW = 4 * PD
        gdA = P.barrier()
        with ExitStack() as pa:
            ea = pa.enter_context
            xraw = [ea(nc.sbuf_tensor(f"xraw{i}", [PD, E], BF16)) for i in range(2)]
            stage = [ea(nc.sbuf_tensor(f"xstg{i}", [PD, FC, TGW], BF16)) for i in range(2)]
            ps_tr = [ea(nc.psum_tensor(f"ps_trA{i}", [PD, 1024], BF16)) for i in range(2)]

            tok_xr_use = {0: None, 1: None}
            tok_stg_st = {0: None, 1: None}
            tok_ptr_rd = {0: None, 1: None}

            for tg in range(NTB // 4):
                gpar = tg % 2
                last_ev = None
                for ti in range(4):
                    tb = tg * 4 + ti
                    par = tb % 2
                    tok_ld = P.do(
                        "gpsimd",
                        lambda g, tb=tb, par=par: g.dma_start(
                            out=xraw[par][:], in_=x_in[tb * PD:(tb + 1) * PD, :]),
                        waits=[("pe", tok_xr_use[par]) if tok_xr_use[par] else None],
                        sem=f"dx{par}", inc=16)
                    for fc in range(FC):
                        pp = fc % 2
                        tok_tr = P.do(
                            "tensor",
                            lambda t, par=par, fc=fc, pp=pp: t.transpose(
                                ps_tr[pp][:, 0:PD],
                                xraw[par][:, fc * PD:(fc + 1) * PD], identb[:]),
                            waits=[(f"dx{par}", tok_ld), ("dtab", NTAB),
                                   ("act", tok_ptr_rd[pp]) if tok_ptr_rd[pp] else None]
                                  + (gdA if tb == 0 else []),
                            sem="pe")
                        tok_ev = P.do(
                            "scalar",
                            lambda s, gpar=gpar, fc=fc, pp=pp, ti=ti: s.copy(
                                stage[gpar][:, fc, ti * PD:(ti + 1) * PD],
                                ps_tr[pp][:, 0:PD]),
                            waits=[("pe", tok_tr),
                                   (f"dso{gpar}", tok_stg_st[gpar])
                                   if tok_stg_st[gpar] else None])
                        tok_ptr_rd[pp] = tok_ev
                        last_ev = tok_ev
                    tok_xr_use[par] = P.cnt["pe"]
                nfc = max(1, FC // 4)
                for j in range(0, FC, nfc):
                    tok_st = P.do(
                        "sync",
                        lambda s, tg=tg, gpar=gpar, j=j, nfc=nfc: s.dma_start(
                            out=xT_r[:, j:j + nfc, tg * TGW:(tg + 1) * TGW],
                            in_=stage[gpar][:, j:j + nfc, :]),
                        waits=[("act", last_ev)], sem=f"dso{gpar}", inc=16)
                tok_stg_st[gpar] = tok_st
            NDSO0 = P.cnt.get("dso0", 0)
            NDSO1 = P.cnt.get("dso1", 0)

        P.cut("A", stop_after)
        def proj_pass(kind):
            gdB = P.barrier()
            with ExitStack() as pb:
                eb = pb.enter_context
                if kind == "K":
                    wb = eb(nc.sbuf_tensor("wkb", [PD, FC, KVDIM], BF16))
                    w_src = wk_in.rearrange("(fc p) kv -> p fc kv", p=PD)
                    cosk = eb(nc.sbuf_tensor("cosk_s", [PD, SBT], F32))
                    sink = eb(nc.sbuf_tensor("sink_s", [PD, SBT], F32))
                    kraw = [eb(nc.sbuf_tensor(f"kraw{i}", [PD, 512], F32)) for i in range(2)]
                    rsc = [eb(nc.sbuf_tensor(f"rsc{i}", [PD, 512], F32)) for i in range(2)]
                    ps_sw = [eb(nc.psum_tensor(f"ps_swK{i}", [PD, 512], F32))
                             for i in range(2)]
                    kst = [eb(nc.sbuf_tensor(f"kst{i}", [PD, SBT], BF16)) for i in range(2)]

                else:
                    wb = eb(nc.sbuf_tensor("wvb", [PD, FC, KVDIM], BF16))
                    w_src = wv_in.rearrange("(fc p) kv -> p fc kv", p=PD)
                    TB_G = min(8, SBT // PD)
                    vst = [eb(nc.sbuf_tensor(f"vst{i}", [PD, TB_G, KVDIM], BF16))
                           for i in range(2)]
                xsup = eb(nc.sbuf_tensor(f"xsup{kind}", [PD, FC, SBT], BF16))
                psb = [eb(nc.psum_tensor(f"psB{kind}{i}", [PD, 512], F32))
                       for i in range(4)]
                ntab_now = P.cnt["dtab"]

                wsem = f"dw{kind}"
                step = max(1, FC // 4)
                for fc2 in range(0, FC, step):
                    P.do("gpsimd", lambda g, fc2=fc2: g.dma_start(
                        out=wb[:, fc2:fc2 + step, :], in_=w_src[:, fc2:fc2 + step, :]),
                        waits=gdB + [("dks0", P.cnt.get("dks0", 0)),
                                     ("dks1", P.cnt.get("dks1", 0))],
                        sem=wsem, inc=16)
                nw = P.cnt[wsem]

                psb_free = [None] * 4
                tok_stage_st = {0: None, 1: None}
                st_sem = "dks" if kind == "K" else "dvs"
                xsup_pe_done = None
                kraw_rd = {0: None, 1: None}
                sw_free = {0: None, 1: None}
                ck_rope_done = None

                for sb in range(NSB):
                    nl = max(1, FC // 8)
                    for j in range(0, FC, nl):
                        P.do("sync", lambda s, sb=sb, j=j, nl=nl: s.dma_start(
                            out=xsup[:, j:j + nl, :],
                            in_=xT_r[:, j:j + nl, sb * SBT:(sb + 1) * SBT]),
                            waits=gdB + [("dso0", NDSO0), ("dso1", NDSO1),
                                   ("pe", xsup_pe_done) if xsup_pe_done else None],
                            sem=f"dxs{kind}", inc=16)
                    tok_xs = P.cnt[f"dxs{kind}"]

                    if kind == "K":
                        P.do("sync", lambda s, sb=sb: s.dma_start(
                            out=cosk[:], in_=cosk_in[:, sb * SBT:(sb + 1) * SBT]),
                            waits=gdB + [("dso0", NDSO0), ("dso1", NDSO1),
                                         ("dve", ck_rope_done) if ck_rope_done else None],
                            sem="dck", inc=16)
                        P.do("sync", lambda s, sb=sb: s.dma_start(
                            out=sink[:], in_=sink_in[:, sb * SBT:(sb + 1) * SBT]),
                            sem="dck", inc=16)
                        tok_ck = P.cnt["dck"]
                        TQK = SBT // 512
                        for g in range(KVH):
                            spar = (sb * KVH + g) % 2
                            last_rope = None
                            for tqp in range(0, max(2, TQK), 2):
                                bks = [tqp % 4, (tqp % 4) + 1][:min(2, TQK)]
                                for fc in range(FC):
                                    for jj, bk in enumerate(bks):
                                        P.do("tensor", lambda t, g=g, fc=fc, tq=tqp + jj, bk=bk: t.matmul(
                                            psb[bk][:],
                                            wb[:, fc, g * PD:(g + 1) * PD],
                                            xsup[:, fc, tq * 512:(tq + 1) * 512],
                                            start=(fc == 0), stop=(fc == FC - 1)),
                                            waits=[("dxs" + kind, tok_xs), (wsem, nw),
                                                   psb_free[bk] if fc == 0 else None],
                                            sem="pe")
                                tok_mm = P.cnt["pe"]
                                for jj, bk in enumerate(bks):
                                    tq = tqp + jj
                                    rp = tq % 2
                                    tok_ev = P.do("scalar", lambda s, bk=bk, g=g, rp=rp: s.add(
                                        kraw[rp][:], psb[bk][:], bkT[:, g:g + 1]),
                                        waits=[("pe", tok_mm), ("dtab", ntab_now)]
                                              + (kraw_rd[rp] or []))
                                    psb_free[bk] = ("act", tok_ev)
                                    cs = slice(tq * 512, (tq + 1) * 512)
                                    dst = kst[spar][:, tq * 512:(tq + 1) * 512]
                                    sp = tq % 2
                                    tok_sw = P.do("tensor", lambda t, rp=rp, sp=sp: t.matmul(
                                        ps_sw[sp][:], permf[:], kraw[rp][:],
                                        start=True, stop=True, skip_group_check=True),
                                        waits=[("act", tok_ev), ("dtab", NTAB),
                                               sw_free[sp]],
                                        sem="pe")
                                    w0 = [("act", tok_ev), ("dck", tok_ck),
                                          (st_sem + str(spar), tok_stage_st[spar])
                                          if (tok_stage_st[spar] and tq == 0) else None]
                                    tp1 = P.do("vector", lambda v, rp=rp, cs=cs: v.tensor_mul(
                                        rsc[0][:], kraw[rp][:], cosk[:, cs]),
                                        waits=w0)
                                    tp2 = P.do("vector", lambda v, sp=sp, cs=cs: v.tensor_mul(
                                        rsc[1][:], ps_sw[sp][:], sink[:, cs]),
                                        waits=[("pe", tok_sw)])
                                    sw_free[sp] = ("dve", tp2)
                                    to1 = P.do("vector", lambda v, dst=dst: v.tensor_add(
                                        dst[:], rsc[0][:], rsc[1][:]))
                                    kraw_rd[rp] = [("pe", tok_sw), ("dve", tp1)]
                                    last_rope = to1
                                    ck_rope_done = to1
                            tok_st = P.do("sync", lambda s, g=g, sb=sb, spar=spar: s.dma_start(
                                out=kT_dram.ap()[g * PD:(g + 1) * PD,
                                                 sb * SBT:(sb + 1) * SBT],
                                in_=kst[spar][:]),
                                waits=[("dve", last_rope)],
                                sem=st_sem + str(spar), inc=16)
                            tok_stage_st[spar] = tok_st
                    else:
                        NTBS = SBT // PD
                        TB_G = min(8, NTBS)
                        NDV = max(1, KVDIM // 512)
                        Wv = min(512, KVDIM)
                        for tb8 in range(0, NTBS, TB_G):
                            par = (sb * (NTBS // TB_G) + tb8 // TB_G) % 2
                            last_vev = None
                            for tb in range(tb8, tb8 + TB_G):
                                banks = [(tb % 2) * 2 + i for i in range(NDV)]
                                for fc in range(FC):
                                    for dvh in range(NDV):
                                        bk = banks[dvh]
                                        P.do("tensor", lambda t, tb=tb, fc=fc, dvh=dvh, bk=bk: t.matmul(
                                            psb[bk][:, 0:Wv],
                                            xsup[:, fc, tb * PD:(tb + 1) * PD],
                                            wb[:, fc, dvh * 512:dvh * 512 + Wv],
                                            start=(fc == 0), stop=(fc == FC - 1)),
                                            waits=[("dxs" + kind, tok_xs), (wsem, nw),
                                                   psb_free[bk] if fc == 0 else None,
                                                   (st_sem + str(par), tok_stage_st[par])
                                                   if (fc == 0 and tb == tb8 and tok_stage_st[par]) else None],
                                            sem="pe")
                                tok_mm = P.cnt["pe"]
                                for dvh in range(NDV):
                                    bk = banks[dvh]
                                    tok_ev = P.do("vector", lambda v, tb=tb, tb8=tb8, dvh=dvh, bk=bk, par=par: v.tensor_add(
                                        vst[par][:, tb - tb8, dvh * 512:dvh * 512 + Wv],
                                        psb[bk][:, 0:Wv],
                                        bvb[:, dvh * 512:dvh * 512 + Wv]),
                                        waits=[("pe", tok_mm), ("dve", tok_bvb)])
                                    psb_free[bk] = ("dve", tok_ev)
                                    last_vev = tok_ev
                            tok_st = P.do("sync", lambda s, sb=sb, tb8=tb8, par=par: s.dma_start(
                                out=v_w[:, sb * NTBS + tb8:sb * NTBS + tb8 + TB_G, :],
                                in_=vst[par][:]),
                                waits=[("dve", last_vev)],
                                sem=st_sem + str(par), inc=16)
                            tok_stage_st[par] = tok_st
                    xsup_pe_done = P.cnt["pe"]

        proj_pass("K")
        proj_pass("V")
        NKS0, NKS1 = P.cnt.get("dks0", 0), P.cnt.get("dks1", 0)
        NVS0, NVS1 = P.cnt.get("dvs0", 0), P.cnt.get("dvs1", 0)

        if dump in ("kT", "v", "xT"):
            if dump == "kT":
                o = out_t.rearrange("(a b) t -> a (b t)", b=8192 // T)
                P.do("gpsimd", lambda g, o=o: g.dma_start(
                    out=o, in_=kT_dram.ap()[0:KVDIM // 2, :]),
                    waits=[("dks0", P.cnt.get("dks0", 0)), ("dks1", P.cnt.get("dks1", 0))],
                    sem="ddump", inc=16)
            elif dump == "v":
                P.do("gpsimd", lambda g: g.dma_start(
                    out=out_t, in_=v_dram.ap()[0:E, :]),
                    waits=[("dvs0", P.cnt.get("dvs0", 0)), ("dvs1", P.cnt.get("dvs1", 0))],
                    sem="ddump", inc=16)
            else:
                P.do("gpsimd", lambda g: g.dma_start(
                    out=out_t, in_=xT_dram.ap()[:, 0:T]),
                    waits=[("dso0", NDSO0), ("dso1", NDSO1)],
                    sem="ddump", inc=16)
            P.do("gpsimd", lambda g: g.nop(), waits=[("ddump", 16)])
            P.cut("DUMP", "DUMP")
        P.cut("B", stop_after)
        sT = e(nc.sbuf_tensor("sT", [PD, FC, T], BF16))
        gdT = P.barrier()
        nl = max(1, FC // 4)
        for j in range(0, FC, nl):
            P.do("sync", lambda s, j=j, nl=nl: s.dma_start(
                out=sT[:, j:j + nl, :], in_=xT_r[:, j:j + nl, 0:T]),
                waits=gdT + [("dso0", NDSO0), ("dso1", NDSO1),
                             ("dks0", NKS0), ("dks1", NKS1),
                             ("dvs0", NVS0), ("dvs1", NVS1)],
                sem="dsT", inc=16)
        NDST = P.cnt["dsT"]

        pcd = ExitStack()
        qT = pcd.enter_context(nc.sbuf_tensor("qT", [PD, H, T], BF16))
        cosq = pcd.enter_context(nc.sbuf_tensor("cosq_s", [PD, T], F32))
        sinq = pcd.enter_context(nc.sbuf_tensor("sinq_s", [PD, T], F32))
        gdC = P.barrier()
        P.do("sync", lambda s: s.dma_start(out=cosq[:], in_=cosq_in),
             waits=gdC + [("dks0", NKS0), ("dks1", NKS1),
                          ("dvs0", NVS0), ("dvs1", NVS1)],
             sem="dcq", inc=16)
        P.do("sync", lambda s: s.dma_start(out=sinq[:], in_=sinq_in),
             sem="dcq", inc=16)
        NCQ = P.cnt["dcq"]
        with ExitStack() as pc:
            ec = pc.enter_context
            HB = 2
            wqb = [ec(nc.sbuf_tensor(f"wqb{i}", [PD, FC, HB * PD], BF16)) for i in range(2)]
            qraw = [ec(nc.sbuf_tensor(f"qraw{i}", [PD, T], F32)) for i in range(2)]
            rscq = [ec(nc.sbuf_tensor(f"rscq{i}", [PD, T], F32)) for i in range(2)]
            psc = [ec(nc.psum_tensor(f"psC{i}", [PD, 512], F32)) for i in range(4)]
            ps_swq = [ec(nc.psum_tensor(f"ps_swQ{i}", [PD, 512], F32)) for i in range(2)]
            wq_r = wq_in.rearrange("(fc p) o -> p fc o", p=PD)

            tok_wq = {0: None, 1: None}
            tok_wq_use = {0: None, 1: None}
            psc_free = [None] * 4
            qraw_free = {0: None, 1: None}
            swq_free = {0: None, 1: None}
            qsw_last = {0: None, 1: None}

            for blki in range(H // HB):
                par = blki % 2
                nfc = max(1, FC // 4)
                for j in range(0, FC, nfc):
                    tok_wq[par] = P.do("gpsimd", lambda g, blki=blki, par=par, j=j, nfc=nfc: g.dma_start(
                        out=wqb[par][:, j:j + nfc, :],
                        in_=wq_r[:, j:j + nfc, blki * HB * PD:(blki + 1) * HB * PD]),
                        waits=[("pe", tok_wq_use[par]) if tok_wq_use[par] else None]
                              + (gdC + [("dks0", NKS0), ("dks1", NKS1),
                                        ("dvs0", NVS0), ("dvs1", NVS1)]
                                 if blki < 2 else []),
                        sem=f"dwq{par}", inc=16)
                for hh in range(HB):
                    h = blki * HB + hh
                    hp = h % 2
                    tok_ev = None
                    for qb in range(NQB):
                        bk = (h % 2) * 2 + (qb % 2)
                        for fc in range(FC):
                            P.do("tensor", lambda t, fc=fc, hh=hh, qb=qb, par=par, bk=bk: t.matmul(
                                psc[bk][:, 0:QB],
                                wqb[par][:, fc, hh * PD:(hh + 1) * PD],
                                sT[:, fc, qb * QB:(qb + 1) * QB],
                                start=(fc == 0), stop=(fc == FC - 1)),
                                waits=[(f"dwq{par}", tok_wq[par]), ("dsT", NDST),
                                       psc_free[bk] if fc == 0 else None,
                                       ("dve", qraw_free[hp])
                                       if (fc == 0 and qb == 0 and qraw_free[hp]) else None,
                                       qsw_last[hp] if (fc == 0 and qb == 0) else None],
                                sem="pe")
                        tok_mm = P.cnt["pe"]
                        tok_ev = P.do("scalar", lambda s, h=h, qb=qb, hp=hp, bk=bk: s.add(
                            qraw[hp][:, qb * QB:(qb + 1) * QB], psc[bk][:, 0:QB],
                            bqT[:, h:h + 1]),
                            waits=[("pe", tok_mm)])
                        psc_free[bk] = ("act", tok_ev)
                    dst = qT[:, h, :]
                    tok_sw = {}
                    for qb in range(NQB):
                        sp = qb % 2
                        tok_sw[qb] = P.do("tensor", lambda t, hp=hp, qb=qb, sp=sp: t.matmul(
                            ps_swq[sp][:, 0:QB], permf[:],
                            qraw[hp][:, qb * QB:(qb + 1) * QB],
                            start=True, stop=True, skip_group_check=True),
                            waits=[("act", tok_ev), ("dtab", NTAB),
                                   swq_free[sp]],
                            sem="pe")
                    P.do("vector", lambda v, hp=hp: v.tensor_mul(
                        rscq[0][:], qraw[hp][:], cosq[:]),
                        waits=[("act", tok_ev), ("dcq", NCQ)])
                    for qb in range(NQB):
                        sp = qb % 2
                        qsl = slice(qb * QB, (qb + 1) * QB)
                        t2 = P.do("vector", lambda v, sp=sp, qsl=qsl: v.tensor_mul(
                            rscq[1][:, qsl], ps_swq[sp][:, 0:QB], sinq[:, qsl]),
                            waits=[("pe", tok_sw[qb])])
                        swq_free[sp] = ("dve", t2)
                    tm4 = P.do("vector", lambda v, dst=dst: v.tensor_add(
                        dst[:], rscq[0][:], rscq[1][:]))
                    qraw_free[hp] = tm4
                    qsw_last[hp] = ("pe", tok_sw[NQB - 1])
                tok_wq_use[par] = P.cnt["pe"]
            tok_qT_dve = P.cnt["dve"]

        if dump == "qT":
            o = out_t.rearrange("(h p) t -> p h t", p=PD)
            P.do("gpsimd", lambda g, o=o: g.dma_start(out=o, in_=qT[:]),
                 waits=[("dve", P.cnt["dve"])], sem="ddump", inc=16)
            P.do("gpsimd", lambda g: g.nop(), waits=[("ddump", 16)])
            P.cut("DUMP", "DUMP")
        P.cut("C", stop_after)
        gdD = P.barrier()
        with ExitStack() as pd:
            ed = pd.enter_context
            kS = ed(nc.sbuf_tensor("kS", [PD, S], BF16))
            vS = ed(nc.sbuf_tensor("vS", [PD, KC, D], BF16))
            pT = [ed(nc.sbuf_tensor(f"pT{i}", [PD, PW], BF16)) for i in range(3)]
            acc = [ed(nc.sbuf_tensor(f"acc{i}", [PD, PW], F32)) for i in range(2)]
            accb = ed(nc.sbuf_tensor("accb", [PD, PW], BF16))
            rl = ed(nc.sbuf_tensor("rl", [1, QB], F32))
            bcN = ed(nc.sbuf_tensor("bcN", [PD, QB], F32))
            tmpf = ed(nc.sbuf_tensor("tmpf", [PD, QB], F32))
            tmpb = ed(nc.sbuf_tensor("tmpb", [PD, QB], BF16))
            expb_t = ed(nc.sbuf_tensor("expb_t", [PD, 1], F32))
            exps_t = ed(nc.sbuf_tensor("exps_t", [PD, 1], F32))
            sc = [ed(nc.psum_tensor(f"scD{i}", [PD, PW], F32)) for i in range(2)]
            oPs2 = [ed(nc.psum_tensor(f"oPs{i}", [PD, 512], F32)) for i in range(2)]
            lpsb = ed(nc.psum_tensor("lps", [PD, 512], F32))
            bcpb = ed(nc.psum_tensor("bcp", [PD, 512], F32))

            P.do("vector", lambda v: v.memset(expb_t[:], cfg.expbias))
            tok_expc = P.do("vector", lambda v: v.memset(exps_t[:], cfg.scale))
            tok_kv_pe_done = None
            sc_free = {0: None, 1: None}
            pT_free = [None, None, None]
            oPs_free = None
            prev_tree = {"accv": None, "accp": None}

            for g in range(KVH):
                P.do("sync", lambda s, g=g: s.dma_start(
                    out=kS[:], in_=kT_dram.ap()[g * PD:(g + 1) * PD, :]),
                    waits=gdD + [("dks0", NKS0), ("dks1", NKS1),
                           ("dvs0", NVS0), ("dvs1", NVS1),
                           ("pe", tok_kv_pe_done) if tok_kv_pe_done else None],
                    sem="dkv", inc=16)
                nkcl = max(1, KC // 8)
                for j in range(0, KC, nkcl):
                    P.do("sync", lambda s, g=g, j=j, nkcl=nkcl: s.dma_start(
                        out=vS[:, j:j + nkcl, :],
                        in_=v_w[:, j:j + nkcl, g * PD:(g + 1) * PD]),
                        sem="dkv", inc=16)
                tok_kv = P.cnt["dkv"]
                if g == 0:
                    P.do("sync", lambda s: s.nop(), waits=[("dkv", tok_kv)])
                    if dump == "vS":
                        o = out_t.rearrange("(a p) t -> p a t", p=PD)
                        vv = vS.ap().rearrange("p (c8 c) d -> p c8 (c d)", c8=8)
                        P.do("gpsimd", lambda g2, o=o, vv=vv: g2.dma_start(
                            out=o[:, 0:8, :], in_=vv),
                            waits=[("dkv", tok_kv)], sem="ddump", inc=16)
                        P.do("gpsimd", lambda g2: g2.nop(), waits=[("ddump", 16)])
                        P.cut("DUMP", "DUMP")
                    P.cut("D0", stop_after)

                for qb in range(NQB):
                    qsl = slice(qb * QB, (qb + 1) * QB)
                    for pairi in range(NPAIR):
                        h0 = g * REP + pairi * 2
                        h1 = h0 + 1
                        first_pair = (g == 0 and qb == 0 and pairi == 0)
                        tok_z0 = P.do("vector", lambda en: en.memset(acc[0][:], 0.0),
                                      waits=[("dve", prev_tree["accv"])
                                             if prev_tree["accv"] else None]
                                            + (gdD if first_pair else []))
                        if g == 0 and qb == 0 and pairi == 0:
                            P.cut("DM0", stop_after)
                        tok_exp = {}
                        tok_accop = {}
                        for kc in range(KC):
                            if g == 0 and qb == 0 and pairi == 0 and kc in (1, 2, 4, 16):
                                P.cut(f"DK{kc}", stop_after)
                            if (g == 0 and qb == 0 and pairi == 0 and kc == 16
                                    and stop_after in ("DNOPV", "DQK", "DQE")):
                                P.cut(stop_after, stop_after)
                            par = kc % 2
                            pi = kc % 3
                            P.do("tensor", lambda t, kc=kc, h0=h0, par=par, qsl=qsl: t.matmul(
                                sc[par][:, 0:QB], kS[:, kc * PD:(kc + 1) * PD],
                                qT[:, h0, qsl], start=True, stop=True,
                                skip_group_check=True),
                                waits=[("dkv", tok_kv), ("dve", tok_qT_dve),
                                       sc_free[par]],
                                sem="pe")
                            tok_qk = P.do("tensor", lambda t, kc=kc, h1=h1, par=par, qsl=qsl: t.matmul(
                                sc[par][:, QB:PW], kS[:, kc * PD:(kc + 1) * PD],
                                qT[:, h1, qsl], start=True, stop=True,
                                skip_group_check=True),
                                sem="pe")
                            if kc >= 1 and stop_after not in ("DNOPV", "DQK", "DQE"):
                                kcp = kc - 1
                                pip = kcp % 3
                                P.do("tensor", lambda t, kcp=kcp, pip=pip: t.matmul(
                                    oPs2[0][:, 0:QB], vS[:, kcp, :], pT[pip][:, 0:QB],
                                    start=(kcp == 0), stop=False,
                                    skip_group_check=True),
                                    waits=[("act", tok_exp[kcp]),
                                           oPs_free if kcp == 0 else None],
                                    sem="pe")
                                P.do("tensor", lambda t, kcp=kcp, pip=pip: t.matmul(
                                    oPs2[1][:, 0:QB], vS[:, kcp, :], pT[pip][:, QB:PW],
                                    start=(kcp == 0), stop=False,
                                    skip_group_check=True),
                                    sem="pe")
                            if g == 0 and qb == 0 and pairi == 0 and kc == 0:
                                if stop_after == "DQ1W":
                                    P.do("sync", lambda s: s.nop(),
                                         waits=[("pe", tok_qk)])
                                    P.cut("DQ1W", stop_after)
                                P.cut("DQ1", stop_after)
                            if stop_after == "DQK":
                                sc_free[par] = ("pe", tok_qk)
                                continue
                            tok_exp[kc] = P.do("scalar", lambda s, par=par, pi=pi: s.activation(
                                pT[pi][:], sc[par][:], AF.Exp,
                                bias=expb_t[:], scale=exps_t[:]),
                                waits=[("pe", tok_qk), pT_free[pi],
                                       ("dve", tok_expc)])
                            sc_free[par] = ("act", tok_exp[kc])
                            if g == 0 and qb == 0 and pairi == 0 and kc == 0:
                                P.cut("DE1", stop_after)
                            if stop_after == "DQE":
                                pT_free[pi] = ("act", tok_exp[kc])
                                continue
                            tok_accop[kc] = P.do("vector", lambda en, pi=pi: en.tensor_add(
                                acc[0][:], acc[0][:], pT[pi][:]),
                                waits=[("act", tok_exp[kc]),
                                       ("dve", tok_z0) if kc == 0 else None])
                            pT_free[pi] = ("dve", tok_accop[kc])
                        if g == 0 and qb == 0 and pairi == 0:
                            P.cut("D1", stop_after)
                        kcp = KC - 1
                        pip = kcp % 3
                        P.do("tensor", lambda t, kcp=kcp, pip=pip: t.matmul(
                            oPs2[0][:, 0:QB], vS[:, kcp, :], pT[pip][:, 0:QB],
                            start=(kcp == 0), stop=True, skip_group_check=True),
                            waits=[("act", tok_exp.get(kcp)),
                                   oPs_free if kcp == 0 else None],
                            sem="pe")
                        tok_pvl = P.do("tensor", lambda t, kcp=kcp, pip=pip: t.matmul(
                            oPs2[1][:, 0:QB], vS[:, kcp, :], pT[pip][:, QB:PW],
                            start=(kcp == 0), stop=True, skip_group_check=True),
                            sem="pe")

                        tok_t3 = P.do("vector", lambda v: v.tensor_copy(
                            accb[:], acc[0][:]),
                            waits=[("dve", tok_accop.get(KC - 1))])
                        prev_tree["accv"] = tok_t3
                        last_cp = None
                        for hh, h in enumerate((h0, h1)):
                            hsl = slice(hh * QB, (hh + 1) * QB)
                            tok_lp = P.do("tensor", lambda t, hsl=hsl: t.matmul(
                                lpsb[0:1, 0:QB], ones128b[:], accb[:, hsl],
                                start=True, stop=True, skip_group_check=True),
                                waits=[("dve", tok_t3)], sem="pe")
                            tok_rl = P.do("vector", lambda v: v.reciprocal(
                                rl[:], lpsb[0:1, 0:QB]),
                                waits=[("pe", tok_lp)])
                            tok_bc = P.do("tensor", lambda t: t.matmul(
                                bcpb[:, 0:QB], ones1f[:], rl[:],
                                start=True, stop=True, skip_group_check=True),
                                waits=[("dve", tok_rl)], sem="pe")
                            tok_bcc = P.do("vector", lambda v: v.tensor_copy(
                                bcN[:], bcpb[:, 0:QB]),
                                waits=[("pe", tok_bc)])
                            P.do("vector", lambda v, hh=hh: v.tensor_mul(
                                tmpf[:], oPs2[hh][:, 0:QB], bcN[:]),
                                waits=[("pe", tok_pvl)])
                            P.do("vector", lambda v, h=h, qsl=qsl: v.tensor_add(
                                tmpb[:], tmpf[:], sT[:, h, qsl]))
                            last_cp = P.do("vector", lambda v, h=h, qsl=qsl: v.tensor_copy(
                                sT[:, h, qsl], tmpb[:]))
                        oPs_free = ("dve", last_cp)
                        if g == 0 and qb == 0 and pairi == 0 and dump == "lpath":
                            t_e1 = P.do("vector", lambda v: v.tensor_copy(
                                tmpf[:], oPs2[0][:, 0:QB]))
                            P.do("gpsimd", lambda g2: g2.dma_start(
                                out=out_t[0:PD, 0:QB], in_=tmpf[:]),
                                waits=[("dve", t_e1)], sem="ddump", inc=16)
                            P.do("gpsimd", lambda g2: g2.dma_start(
                                out=out_t[PD:2 * PD, 0:PW], in_=accb[:]),
                                waits=[("dve", t_e1)], sem="ddump", inc=16)
                            P.do("gpsimd", lambda g2: g2.dma_start(
                                out=out_t[256:257, 0:QB], in_=rl[:]),
                                waits=[("dve", t_e1)], sem="ddump", inc=16)
                            P.do("gpsimd", lambda g2: g2.dma_start(
                                out=out_t[384:384 + PD, 0:QB], in_=bcN[:]),
                                waits=[("dve", t_e1)], sem="ddump", inc=16)
                            P.do("gpsimd", lambda g2: g2.nop(), waits=[("ddump", 64)])
                            P.cut("DUMP", "DUMP")
                        if g == 0 and qb == 0 and pairi == 0:
                            P.cut("D2", stop_after)
                tok_kv_pe_done = P.cnt["pe"]
                if g == 0 and dump == "vSbusy":
                    o = out_t.rearrange("(a p) t -> p a t", p=PD)
                    vv = vS.ap().rearrange("p (c8 c) d -> p c8 (c d)", c8=8)
                    P.do("gpsimd", lambda g2, o=o, vv=vv: g2.dma_start(
                        out=o[:, 0:8, :], in_=vv),
                        waits=[("pe", tok_kv_pe_done), ("dve", P.cnt["dve"])],
                        sem="ddump", inc=16)
                    P.do("gpsimd", lambda g2: g2.nop(), waits=[("ddump", 16)])
                    P.cut("DUMP", "DUMP")
        pcd.close()

        if dump == "sT":
            o = out_t.rearrange("(h p) t -> p h t", p=PD)
            P.do("gpsimd", lambda g, o=o: g.dma_start(out=o, in_=sT[:]),
                 waits=[("dve", P.cnt["dve"])], sem="ddump", inc=16)
            P.do("gpsimd", lambda g: g.nop(), waits=[("ddump", 16)])
            P.cut("DUMP", "DUMP")
        P.cut("D", stop_after)
        lnt = e(nc.sbuf_tensor("lntmp", [PD, QB], F32))

        def emit_ln(src_chunks, apply_out, tag, entry_list):
            with ExitStack() as pl:
                el = pl.enter_context
                sq = [el(nc.sbuf_tensor(f"sq{tag}{i}", [PD, QB], BF16)) for i in range(2)]
                rows = el(nc.sbuf_tensor(f"rows{tag}", [1, 6 * QB], F32))
                a_sb = el(nc.sbuf_tensor(f"asb{tag}", [PD, QB], F32))
                b_sb = el(nc.sbuf_tensor(f"bsb{tag}", [PD, QB], F32))
                eps_t = el(nc.sbuf_tensor(f"eps{tag}", [1, 1], F32))
                nh_t = el(nc.sbuf_tensor(f"nh{tag}", [1, 1], F32))
                ps1 = el(nc.psum_tensor(f"ps1{tag}", [PD, 512], F32))
                ps2 = el(nc.psum_tensor(f"ps2{tag}", [PD, 512], F32))
                psb2 = el(nc.psum_tensor(f"psbb{tag}", [PD, 512], F32))

                mrow = rows[:, 0:QB]
                msrow = rows[:, QB:2 * QB]
                m2row = rows[:, 2 * QB:3 * QB]
                varrow = rows[:, 3 * QB:4 * QB]
                arow = rows[:, 4 * QB:5 * QB]
                brow = rows[:, 5 * QB:6 * QB]
                sq_free = {0: None, 1: None}
                P.do("vector", lambda v: v.memset(eps_t[:], 1e-5))
                tok_lnc = P.do("vector", lambda v: v.memset(nh_t[:], -0.5))

                for qb in range(NQB):
                    qsl = slice(qb * QB, (qb + 1) * QB)
                    tok_s1 = tok_s2 = None
                    for fc in range(FC):
                        sp = fc % 2
                        src = src_chunks(fc)
                        tok_sq = P.do("vector", lambda v, src=src, sp=sp, qsl=qsl: v.tensor_mul(
                            sq[sp][:], src[:, qsl], src[:, qsl]),
                            waits=entry_list + [sq_free[sp]])
                        tok_s1 = P.do("tensor", lambda t, src=src, qsl=qsl, fc=fc: t.matmul(
                            ps1[0:1, 0:QB], ones128b[:], src[:, qsl],
                            start=(fc == 0), stop=(fc == FC - 1),
                            skip_group_check=True),
                            waits=entry_list + [("dve", tok_setup)], sem="pe")
                        tok_s2 = P.do("tensor", lambda t, sp=sp, fc=fc: t.matmul(
                            ps2[0:1, 0:QB], ones128b[:], sq[sp][:],
                            start=(fc == 0), stop=(fc == FC - 1),
                            skip_group_check=True),
                            waits=[("dve", tok_sq)], sem="pe")
                        sq_free[sp] = ("pe", tok_s2)
                    inv = 1.0 / float(E)
                    P.do("vector", lambda v, inv=inv: v.tensor_scalar_mul(
                        mrow, ps1[0:1, 0:QB], inv), waits=[("pe", tok_s1)])
                    P.do("vector", lambda v, inv=inv: v.tensor_scalar_mul(
                        msrow, ps2[0:1, 0:QB], inv), waits=[("pe", tok_s2)])
                    P.do("vector", lambda v: v.tensor_mul(m2row, mrow, mrow))
                    t_var = P.do("vector", lambda v: v.tensor_sub(varrow, msrow, m2row))
                    t_ln = P.do("scalar", lambda s: s.activation(
                        arow, varrow, AF.Ln, bias=eps_t[0:1, :], scale=1.0),
                        waits=[("dve", t_var), ("dve", tok_lnc)])
                    t_rs = P.do("scalar", lambda s: s.activation(
                        arow, arow, AF.Exp, bias=0.0, scale=nh_t[0:1, :]))
                    t_mr = P.do("vector", lambda v: v.tensor_mul(brow, mrow, arow),
                                waits=[("act", t_rs)])
                    t_nb = P.do("vector", lambda v: v.tensor_scalar_mul(brow, brow, -1.0))
                    t_bca = P.do("tensor", lambda t: t.matmul(
                        psb2[:, 0:QB], ones1f[:], arow, start=True, stop=True,
                        skip_group_check=True),
                        waits=[("dve", t_nb)], sem="pe")
                    t_ca = P.do("vector", lambda v: v.tensor_copy(a_sb[:], psb2[:, 0:QB]),
                                waits=[("pe", t_bca)])
                    t_bcb = P.do("tensor", lambda t: t.matmul(
                        psb2[:, 0:QB], ones1f[:], brow, start=True, stop=True,
                        skip_group_check=True),
                        waits=[("dve", t_ca)], sem="pe")
                    t_cb = P.do("vector", lambda v: v.tensor_copy(b_sb[:], psb2[:, 0:QB]),
                                waits=[("pe", t_bcb)])
                    for fc in range(FC):
                        apply_out(fc, qb, a_sb, b_sb, [("dve", t_cb)])

        def ln1_apply(fc, qb, a_sb, b_sb, waits):
            qsl = slice(qb * QB, (qb + 1) * QB)
            P.do("vector", lambda v, fc=fc, qsl=qsl: v.tensor_mul(
                lnt[:], sT[:, fc, qsl], a_sb[:]), waits=waits)
            P.do("vector", lambda v, fc=fc, qsl=qsl: v.tensor_add(
                sT[:, fc, qsl], lnt[:], b_sb[:]))

        emit_ln(lambda fc: sT[:, fc, :], ln1_apply, "1", P.barrier())
        tok_ln1 = P.cnt["dve"]

        if dump == "sTe":
            o = out_t.rearrange("(h p) t -> p h t", p=PD)
            P.do("gpsimd", lambda g, o=o: g.dma_start(out=o, in_=sT[:]),
                 waits=[("dve", P.cnt["dve"])], sem="ddump", inc=16)
            P.do("gpsimd", lambda g: g.nop(), waits=[("ddump", 16)])
            P.cut("DUMP", "DUMP")
        P.cut("E", stop_after)
        v2T = e(nc.sbuf_tensor("v2T", [PD, FC, T], BF16))
        gdF = P.barrier()
        with ExitStack() as pf:
            ef = pf.enter_context
            wpb = [ef(nc.sbuf_tensor(f"wpb{i}", [PD, E], BF16)) for i in range(2)]
            wnb = [ef(nc.sbuf_tensor(f"wnb{i}", [PD, E], BF16)) for i in range(2)]
            bin2 = ef(nc.sbuf_tensor("bin2", [PD, E], BF16))
            wd = ef(nc.sbuf_tensor("wd", [PD, E], BF16))
            wdT = [ef(nc.sbuf_tensor(f"wdT{i}", [PD, FC, PD], BF16)) for i in range(2)]
            psf = [ef(nc.psum_tensor(f"psF{i}", [PD, 512], F32)) for i in range(2)]
            pst = [ef(nc.psum_tensor(f"psFT{i}", [PD, 1024], BF16)) for i in range(2)]

            tok_w_use = {0: None, 1: None}
            tok_wd_use_any = None
            tok_wdT_use = {0: None, 1: None}
            tok_pst_rd = {0: None, 1: None}
            psf_free = {0: None, 1: None}

            for ob in range(FC):
                par = ob % 2
                t_lp = P.do("gpsimd", lambda g, ob=ob, par=par: g.dma_start(
                    out=wpb[par][:], in_=wpos_in[ob * PD:(ob + 1) * PD, :]),
                    waits=[("pool", tok_w_use[par]) if tok_w_use[par] else None]
                          + (gdF if ob < 2 else []),
                    sem=f"dfp{par}", inc=16)
                t_lnn = P.do("gpsimd", lambda g, ob=ob, par=par: g.dma_start(
                    out=wnb[par][:], in_=wneg_in[ob * PD:(ob + 1) * PD, :]),
                    sem=f"dfn{par}", inc=16)
                P.do("gpsimd", lambda g, par=par: g.tensor_single_scalar(
                    bin2[:], wnb[par][:], 0.0, ALU.is_gt),
                    waits=[(f"dfn{par}", t_lnn)])
                P.do("gpsimd", lambda g, par=par: g.tensor_single_scalar(
                    wd[:], wpb[par][:], 0.0, ALU.is_gt),
                    waits=[(f"dfp{par}", t_lp),
                           ("pe", tok_wd_use_any) if tok_wd_use_any else None])
                t_wd = P.do("gpsimd", lambda g: g.tensor_sub(
                    wd[:], wd[:], bin2[:]))
                tok_w_use[par] = t_wd
                last_ev = None
                for fc in range(FC):
                    pp = fc % 2
                    t_tr = P.do("tensor", lambda t, fc=fc, pp=pp: t.transpose(
                        pst[pp][:, 0:PD], wd[:, fc * PD:(fc + 1) * PD], identb[:]),
                        waits=[("pool", t_wd),
                               ("dve", tok_pst_rd[pp]) if tok_pst_rd[pp] else None,
                               ("pe", tok_wdT_use[par])
                               if (fc == 0 and tok_wdT_use[par]) else None],
                        sem="pe")
                    t_evt = P.do("vector", lambda v, par=par, fc=fc, pp=pp: v.tensor_copy(
                        wdT[par][:, fc, :], pst[pp][:, 0:PD]),
                        waits=[("pe", t_tr)])
                    tok_pst_rd[pp] = t_evt
                    last_ev = t_evt
                tok_wd_use_any = P.cnt["pe"]
                for fc in range(FC):
                    for qb in range(NQB):
                        P.do("tensor", lambda t, par=par, fc=fc, qb=qb: t.matmul(
                            psf[qb % 2][:, 0:QB], wdT[par][:, fc, :],
                            sT[:, fc, qb * QB:(qb + 1) * QB],
                            start=(fc == 0), stop=(fc == FC - 1),
                            skip_group_check=True),
                            waits=[("dve", last_ev), ("dve", tok_ln1),
                                   psf_free[qb % 2] if fc == 0 else None],
                            sem="pe")
                tok_mm = P.cnt["pe"]
                tok_wdT_use[par] = tok_mm
                for qb in range(NQB):
                    qsl = slice(qb * QB, (qb + 1) * QB)
                    t_ev = P.do("vector", lambda v, ob=ob, qb=qb, qsl=qsl: v.tensor_add(
                        v2T[:, ob, qsl], psf[qb % 2][:, 0:QB], sT[:, ob, qsl]),
                        waits=[("pe", tok_mm)])
                    psf_free[qb % 2] = ("dve", t_ev)

        if dump == "v2T":
            o = out_t.rearrange("(h p) t -> p h t", p=PD)
            P.do("gpsimd", lambda g, o=o: g.dma_start(out=o, in_=v2T[:]),
                 waits=[("dve", P.cnt["dve"])], sem="ddump", inc=16)
            P.do("gpsimd", lambda g: g.nop(), waits=[("ddump", 16)])
            P.cut("DUMP", "DUMP")
        P.cut("F", stop_after)
        outstage = [e(nc.sbuf_tensor(f"outs{i}", [PD, QB], F32)) for i in range(2)]
        out_st = {0: None, 1: None}
        _ln2_n = [0]

        def ln2_apply(fc, qb, a_sb, b_sb, waits):
            par = _ln2_n[0] % 2
            _ln2_n[0] += 1
            qsl = slice(qb * QB, (qb + 1) * QB)
            w = list(waits)
            if out_st[par] is not None:
                w.append((f"dob{par}", out_st[par]))
            P.do("vector", lambda v, fc=fc, qsl=qsl: v.tensor_mul(
                lnt[:], v2T[:, fc, qsl], a_sb[:]), waits=w)
            t2 = P.do("vector", lambda v, par=par: v.tensor_add(
                outstage[par][:], lnt[:], b_sb[:]))
            out_st[par] = P.do("sync", lambda s, fc=fc, qsl=qsl, par=par: s.dma_start(
                out=out_t[fc * PD:(fc + 1) * PD, qsl], in_=outstage[par][:]),
                waits=[("dve", t2)], sem=f"dob{par}", inc=16)

        emit_ln(lambda fc: v2T[:, fc, :], ln2_apply, "2", P.barrier())

        P.do("sync", lambda s: s.nop(),
             waits=[("dob0", P.cnt.get("dob0", 0)), ("dob1", P.cnt.get("dob1", 0))])

        blk = e(nc.Block())
        P.finalize(blk)
    return nc



def _rope_tables(positions):
    j = np.arange(0, D, 2, dtype=np.float32) / np.float32(D)
    inv_freq = (np.float32(1.0) / (np.float32(10000.0) ** j)).astype(np.float32)
    ang = positions.astype(np.float32)[:, None] * inv_freq[None, :]
    cos = np.cos(ang).T.astype(np.float32)
    sin = np.sin(ang).T.astype(np.float32)
    c2 = np.concatenate([cos, cos], axis=0)
    s2m = np.concatenate([-sin, sin], axis=0)
    return np.ascontiguousarray(c2), np.ascontiguousarray(s2m)


def make_inputs(cfg, core, x, wq, bq, wk, bk, wv, bv, w_pos, w_neg):
    S, T = cfg.S, cfg.T
    t0 = core * T
    perm = np.concatenate([np.arange(t0, t0 + T), np.arange(0, t0),
                           np.arange(t0 + T, S)])
    xr = np.ascontiguousarray(x[perm])
    cosk, sink = _rope_tables(perm.astype(np.float32))
    return {
        "x": xr,
        "wq": wq, "wk": wk, "wv": wv,
        "wpos": w_pos, "wneg": w_neg,
        "bqT": np.ascontiguousarray(bq.reshape(cfg.FC, PD).T),
        "bkT": np.ascontiguousarray(bk.reshape(cfg.KVH, PD).T),
        "bv": np.ascontiguousarray(bv.reshape(1, cfg.KVDIM)),
        "cosq": np.ascontiguousarray(cosk[:, 0:T]),
        "sinq": np.ascontiguousarray(sink[:, 0:T]),
        "cosk": cosk, "sink": sink,
        "ident": np.eye(PD, dtype=BF),
        "perm": np.ascontiguousarray(
            np.eye(PD, dtype=np.float32)[:, list(range(64, PD)) + list(range(64))]),
    }


def kernel(L0_attn_x, L0_attn_wq, L0_attn_bq, L0_attn_wk, L0_attn_bk,
           L0_attn_wv, L0_attn_bv, L0_ffn_w_pos, L0_ffn_w_neg):
    cfg = FULL
    x = np.ascontiguousarray(np.asarray(L0_attn_x, dtype=np.float32).reshape(cfg.S, cfg.E))
    wq = np.ascontiguousarray(np.asarray(L0_attn_wq, dtype=np.float32))
    bq = np.asarray(L0_attn_bq, dtype=np.float32)
    wk = np.ascontiguousarray(np.asarray(L0_attn_wk, dtype=np.float32))
    bk = np.asarray(L0_attn_bk, dtype=np.float32)
    wv = np.ascontiguousarray(np.asarray(L0_attn_wv, dtype=np.float32))
    bv = np.asarray(L0_attn_bv, dtype=np.float32)
    wp = np.ascontiguousarray(np.asarray(L0_ffn_w_pos, dtype=np.float32))
    wn = np.ascontiguousarray(np.asarray(L0_ffn_w_neg, dtype=np.float32))

    nc = build(cfg)
    in_maps = [make_inputs(cfg, c, x, wq, bq, wk, bk, wv, bv, wp, wn)
               for c in range(cfg.R)]
    res = run_bass_kernel_spmd(nc, in_maps, core_ids=list(range(cfg.R)))
    out = np.concatenate(
        [np.asarray(res.results[c]["out_t"]).T for c in range(cfg.R)], axis=0)
    return out.reshape(1, cfg.S, cfg.E).astype(np.float32)


# version 08
# speedup vs baseline: 1.3881x; 1.3569x over previous
import math
from contextlib import ExitStack

import numpy as np
import ml_dtypes

import concourse.bass as bass
import concourse.mybir as mybir
from concourse.bass_utils import run_bass_kernel_spmd

F32 = mybir.dt.float32
BF16 = mybir.dt.bfloat16
AF = mybir.ActivationFunctionType
ALU = mybir.AluOpType
BF = ml_dtypes.bfloat16

D = 128
PD = 128


class Cfg:
    def __init__(self, E, H, KVH, S, R, T):
        self.E, self.H, self.KVH, self.S, self.R, self.T = E, H, KVH, S, R, T
        self.REP = H // KVH
        self.FC = E // PD
        self.KC = S // PD
        self.QB = min(512, T)
        self.NQB = T // self.QB
        self.PW = 2 * self.QB
        self.KVDIM = KVH * D
        self.SBT = min(1024, S)
        self.NSB = S // self.SBT
        self.NPAIR = self.REP // 2
        assert self.REP == 4 and E % PD == 0 and S % PD == 0 and T % self.QB == 0
        self.scale = 1.0 / math.sqrt(float(D))
        self.expbias = -20.0


FULL = Cfg(E=4096, H=32, KVH=8, S=8192, R=8, T=1024)

TICK = {"vector": "dve", "gpsimd": "pool", "scalar": "act", "tensor": "pe"}
CHAINED = {"vector", "gpsimd", "scalar"}


class Prog:
    ENGS = ("sync", "scalar", "vector", "tensor", "gpsimd")

    def __init__(self, nc, stack):
        self.nc = nc
        self.stack = stack
        self.q = {k: [] for k in self.ENGS}
        self.handles = {}
        self.cnt = {}
        self.lastw = {}
        self.enabled = True

    def cut(self, label, stop_after):
        if stop_after is not None and label == stop_after:
            self.enabled = False

    def sem(self, name):
        if name not in self.handles:
            self.handles[name] = self.stack.enter_context(self.nc.semaphore(name))
            self.cnt.setdefault(name, 0)
        return self.handles[name]

    def wait(self, eng, name, val):
        if val is None or val <= 0:
            return
        if self.lastw.get((eng, name), 0) >= val:
            return
        assert val <= self.cnt.get(name, 0), (
            f"wait({eng},{name},{val}) > scheduled {self.cnt.get(name, 0)}")
        self.lastw[(eng, name)] = val
        h = self.sem(name)
        if self.enabled:
            self.q[eng].append(lambda e, h=h, v=val: e.wait_ge(h, v))

    def do(self, eng, fn, waits=(), sem=None, inc=1):
        if eng in CHAINED:
            t = TICK[eng]
            if sem is None:
                sem = t
            if sem == t and self.cnt.get(t, 0) > 0:
                self.wait(eng, t, self.cnt[t])
        for wv in waits:
            if wv is None or wv[1] is None:
                continue
            self.wait(eng, wv[0], wv[1])
        if sem is None:
            if self.enabled:
                self.q[eng].append(fn)
            return None
        h = self.sem(sem)
        self.cnt[sem] += inc
        tok = self.cnt[sem]
        if self.enabled:
            self.q[eng].append(lambda e, h=h, fn=fn, inc=inc: fn(e).then_inc(h, inc))
        return tok

    def barrier(self):
        return [(s, self.cnt[s]) for s in ("pe", "act", "dve", "pool")
                if self.cnt.get(s, 0) > 0]

    def finalize(self, block):
        for eng in self.ENGS:
            fns = self.q[eng]

            def runner(e, fns=fns):
                for f in fns:
                    f(e)

            getattr(block, eng)(runner)


def build(cfg: Cfg, stop_after=None, dump=None):
    nc = bass.Bass("TRN2", target_bir_lowering=False)
    E, H, KVH, S, T = cfg.E, cfg.H, cfg.KVH, cfg.S, cfg.T
    FC, KC, QB, NQB, PW = cfg.FC, cfg.KC, cfg.QB, cfg.NQB, cfg.PW
    KVDIM, SBT, NSB, REP, NPAIR = cfg.KVDIM, cfg.SBT, cfg.NSB, cfg.REP, cfg.NPAIR

    x_in = nc.dram_tensor("x", [S, E], F32, kind="ExternalInput").ap()
    wq_in = nc.dram_tensor("wq", [E, E], F32, kind="ExternalInput").ap()
    wk_in = nc.dram_tensor("wk", [E, KVDIM], F32, kind="ExternalInput").ap()
    wv_in = nc.dram_tensor("wv", [E, KVDIM], F32, kind="ExternalInput").ap()
    wpos_in = nc.dram_tensor("wpos", [E, E], F32, kind="ExternalInput").ap()
    wneg_in = nc.dram_tensor("wneg", [E, E], F32, kind="ExternalInput").ap()
    bqT_in = nc.dram_tensor("bqT", [PD, FC], F32, kind="ExternalInput").ap()
    bkT_in = nc.dram_tensor("bkT", [PD, KVH], F32, kind="ExternalInput").ap()
    bv_in = nc.dram_tensor("bv", [1, KVDIM], F32, kind="ExternalInput").ap()
    cosq_in = nc.dram_tensor("cosq", [PD, T], F32, kind="ExternalInput").ap()
    sinq_in = nc.dram_tensor("sinq", [PD, T], F32, kind="ExternalInput").ap()
    cosk_in = nc.dram_tensor("cosk", [PD, S], F32, kind="ExternalInput").ap()
    sink_in = nc.dram_tensor("sink", [PD, S], F32, kind="ExternalInput").ap()
    perm_in = nc.dram_tensor("perm", [PD, PD], F32, kind="ExternalInput").ap()
    ident_in = nc.dram_tensor("ident", [PD, PD], BF16, kind="ExternalInput").ap()
    out_t = nc.dram_tensor("out_t", [E, T], F32, kind="ExternalOutput").ap()

    xT_dram = nc.dram_tensor("xT_dram", [E, S], BF16)
    kT_dram = nc.dram_tensor("kT_dram", [KVDIM, S], BF16)
    v_dram = nc.dram_tensor("v_dram", [S, KVDIM], BF16)
    xT_r = xT_dram.ap().rearrange("(fc p) t -> p fc t", p=PD)
    v_w = v_dram.ap().rearrange("(b p) kv -> p b kv", p=PD)

    with ExitStack() as top:
        P = Prog(nc, top)
        e = top.enter_context

        consts = e(nc.sbuf_tensor("consts", [PD, 8], F32))
        CONSTS = {}
        for v in (0.0, 1.0, cfg.scale, cfg.expbias, 1e-5, -0.5):
            v = float(v)
            if v not in CONSTS:
                i = len(CONSTS)
                CONSTS[v] = i
                nc.const_aps.register(F32, v, consts[:, i:i + 1])

        ones1f = e(nc.sbuf_tensor("ones1f", [1, PD], F32))
        ones128b = e(nc.sbuf_tensor("ones128b", [PD, 1], BF16))
        ones32b = e(nc.sbuf_tensor("ones32b", [32, 1], BF16))
        identb = e(nc.sbuf_tensor("identb", [PD, PD], BF16))
        bqT = e(nc.sbuf_tensor("bqT_s", [PD, FC], F32))
        bkT = e(nc.sbuf_tensor("bkT_s", [PD, KVH], F32))
        bv_row = e(nc.sbuf_tensor("bv_row", [1, KVDIM], F32))
        bvb = e(nc.sbuf_tensor("bvb", [PD, KVDIM], F32))
        permf = e(nc.sbuf_tensor("permf", [PD, PD], F32))

        for dst, src in ((identb, ident_in), (bqT, bqT_in), (bkT, bkT_in),
                         (bv_row, bv_in), (permf, perm_in)):
            P.do("sync", lambda s, d=dst, sp=src: s.dma_start(out=d[:], in_=sp),
                 sem="dtab", inc=16)
        NTAB = P.cnt["dtab"]

        for v, i in CONSTS.items():
            P.do("vector", lambda en, ap=consts[:, i:i + 1], v=v: en.memset(ap, v))
        P.do("vector", lambda en: en.memset(ones1f[:], 1.0))
        P.do("vector", lambda en: en.memset(ones128b[:], 1.0))
        tok_setup = P.do("vector", lambda en: en.memset(ones32b[:], 1.0))


        with nc.psum_tensor("ps_bv", [PD, 512], F32) as ps_bv:
            Wv = min(512, KVDIM)
            tok_prev = None
            for i in range(KVDIM // Wv):
                sl = slice(i * Wv, (i + 1) * Wv)
                tok_mm = P.do(
                    "tensor",
                    lambda t, sl=sl: t.matmul(ps_bv[:, 0:Wv], ones1f[:],
                                              bv_row[:, sl], start=True, stop=True),
                    waits=[("dtab", NTAB), ("dve", tok_setup),
                           ("dve", tok_prev) if tok_prev else None],
                    sem="pe")
                tok_prev = P.do(
                    "vector",
                    lambda v, sl=sl: v.tensor_copy(bvb[:, sl], ps_bv[:, 0:Wv]),
                    waits=[("pe", tok_mm)])
            tok_bvb = tok_prev

        P.cut("setup", stop_after)
        NTB = S // PD
        TGW = 4 * PD
        gdA = P.barrier()
        with ExitStack() as pa:
            ea = pa.enter_context
            xraw = [ea(nc.sbuf_tensor(f"xraw{i}", [PD, E], BF16)) for i in range(2)]
            stage = [ea(nc.sbuf_tensor(f"xstg{i}", [PD, FC, TGW], BF16)) for i in range(2)]
            ps_tr = [ea(nc.psum_tensor(f"ps_trA{i}", [PD, 1024], BF16)) for i in range(2)]

            tok_xr_use = {0: None, 1: None}
            tok_stg_st = {0: None, 1: None}
            tok_ptr_rd = {0: None, 1: None}

            for tg in range(NTB // 4):
                gpar = tg % 2
                last_ev = None
                for ti in range(4):
                    tb = tg * 4 + ti
                    par = tb % 2
                    tok_ld = P.do(
                        "gpsimd",
                        lambda g, tb=tb, par=par: g.dma_start(
                            out=xraw[par][:], in_=x_in[tb * PD:(tb + 1) * PD, :]),
                        waits=[("pe", tok_xr_use[par]) if tok_xr_use[par] else None],
                        sem=f"dx{par}", inc=16)
                    for fc in range(FC):
                        pp = fc % 2
                        tok_tr = P.do(
                            "tensor",
                            lambda t, par=par, fc=fc, pp=pp: t.transpose(
                                ps_tr[pp][:, 0:PD],
                                xraw[par][:, fc * PD:(fc + 1) * PD], identb[:]),
                            waits=[(f"dx{par}", tok_ld), ("dtab", NTAB),
                                   ("act", tok_ptr_rd[pp]) if tok_ptr_rd[pp] else None]
                                  + (gdA if tb == 0 else []),
                            sem="pe")
                        tok_ev = P.do(
                            "scalar",
                            lambda s, gpar=gpar, fc=fc, pp=pp, ti=ti: s.copy(
                                stage[gpar][:, fc, ti * PD:(ti + 1) * PD],
                                ps_tr[pp][:, 0:PD]),
                            waits=[("pe", tok_tr),
                                   (f"dso{gpar}", tok_stg_st[gpar])
                                   if tok_stg_st[gpar] else None])
                        tok_ptr_rd[pp] = tok_ev
                        last_ev = tok_ev
                    tok_xr_use[par] = P.cnt["pe"]
                nfc = max(1, FC // 4)
                for j in range(0, FC, nfc):
                    tok_st = P.do(
                        "sync",
                        lambda s, tg=tg, gpar=gpar, j=j, nfc=nfc: s.dma_start(
                            out=xT_r[:, j:j + nfc, tg * TGW:(tg + 1) * TGW],
                            in_=stage[gpar][:, j:j + nfc, :]),
                        waits=[("act", last_ev)], sem=f"dso{gpar}", inc=16)
                tok_stg_st[gpar] = tok_st
            NDSO0 = P.cnt.get("dso0", 0)
            NDSO1 = P.cnt.get("dso1", 0)

        P.cut("A", stop_after)
        def proj_pass(kind):
            gdB = P.barrier()
            with ExitStack() as pb:
                eb = pb.enter_context
                if kind == "K":
                    wb = eb(nc.sbuf_tensor("wkb", [PD, FC, KVDIM], BF16))
                    w_src = wk_in.rearrange("(fc p) kv -> p fc kv", p=PD)
                    cosk = eb(nc.sbuf_tensor("cosk_s", [PD, SBT], F32))
                    sink = eb(nc.sbuf_tensor("sink_s", [PD, SBT], F32))
                    kraw = [eb(nc.sbuf_tensor(f"kraw{i}", [PD, 512], F32)) for i in range(2)]
                    rsc = [eb(nc.sbuf_tensor(f"rsc{i}", [PD, 512], F32)) for i in range(2)]
                    ps_sw = [eb(nc.psum_tensor(f"ps_swK{i}", [PD, 512], F32))
                             for i in range(2)]
                    kst = [eb(nc.sbuf_tensor(f"kst{i}", [PD, SBT], BF16)) for i in range(2)]

                else:
                    wb = eb(nc.sbuf_tensor("wvb", [PD, FC, KVDIM], BF16))
                    w_src = wv_in.rearrange("(fc p) kv -> p fc kv", p=PD)
                    TB_G = min(8, SBT // PD)
                    vst = [eb(nc.sbuf_tensor(f"vst{i}", [PD, TB_G, KVDIM], BF16))
                           for i in range(2)]
                xsup = eb(nc.sbuf_tensor(f"xsup{kind}", [PD, FC, SBT], BF16))
                psb = [eb(nc.psum_tensor(f"psB{kind}{i}", [PD, 512], F32))
                       for i in range(4)]
                ntab_now = P.cnt["dtab"]

                wsem = f"dw{kind}"
                step = max(1, FC // 4)
                for fc2 in range(0, FC, step):
                    P.do("gpsimd", lambda g, fc2=fc2: g.dma_start(
                        out=wb[:, fc2:fc2 + step, :], in_=w_src[:, fc2:fc2 + step, :]),
                        waits=gdB + [("dks0", P.cnt.get("dks0", 0)),
                                     ("dks1", P.cnt.get("dks1", 0))],
                        sem=wsem, inc=16)
                nw = P.cnt[wsem]

                psb_free = [None] * 4
                tok_stage_st = {0: None, 1: None}
                st_sem = "dks" if kind == "K" else "dvs"
                xsup_pe_done = None
                kraw_rd = {0: None, 1: None}
                sw_free = {0: None, 1: None}
                ck_rope_done = None

                for sb in range(NSB):
                    nl = max(1, FC // 8)
                    for j in range(0, FC, nl):
                        P.do("sync", lambda s, sb=sb, j=j, nl=nl: s.dma_start(
                            out=xsup[:, j:j + nl, :],
                            in_=xT_r[:, j:j + nl, sb * SBT:(sb + 1) * SBT]),
                            waits=gdB + [("dso0", NDSO0), ("dso1", NDSO1),
                                   ("pe", xsup_pe_done) if xsup_pe_done else None],
                            sem=f"dxs{kind}", inc=16)
                    tok_xs = P.cnt[f"dxs{kind}"]

                    if kind == "K":
                        P.do("sync", lambda s, sb=sb: s.dma_start(
                            out=cosk[:], in_=cosk_in[:, sb * SBT:(sb + 1) * SBT]),
                            waits=gdB + [("dso0", NDSO0), ("dso1", NDSO1),
                                         ("dve", ck_rope_done) if ck_rope_done else None],
                            sem="dck", inc=16)
                        P.do("sync", lambda s, sb=sb: s.dma_start(
                            out=sink[:], in_=sink_in[:, sb * SBT:(sb + 1) * SBT]),
                            sem="dck", inc=16)
                        tok_ck = P.cnt["dck"]
                        TQK = SBT // 512
                        for g in range(KVH):
                            spar = (sb * KVH + g) % 2
                            last_rope = None
                            for tqp in range(0, max(2, TQK), 2):
                                bks = [tqp % 4, (tqp % 4) + 1][:min(2, TQK)]
                                for fc in range(FC):
                                    for jj, bk in enumerate(bks):
                                        P.do("tensor", lambda t, g=g, fc=fc, tq=tqp + jj, bk=bk: t.matmul(
                                            psb[bk][:],
                                            wb[:, fc, g * PD:(g + 1) * PD],
                                            xsup[:, fc, tq * 512:(tq + 1) * 512],
                                            start=(fc == 0), stop=(fc == FC - 1)),
                                            waits=[("dxs" + kind, tok_xs), (wsem, nw),
                                                   psb_free[bk] if fc == 0 else None],
                                            sem="pe")
                                tok_mm = P.cnt["pe"]
                                for jj, bk in enumerate(bks):
                                    tq = tqp + jj
                                    rp = tq % 2
                                    tok_ev = P.do("scalar", lambda s, bk=bk, g=g, rp=rp: s.add(
                                        kraw[rp][:], psb[bk][:], bkT[:, g:g + 1]),
                                        waits=[("pe", tok_mm), ("dtab", ntab_now)]
                                              + (kraw_rd[rp] or []))
                                    psb_free[bk] = ("act", tok_ev)
                                    cs = slice(tq * 512, (tq + 1) * 512)
                                    dst = kst[spar][:, tq * 512:(tq + 1) * 512]
                                    sp = tq % 2
                                    tok_sw = P.do("tensor", lambda t, rp=rp, sp=sp: t.matmul(
                                        ps_sw[sp][:], permf[:], kraw[rp][:],
                                        start=True, stop=True, skip_group_check=True),
                                        waits=[("act", tok_ev), ("dtab", NTAB),
                                               sw_free[sp]],
                                        sem="pe")
                                    w0 = [("act", tok_ev), ("dck", tok_ck),
                                          (st_sem + str(spar), tok_stage_st[spar])
                                          if (tok_stage_st[spar] and tq == 0) else None]
                                    tp1 = P.do("vector", lambda v, rp=rp, cs=cs: v.tensor_mul(
                                        rsc[0][:], kraw[rp][:], cosk[:, cs]),
                                        waits=w0)
                                    tp2 = P.do("vector", lambda v, sp=sp, cs=cs: v.tensor_mul(
                                        rsc[1][:], ps_sw[sp][:], sink[:, cs]),
                                        waits=[("pe", tok_sw)])
                                    sw_free[sp] = ("dve", tp2)
                                    to1 = P.do("vector", lambda v, dst=dst: v.tensor_add(
                                        dst[:], rsc[0][:], rsc[1][:]))
                                    kraw_rd[rp] = [("pe", tok_sw), ("dve", tp1)]
                                    last_rope = to1
                                    ck_rope_done = to1
                            tok_st = P.do("sync", lambda s, g=g, sb=sb, spar=spar: s.dma_start(
                                out=kT_dram.ap()[g * PD:(g + 1) * PD,
                                                 sb * SBT:(sb + 1) * SBT],
                                in_=kst[spar][:]),
                                waits=[("dve", last_rope)],
                                sem=st_sem + str(spar), inc=16)
                            tok_stage_st[spar] = tok_st
                    else:
                        NTBS = SBT // PD
                        TB_G = min(8, NTBS)
                        NDV = max(1, KVDIM // 512)
                        Wv = min(512, KVDIM)
                        for tb8 in range(0, NTBS, TB_G):
                            par = (sb * (NTBS // TB_G) + tb8 // TB_G) % 2
                            last_vev = None
                            for tb in range(tb8, tb8 + TB_G):
                                banks = [(tb % 2) * 2 + i for i in range(NDV)]
                                for fc in range(FC):
                                    for dvh in range(NDV):
                                        bk = banks[dvh]
                                        P.do("tensor", lambda t, tb=tb, fc=fc, dvh=dvh, bk=bk: t.matmul(
                                            psb[bk][:, 0:Wv],
                                            xsup[:, fc, tb * PD:(tb + 1) * PD],
                                            wb[:, fc, dvh * 512:dvh * 512 + Wv],
                                            start=(fc == 0), stop=(fc == FC - 1)),
                                            waits=[("dxs" + kind, tok_xs), (wsem, nw),
                                                   psb_free[bk] if fc == 0 else None,
                                                   (st_sem + str(par), tok_stage_st[par])
                                                   if (fc == 0 and tb == tb8 and tok_stage_st[par]) else None],
                                            sem="pe")
                                tok_mm = P.cnt["pe"]
                                for dvh in range(NDV):
                                    bk = banks[dvh]
                                    tok_ev = P.do("vector", lambda v, tb=tb, tb8=tb8, dvh=dvh, bk=bk, par=par: v.tensor_add(
                                        vst[par][:, tb - tb8, dvh * 512:dvh * 512 + Wv],
                                        psb[bk][:, 0:Wv],
                                        bvb[:, dvh * 512:dvh * 512 + Wv]),
                                        waits=[("pe", tok_mm), ("dve", tok_bvb)])
                                    psb_free[bk] = ("dve", tok_ev)
                                    last_vev = tok_ev
                            tok_st = P.do("sync", lambda s, sb=sb, tb8=tb8, par=par: s.dma_start(
                                out=v_w[:, sb * NTBS + tb8:sb * NTBS + tb8 + TB_G, :],
                                in_=vst[par][:]),
                                waits=[("dve", last_vev)],
                                sem=st_sem + str(par), inc=16)
                            tok_stage_st[par] = tok_st
                    xsup_pe_done = P.cnt["pe"]

        proj_pass("K")
        proj_pass("V")
        NKS0, NKS1 = P.cnt.get("dks0", 0), P.cnt.get("dks1", 0)
        NVS0, NVS1 = P.cnt.get("dvs0", 0), P.cnt.get("dvs1", 0)

        if dump in ("kT", "v", "xT"):
            if dump == "kT":
                o = out_t.rearrange("(a b) t -> a (b t)", b=8192 // T)
                P.do("gpsimd", lambda g, o=o: g.dma_start(
                    out=o, in_=kT_dram.ap()[0:KVDIM // 2, :]),
                    waits=[("dks0", P.cnt.get("dks0", 0)), ("dks1", P.cnt.get("dks1", 0))],
                    sem="ddump", inc=16)
            elif dump == "v":
                P.do("gpsimd", lambda g: g.dma_start(
                    out=out_t, in_=v_dram.ap()[0:E, :]),
                    waits=[("dvs0", P.cnt.get("dvs0", 0)), ("dvs1", P.cnt.get("dvs1", 0))],
                    sem="ddump", inc=16)
            else:
                P.do("gpsimd", lambda g: g.dma_start(
                    out=out_t, in_=xT_dram.ap()[:, 0:T]),
                    waits=[("dso0", NDSO0), ("dso1", NDSO1)],
                    sem="ddump", inc=16)
            P.do("gpsimd", lambda g: g.nop(), waits=[("ddump", 16)])
            P.cut("DUMP", "DUMP")
        P.cut("B", stop_after)
        sT = e(nc.sbuf_tensor("sT", [PD, FC, T], BF16))
        gdT = P.barrier()
        nl = max(1, FC // 4)
        for j in range(0, FC, nl):
            P.do("sync", lambda s, j=j, nl=nl: s.dma_start(
                out=sT[:, j:j + nl, :], in_=xT_r[:, j:j + nl, 0:T]),
                waits=gdT + [("dso0", NDSO0), ("dso1", NDSO1),
                             ("dks0", NKS0), ("dks1", NKS1),
                             ("dvs0", NVS0), ("dvs1", NVS1)],
                sem="dsT", inc=16)
        NDST = P.cnt["dsT"]

        pcd = ExitStack()
        qT = pcd.enter_context(nc.sbuf_tensor("qT", [PD, H, T], BF16))
        cosq = pcd.enter_context(nc.sbuf_tensor("cosq_s", [PD, T], F32))
        sinq = pcd.enter_context(nc.sbuf_tensor("sinq_s", [PD, T], F32))
        gdC = P.barrier()
        P.do("sync", lambda s: s.dma_start(out=cosq[:], in_=cosq_in),
             waits=gdC + [("dks0", NKS0), ("dks1", NKS1),
                          ("dvs0", NVS0), ("dvs1", NVS1)],
             sem="dcq", inc=16)
        P.do("sync", lambda s: s.dma_start(out=sinq[:], in_=sinq_in),
             sem="dcq", inc=16)
        NCQ = P.cnt["dcq"]
        with ExitStack() as pc:
            ec = pc.enter_context
            HB = 2
            wqb = [ec(nc.sbuf_tensor(f"wqb{i}", [PD, FC, HB * PD], BF16)) for i in range(2)]
            qraw = [ec(nc.sbuf_tensor(f"qraw{i}", [PD, T], F32)) for i in range(2)]
            rscq = [ec(nc.sbuf_tensor(f"rscq{i}", [PD, T], F32)) for i in range(2)]
            psc = [ec(nc.psum_tensor(f"psC{i}", [PD, 512], F32)) for i in range(4)]
            ps_swq = [ec(nc.psum_tensor(f"ps_swQ{i}", [PD, 512], F32)) for i in range(2)]
            wq_r = wq_in.rearrange("(fc p) o -> p fc o", p=PD)

            tok_wq = {0: None, 1: None}
            tok_wq_use = {0: None, 1: None}
            psc_free = [None] * 4
            qraw_free = {0: None, 1: None}
            swq_free = {0: None, 1: None}
            qsw_last = {0: None, 1: None}

            for blki in range(H // HB):
                par = blki % 2
                nfc = max(1, FC // 4)
                for j in range(0, FC, nfc):
                    tok_wq[par] = P.do("gpsimd", lambda g, blki=blki, par=par, j=j, nfc=nfc: g.dma_start(
                        out=wqb[par][:, j:j + nfc, :],
                        in_=wq_r[:, j:j + nfc, blki * HB * PD:(blki + 1) * HB * PD]),
                        waits=[("pe", tok_wq_use[par]) if tok_wq_use[par] else None]
                              + (gdC + [("dks0", NKS0), ("dks1", NKS1),
                                        ("dvs0", NVS0), ("dvs1", NVS1)]
                                 if blki < 2 else []),
                        sem=f"dwq{par}", inc=16)
                for hh in range(HB):
                    h = blki * HB + hh
                    hp = h % 2
                    tok_ev = None
                    for qb in range(NQB):
                        bk = (h % 2) * 2 + (qb % 2)
                        for fc in range(FC):
                            P.do("tensor", lambda t, fc=fc, hh=hh, qb=qb, par=par, bk=bk: t.matmul(
                                psc[bk][:, 0:QB],
                                wqb[par][:, fc, hh * PD:(hh + 1) * PD],
                                sT[:, fc, qb * QB:(qb + 1) * QB],
                                start=(fc == 0), stop=(fc == FC - 1)),
                                waits=[(f"dwq{par}", tok_wq[par]), ("dsT", NDST),
                                       psc_free[bk] if fc == 0 else None,
                                       ("dve", qraw_free[hp])
                                       if (fc == 0 and qb == 0 and qraw_free[hp]) else None,
                                       qsw_last[hp] if (fc == 0 and qb == 0) else None],
                                sem="pe")
                        tok_mm = P.cnt["pe"]
                        tok_ev = P.do("scalar", lambda s, h=h, qb=qb, hp=hp, bk=bk: s.add(
                            qraw[hp][:, qb * QB:(qb + 1) * QB], psc[bk][:, 0:QB],
                            bqT[:, h:h + 1]),
                            waits=[("pe", tok_mm)])
                        psc_free[bk] = ("act", tok_ev)
                    dst = qT[:, h, :]
                    tok_sw = {}
                    for qb in range(NQB):
                        sp = qb % 2
                        tok_sw[qb] = P.do("tensor", lambda t, hp=hp, qb=qb, sp=sp: t.matmul(
                            ps_swq[sp][:, 0:QB], permf[:],
                            qraw[hp][:, qb * QB:(qb + 1) * QB],
                            start=True, stop=True, skip_group_check=True),
                            waits=[("act", tok_ev), ("dtab", NTAB),
                                   swq_free[sp]],
                            sem="pe")
                    P.do("vector", lambda v, hp=hp: v.tensor_mul(
                        rscq[0][:], qraw[hp][:], cosq[:]),
                        waits=[("act", tok_ev), ("dcq", NCQ)])
                    for qb in range(NQB):
                        sp = qb % 2
                        qsl = slice(qb * QB, (qb + 1) * QB)
                        t2 = P.do("vector", lambda v, sp=sp, qsl=qsl: v.tensor_mul(
                            rscq[1][:, qsl], ps_swq[sp][:, 0:QB], sinq[:, qsl]),
                            waits=[("pe", tok_sw[qb])])
                        swq_free[sp] = ("dve", t2)
                    tm4 = P.do("vector", lambda v, dst=dst: v.tensor_add(
                        dst[:], rscq[0][:], rscq[1][:]))
                    qraw_free[hp] = tm4
                    qsw_last[hp] = ("pe", tok_sw[NQB - 1])
                tok_wq_use[par] = P.cnt["pe"]
            tok_qT_dve = P.cnt["dve"]

        if dump == "qT":
            o = out_t.rearrange("(h p) t -> p h t", p=PD)
            P.do("gpsimd", lambda g, o=o: g.dma_start(out=o, in_=qT[:]),
                 waits=[("dve", P.cnt["dve"])], sem="ddump", inc=16)
            P.do("gpsimd", lambda g: g.nop(), waits=[("ddump", 16)])
            P.cut("DUMP", "DUMP")
        P.cut("C", stop_after)
        gdD = P.barrier()
        with ExitStack() as pd:
            ed = pd.enter_context
            kS = ed(nc.sbuf_tensor("kS", [PD, S], BF16))
            vS = ed(nc.sbuf_tensor("vS", [PD, KC, D], BF16))
            pT = [ed(nc.sbuf_tensor(f"pT{i}", [PD, PW], BF16)) for i in range(3)]
            acc = [ed(nc.sbuf_tensor(f"acc{i}", [PD, PW], F32)) for i in range(2)]
            accb = ed(nc.sbuf_tensor("accb", [PD, PW], BF16))
            rl = ed(nc.sbuf_tensor("rl", [1, QB], F32))
            bcN = ed(nc.sbuf_tensor("bcN", [PD, QB], F32))
            tmpf = ed(nc.sbuf_tensor("tmpf", [PD, QB], F32))
            tmpb = ed(nc.sbuf_tensor("tmpb", [PD, QB], BF16))
            expb_t = ed(nc.sbuf_tensor("expb_t", [PD, 1], F32))
            exps_t = ed(nc.sbuf_tensor("exps_t", [PD, 1], F32))
            sc = [ed(nc.psum_tensor(f"scD{i}", [PD, PW], F32)) for i in range(2)]
            oPs2 = [ed(nc.psum_tensor(f"oPs{i}", [PD, 512], F32)) for i in range(2)]
            lpsb = ed(nc.psum_tensor("lps", [PD, 512], F32))
            bcpb = ed(nc.psum_tensor("bcp", [PD, 512], F32))

            P.do("vector", lambda v: v.memset(expb_t[:], cfg.expbias))
            tok_expc = P.do("vector", lambda v: v.memset(exps_t[:], cfg.scale))
            tok_kv_pe_done = None
            sc_free = {0: None, 1: None}
            pT_free = [None, None, None]
            oPs_free = None
            prev_tree = {"accv": None, "accp": None}

            for g in range(KVH):
                P.do("sync", lambda s, g=g: s.dma_start(
                    out=kS[:], in_=kT_dram.ap()[g * PD:(g + 1) * PD, :]),
                    waits=gdD + [("dks0", NKS0), ("dks1", NKS1),
                           ("dvs0", NVS0), ("dvs1", NVS1),
                           ("pe", tok_kv_pe_done) if tok_kv_pe_done else None],
                    sem="dkv", inc=16)
                nkcl = max(1, KC // 8)
                for j in range(0, KC, nkcl):
                    P.do("sync", lambda s, g=g, j=j, nkcl=nkcl: s.dma_start(
                        out=vS[:, j:j + nkcl, :],
                        in_=v_w[:, j:j + nkcl, g * PD:(g + 1) * PD]),
                        sem="dkv", inc=16)
                tok_kv = P.cnt["dkv"]
                if g == 0:
                    P.do("sync", lambda s: s.nop(), waits=[("dkv", tok_kv)])
                    if dump == "vS":
                        o = out_t.rearrange("(a p) t -> p a t", p=PD)
                        vv = vS.ap().rearrange("p (c8 c) d -> p c8 (c d)", c8=8)
                        P.do("gpsimd", lambda g2, o=o, vv=vv: g2.dma_start(
                            out=o[:, 0:8, :], in_=vv),
                            waits=[("dkv", tok_kv)], sem="ddump", inc=16)
                        P.do("gpsimd", lambda g2: g2.nop(), waits=[("ddump", 16)])
                        P.cut("DUMP", "DUMP")
                    P.cut("D0", stop_after)

                for qb in range(NQB):
                    qsl = slice(qb * QB, (qb + 1) * QB)
                    for pairi in range(NPAIR):
                        h0 = g * REP + pairi * 2
                        h1 = h0 + 1
                        first_pair = (g == 0 and qb == 0 and pairi == 0)
                        tok_z0 = P.do("vector", lambda en: en.memset(acc[0][:], 0.0),
                                      waits=[("dve", prev_tree["accv"])
                                             if prev_tree["accv"] else None]
                                            + (gdD if first_pair else []))
                        if g == 0 and qb == 0 and pairi == 0:
                            P.cut("DM0", stop_after)
                        tok_exp = {}
                        tok_accop = {}
                        for kc in range(KC):
                            if g == 0 and qb == 0 and pairi == 0 and kc in (1, 2, 4, 16):
                                P.cut(f"DK{kc}", stop_after)
                            if (g == 0 and qb == 0 and pairi == 0 and kc == 16
                                    and stop_after in ("DNOPV", "DQK", "DQE")):
                                P.cut(stop_after, stop_after)
                            par = kc % 2
                            pi = kc % 3
                            P.do("tensor", lambda t, kc=kc, h0=h0, par=par, qsl=qsl: t.matmul(
                                sc[par][:, 0:QB], kS[:, kc * PD:(kc + 1) * PD],
                                qT[:, h0, qsl], start=True, stop=True,
                                skip_group_check=True),
                                waits=[("dkv", tok_kv), ("dve", tok_qT_dve),
                                       sc_free[par]],
                                sem="pe")
                            tok_qk = P.do("tensor", lambda t, kc=kc, h1=h1, par=par, qsl=qsl: t.matmul(
                                sc[par][:, QB:PW], kS[:, kc * PD:(kc + 1) * PD],
                                qT[:, h1, qsl], start=True, stop=True,
                                skip_group_check=True),
                                sem="pe")
                            if kc >= 1 and stop_after not in ("DNOPV", "DQK", "DQE"):
                                kcp = kc - 1
                                pip = kcp % 3
                                P.do("tensor", lambda t, kcp=kcp, pip=pip: t.matmul(
                                    oPs2[0][:, 0:QB], vS[:, kcp, :], pT[pip][:, 0:QB],
                                    start=(kcp == 0), stop=False,
                                    skip_group_check=True),
                                    waits=[("act", tok_exp[kcp]),
                                           oPs_free if kcp == 0 else None],
                                    sem="pe")
                                P.do("tensor", lambda t, kcp=kcp, pip=pip: t.matmul(
                                    oPs2[1][:, 0:QB], vS[:, kcp, :], pT[pip][:, QB:PW],
                                    start=(kcp == 0), stop=False,
                                    skip_group_check=True),
                                    sem="pe")
                            if g == 0 and qb == 0 and pairi == 0 and kc == 0:
                                if stop_after == "DQ1W":
                                    P.do("sync", lambda s: s.nop(),
                                         waits=[("pe", tok_qk)])
                                    P.cut("DQ1W", stop_after)
                                P.cut("DQ1", stop_after)
                            if stop_after == "DQK":
                                sc_free[par] = ("pe", tok_qk)
                                continue
                            tok_exp[kc] = P.do("scalar", lambda s, par=par, pi=pi: s.activation(
                                pT[pi][:], sc[par][:], AF.Exp,
                                bias=expb_t[:], scale=exps_t[:]),
                                waits=[("pe", tok_qk), pT_free[pi],
                                       ("dve", tok_expc)])
                            sc_free[par] = ("act", tok_exp[kc])
                            if g == 0 and qb == 0 and pairi == 0 and kc == 0:
                                P.cut("DE1", stop_after)
                            if stop_after == "DQE":
                                pT_free[pi] = ("act", tok_exp[kc])
                                continue
                            tok_accop[kc] = P.do("vector", lambda en, pi=pi: en.tensor_add(
                                acc[0][:], acc[0][:], pT[pi][:]),
                                waits=[("act", tok_exp[kc]),
                                       ("dve", tok_z0) if kc == 0 else None])
                            pT_free[pi] = ("dve", tok_accop[kc])
                        if g == 0 and qb == 0 and pairi == 0:
                            P.cut("D1", stop_after)
                        kcp = KC - 1
                        pip = kcp % 3
                        P.do("tensor", lambda t, kcp=kcp, pip=pip: t.matmul(
                            oPs2[0][:, 0:QB], vS[:, kcp, :], pT[pip][:, 0:QB],
                            start=(kcp == 0), stop=True, skip_group_check=True),
                            waits=[("act", tok_exp.get(kcp)),
                                   oPs_free if kcp == 0 else None],
                            sem="pe")
                        tok_pvl = P.do("tensor", lambda t, kcp=kcp, pip=pip: t.matmul(
                            oPs2[1][:, 0:QB], vS[:, kcp, :], pT[pip][:, QB:PW],
                            start=(kcp == 0), stop=True, skip_group_check=True),
                            sem="pe")

                        tok_t3 = P.do("vector", lambda v: v.tensor_copy(
                            accb[:], acc[0][:]),
                            waits=[("dve", tok_accop.get(KC - 1))])
                        prev_tree["accv"] = tok_t3
                        last_cp = None
                        for hh, h in enumerate((h0, h1)):
                            hsl = slice(hh * QB, (hh + 1) * QB)
                            tok_lp = P.do("tensor", lambda t, hsl=hsl: t.matmul(
                                lpsb[0:1, 0:QB], ones128b[:], accb[:, hsl],
                                start=True, stop=True, skip_group_check=True),
                                waits=[("dve", tok_t3)], sem="pe")
                            tok_rl = P.do("vector", lambda v: v.reciprocal(
                                rl[:], lpsb[0:1, 0:QB]),
                                waits=[("pe", tok_lp)])
                            tok_bc = P.do("tensor", lambda t: t.matmul(
                                bcpb[:, 0:QB], ones1f[:], rl[:],
                                start=True, stop=True, skip_group_check=True),
                                waits=[("dve", tok_rl)], sem="pe")
                            tok_bcc = P.do("vector", lambda v: v.tensor_copy(
                                bcN[:], bcpb[:, 0:QB]),
                                waits=[("pe", tok_bc)])
                            P.do("vector", lambda v, hh=hh: v.tensor_mul(
                                tmpf[:], oPs2[hh][:, 0:QB], bcN[:]),
                                waits=[("pe", tok_pvl)])
                            P.do("vector", lambda v, h=h, qsl=qsl: v.tensor_add(
                                tmpb[:], tmpf[:], sT[:, h, qsl]))
                            last_cp = P.do("vector", lambda v, h=h, qsl=qsl: v.tensor_copy(
                                sT[:, h, qsl], tmpb[:]))
                        oPs_free = ("dve", last_cp)
                        if g == 0 and qb == 0 and pairi == 0 and dump == "lpath":
                            t_e1 = P.do("vector", lambda v: v.tensor_copy(
                                tmpf[:], oPs2[0][:, 0:QB]))
                            P.do("gpsimd", lambda g2: g2.dma_start(
                                out=out_t[0:PD, 0:QB], in_=tmpf[:]),
                                waits=[("dve", t_e1)], sem="ddump", inc=16)
                            P.do("gpsimd", lambda g2: g2.dma_start(
                                out=out_t[PD:2 * PD, 0:PW], in_=accb[:]),
                                waits=[("dve", t_e1)], sem="ddump", inc=16)
                            P.do("gpsimd", lambda g2: g2.dma_start(
                                out=out_t[256:257, 0:QB], in_=rl[:]),
                                waits=[("dve", t_e1)], sem="ddump", inc=16)
                            P.do("gpsimd", lambda g2: g2.dma_start(
                                out=out_t[384:384 + PD, 0:QB], in_=bcN[:]),
                                waits=[("dve", t_e1)], sem="ddump", inc=16)
                            P.do("gpsimd", lambda g2: g2.nop(), waits=[("ddump", 64)])
                            P.cut("DUMP", "DUMP")
                        if g == 0 and qb == 0 and pairi == 0:
                            P.cut("D2", stop_after)
                tok_kv_pe_done = P.cnt["pe"]
                if g == 0 and dump == "vSbusy":
                    o = out_t.rearrange("(a p) t -> p a t", p=PD)
                    vv = vS.ap().rearrange("p (c8 c) d -> p c8 (c d)", c8=8)
                    P.do("gpsimd", lambda g2, o=o, vv=vv: g2.dma_start(
                        out=o[:, 0:8, :], in_=vv),
                        waits=[("pe", tok_kv_pe_done), ("dve", P.cnt["dve"])],
                        sem="ddump", inc=16)
                    P.do("gpsimd", lambda g2: g2.nop(), waits=[("ddump", 16)])
                    P.cut("DUMP", "DUMP")
        pcd.close()

        if dump == "sT":
            o = out_t.rearrange("(h p) t -> p h t", p=PD)
            P.do("gpsimd", lambda g, o=o: g.dma_start(out=o, in_=sT[:]),
                 waits=[("dve", P.cnt["dve"])], sem="ddump", inc=16)
            P.do("gpsimd", lambda g: g.nop(), waits=[("ddump", 16)])
            P.cut("DUMP", "DUMP")
        P.cut("D", stop_after)
        lnt = e(nc.sbuf_tensor("lntmp", [PD, QB], F32))

        def emit_ln(src_chunks, apply_out, tag, entry_list):
            with ExitStack() as pl:
                el = pl.enter_context
                sq = [el(nc.sbuf_tensor(f"sq{tag}{i}", [PD, QB], BF16)) for i in range(2)]
                rows = el(nc.sbuf_tensor(f"rows{tag}", [1, 6 * QB], F32))
                a_sb = el(nc.sbuf_tensor(f"asb{tag}", [PD, QB], F32))
                b_sb = el(nc.sbuf_tensor(f"bsb{tag}", [PD, QB], F32))
                eps_t = el(nc.sbuf_tensor(f"eps{tag}", [1, 1], F32))
                nh_t = el(nc.sbuf_tensor(f"nh{tag}", [1, 1], F32))
                ps1 = el(nc.psum_tensor(f"ps1{tag}", [PD, 512], F32))
                ps2 = el(nc.psum_tensor(f"ps2{tag}", [PD, 512], F32))
                psb2 = el(nc.psum_tensor(f"psbb{tag}", [PD, 512], F32))

                mrow = rows[:, 0:QB]
                msrow = rows[:, QB:2 * QB]
                m2row = rows[:, 2 * QB:3 * QB]
                varrow = rows[:, 3 * QB:4 * QB]
                arow = rows[:, 4 * QB:5 * QB]
                brow = rows[:, 5 * QB:6 * QB]
                sq_free = {0: None, 1: None}
                P.do("vector", lambda v: v.memset(eps_t[:], 1e-5))
                tok_lnc = P.do("vector", lambda v: v.memset(nh_t[:], -0.5))

                for qb in range(NQB):
                    qsl = slice(qb * QB, (qb + 1) * QB)
                    tok_s1 = tok_s2 = None
                    for fc in range(FC):
                        sp = fc % 2
                        src = src_chunks(fc)
                        tok_sq = P.do("vector", lambda v, src=src, sp=sp, qsl=qsl: v.tensor_mul(
                            sq[sp][:], src[:, qsl], src[:, qsl]),
                            waits=entry_list + [sq_free[sp]])
                        tok_s1 = P.do("tensor", lambda t, src=src, qsl=qsl, fc=fc: t.matmul(
                            ps1[0:1, 0:QB], ones128b[:], src[:, qsl],
                            start=(fc == 0), stop=(fc == FC - 1),
                            skip_group_check=True),
                            waits=entry_list + [("dve", tok_setup)], sem="pe")
                        tok_s2 = P.do("tensor", lambda t, sp=sp, fc=fc: t.matmul(
                            ps2[0:1, 0:QB], ones128b[:], sq[sp][:],
                            start=(fc == 0), stop=(fc == FC - 1),
                            skip_group_check=True),
                            waits=[("dve", tok_sq)], sem="pe")
                        sq_free[sp] = ("pe", tok_s2)
                    inv = 1.0 / float(E)
                    P.do("vector", lambda v, inv=inv: v.tensor_scalar_mul(
                        mrow, ps1[0:1, 0:QB], inv), waits=[("pe", tok_s1)])
                    P.do("vector", lambda v, inv=inv: v.tensor_scalar_mul(
                        msrow, ps2[0:1, 0:QB], inv), waits=[("pe", tok_s2)])
                    P.do("vector", lambda v: v.tensor_mul(m2row, mrow, mrow))
                    t_var = P.do("vector", lambda v: v.tensor_sub(varrow, msrow, m2row))
                    t_ln = P.do("scalar", lambda s: s.activation(
                        arow, varrow, AF.Ln, bias=eps_t[0:1, :], scale=1.0),
                        waits=[("dve", t_var), ("dve", tok_lnc)])
                    t_rs = P.do("scalar", lambda s: s.activation(
                        arow, arow, AF.Exp, bias=0.0, scale=nh_t[0:1, :]))
                    t_mr = P.do("vector", lambda v: v.tensor_mul(brow, mrow, arow),
                                waits=[("act", t_rs)])
                    t_nb = P.do("vector", lambda v: v.tensor_scalar_mul(brow, brow, -1.0))
                    t_bca = P.do("tensor", lambda t: t.matmul(
                        psb2[:, 0:QB], ones1f[:], arow, start=True, stop=True,
                        skip_group_check=True),
                        waits=[("dve", t_nb)], sem="pe")
                    t_ca = P.do("vector", lambda v: v.tensor_copy(a_sb[:], psb2[:, 0:QB]),
                                waits=[("pe", t_bca)])
                    t_bcb = P.do("tensor", lambda t: t.matmul(
                        psb2[:, 0:QB], ones1f[:], brow, start=True, stop=True,
                        skip_group_check=True),
                        waits=[("dve", t_ca)], sem="pe")
                    t_cb = P.do("vector", lambda v: v.tensor_copy(b_sb[:], psb2[:, 0:QB]),
                                waits=[("pe", t_bcb)])
                    for fc in range(FC):
                        apply_out(fc, qb, a_sb, b_sb, [("dve", t_cb)])

        def ln1_apply(fc, qb, a_sb, b_sb, waits):
            qsl = slice(qb * QB, (qb + 1) * QB)
            P.do("vector", lambda v, fc=fc, qsl=qsl: v.tensor_mul(
                lnt[:], sT[:, fc, qsl], a_sb[:]), waits=waits)
            P.do("vector", lambda v, fc=fc, qsl=qsl: v.tensor_add(
                sT[:, fc, qsl], lnt[:], b_sb[:]))

        emit_ln(lambda fc: sT[:, fc, :], ln1_apply, "1", P.barrier())
        tok_ln1 = P.cnt["dve"]

        if dump == "sTe":
            o = out_t.rearrange("(h p) t -> p h t", p=PD)
            P.do("gpsimd", lambda g, o=o: g.dma_start(out=o, in_=sT[:]),
                 waits=[("dve", P.cnt["dve"])], sem="ddump", inc=16)
            P.do("gpsimd", lambda g: g.nop(), waits=[("ddump", 16)])
            P.cut("DUMP", "DUMP")
        P.cut("E", stop_after)
        v2T = e(nc.sbuf_tensor("v2T", [PD, FC, T], BF16))
        gdF = P.barrier()
        with ExitStack() as pf:
            ef = pf.enter_context
            wpb = [ef(nc.sbuf_tensor(f"wpb{i}", [PD, E], BF16)) for i in range(2)]
            wnb = [ef(nc.sbuf_tensor(f"wnb{i}", [PD, E], BF16)) for i in range(2)]
            bin2 = ef(nc.sbuf_tensor("bin2", [PD, E], BF16))
            wd = ef(nc.sbuf_tensor("wd", [PD, E], BF16))
            wdT = [ef(nc.sbuf_tensor(f"wdT{i}", [PD, FC, PD], BF16)) for i in range(2)]
            psf = [ef(nc.psum_tensor(f"psF{i}", [PD, 512], F32)) for i in range(2)]
            pst = [ef(nc.psum_tensor(f"psFT{i}", [PD, 1024], BF16)) for i in range(2)]

            tok_w_use = {0: None, 1: None}
            tok_wd_use_any = None
            tok_wdT_use = {0: None, 1: None}
            tok_pst_rd = {0: None, 1: None}
            psf_free = {0: None, 1: None}

            for ob in range(FC):
                par = ob % 2
                t_lp = P.do("gpsimd", lambda g, ob=ob, par=par: g.dma_start(
                    out=wpb[par][:], in_=wpos_in[ob * PD:(ob + 1) * PD, :]),
                    waits=[("dve", tok_w_use[par]) if tok_w_use[par] else None]
                          + (gdF if ob < 2 else []),
                    sem=f"dfp{par}", inc=16)
                t_lnn = P.do("gpsimd", lambda g, ob=ob, par=par: g.dma_start(
                    out=wnb[par][:], in_=wneg_in[ob * PD:(ob + 1) * PD, :]),
                    sem=f"dfn{par}", inc=16)
                P.do("vector", lambda v, par=par: v.tensor_single_scalar(
                    bin2[:], wnb[par][:], 0.0, ALU.is_gt),
                    waits=[(f"dfn{par}", t_lnn)])
                P.do("vector", lambda v, par=par: v.tensor_single_scalar(
                    wd[:], wpb[par][:], 0.0, ALU.is_gt),
                    waits=[(f"dfp{par}", t_lp),
                           ("pe", tok_wd_use_any) if tok_wd_use_any else None])
                t_wd = P.do("vector", lambda v: v.tensor_sub(
                    wd[:], wd[:], bin2[:]))
                tok_w_use[par] = t_wd
                last_ev = None
                for fc in range(FC):
                    pp = fc % 2
                    t_tr = P.do("tensor", lambda t, fc=fc, pp=pp: t.transpose(
                        pst[pp][:, 0:PD], wd[:, fc * PD:(fc + 1) * PD], identb[:]),
                        waits=[("dve", t_wd),
                               ("dve", tok_pst_rd[pp]) if tok_pst_rd[pp] else None,
                               ("pe", tok_wdT_use[par])
                               if (fc == 0 and tok_wdT_use[par]) else None],
                        sem="pe")
                    t_evt = P.do("vector", lambda v, par=par, fc=fc, pp=pp: v.tensor_copy(
                        wdT[par][:, fc, :], pst[pp][:, 0:PD]),
                        waits=[("pe", t_tr)])
                    tok_pst_rd[pp] = t_evt
                    last_ev = t_evt
                tok_wd_use_any = P.cnt["pe"]
                for fc in range(FC):
                    for qb in range(NQB):
                        P.do("tensor", lambda t, par=par, fc=fc, qb=qb: t.matmul(
                            psf[qb % 2][:, 0:QB], wdT[par][:, fc, :],
                            sT[:, fc, qb * QB:(qb + 1) * QB],
                            start=(fc == 0), stop=(fc == FC - 1),
                            skip_group_check=True),
                            waits=[("dve", last_ev), ("dve", tok_ln1),
                                   psf_free[qb % 2] if fc == 0 else None],
                            sem="pe")
                tok_mm = P.cnt["pe"]
                tok_wdT_use[par] = tok_mm
                for qb in range(NQB):
                    qsl = slice(qb * QB, (qb + 1) * QB)
                    t_ev = P.do("vector", lambda v, ob=ob, qb=qb, qsl=qsl: v.tensor_add(
                        v2T[:, ob, qsl], psf[qb % 2][:, 0:QB], sT[:, ob, qsl]),
                        waits=[("pe", tok_mm)])
                    psf_free[qb % 2] = ("dve", t_ev)

        if dump == "v2T":
            o = out_t.rearrange("(h p) t -> p h t", p=PD)
            P.do("gpsimd", lambda g, o=o: g.dma_start(out=o, in_=v2T[:]),
                 waits=[("dve", P.cnt["dve"])], sem="ddump", inc=16)
            P.do("gpsimd", lambda g: g.nop(), waits=[("ddump", 16)])
            P.cut("DUMP", "DUMP")
        P.cut("F", stop_after)
        outstage = [e(nc.sbuf_tensor(f"outs{i}", [PD, QB], F32)) for i in range(2)]
        out_st = {0: None, 1: None}
        _ln2_n = [0]

        def ln2_apply(fc, qb, a_sb, b_sb, waits):
            par = _ln2_n[0] % 2
            _ln2_n[0] += 1
            qsl = slice(qb * QB, (qb + 1) * QB)
            w = list(waits)
            if out_st[par] is not None:
                w.append((f"dob{par}", out_st[par]))
            P.do("vector", lambda v, fc=fc, qsl=qsl: v.tensor_mul(
                lnt[:], v2T[:, fc, qsl], a_sb[:]), waits=w)
            t2 = P.do("vector", lambda v, par=par: v.tensor_add(
                outstage[par][:], lnt[:], b_sb[:]))
            out_st[par] = P.do("sync", lambda s, fc=fc, qsl=qsl, par=par: s.dma_start(
                out=out_t[fc * PD:(fc + 1) * PD, qsl], in_=outstage[par][:]),
                waits=[("dve", t2)], sem=f"dob{par}", inc=16)

        emit_ln(lambda fc: v2T[:, fc, :], ln2_apply, "2", P.barrier())

        P.do("sync", lambda s: s.nop(),
             waits=[("dob0", P.cnt.get("dob0", 0)), ("dob1", P.cnt.get("dob1", 0))])

        blk = e(nc.Block())
        P.finalize(blk)
    return nc



def _rope_tables(positions):
    j = np.arange(0, D, 2, dtype=np.float32) / np.float32(D)
    inv_freq = (np.float32(1.0) / (np.float32(10000.0) ** j)).astype(np.float32)
    ang = positions.astype(np.float32)[:, None] * inv_freq[None, :]
    cos = np.cos(ang).T.astype(np.float32)
    sin = np.sin(ang).T.astype(np.float32)
    c2 = np.concatenate([cos, cos], axis=0)
    s2m = np.concatenate([-sin, sin], axis=0)
    return np.ascontiguousarray(c2), np.ascontiguousarray(s2m)


def make_inputs(cfg, core, x, wq, bq, wk, bk, wv, bv, w_pos, w_neg):
    S, T = cfg.S, cfg.T
    t0 = core * T
    perm = np.concatenate([np.arange(t0, t0 + T), np.arange(0, t0),
                           np.arange(t0 + T, S)])
    xr = np.ascontiguousarray(x[perm])
    cosk, sink = _rope_tables(perm.astype(np.float32))
    return {
        "x": xr,
        "wq": wq, "wk": wk, "wv": wv,
        "wpos": w_pos, "wneg": w_neg,
        "bqT": np.ascontiguousarray(bq.reshape(cfg.FC, PD).T),
        "bkT": np.ascontiguousarray(bk.reshape(cfg.KVH, PD).T),
        "bv": np.ascontiguousarray(bv.reshape(1, cfg.KVDIM)),
        "cosq": np.ascontiguousarray(cosk[:, 0:T]),
        "sinq": np.ascontiguousarray(sink[:, 0:T]),
        "cosk": cosk, "sink": sink,
        "ident": np.eye(PD, dtype=BF),
        "perm": np.ascontiguousarray(
            np.eye(PD, dtype=np.float32)[:, list(range(64, PD)) + list(range(64))]),
    }


def kernel(L0_attn_x, L0_attn_wq, L0_attn_bq, L0_attn_wk, L0_attn_bk,
           L0_attn_wv, L0_attn_bv, L0_ffn_w_pos, L0_ffn_w_neg):
    cfg = FULL
    x = np.ascontiguousarray(np.asarray(L0_attn_x, dtype=np.float32).reshape(cfg.S, cfg.E))
    wq = np.ascontiguousarray(np.asarray(L0_attn_wq, dtype=np.float32))
    bq = np.asarray(L0_attn_bq, dtype=np.float32)
    wk = np.ascontiguousarray(np.asarray(L0_attn_wk, dtype=np.float32))
    bk = np.asarray(L0_attn_bk, dtype=np.float32)
    wv = np.ascontiguousarray(np.asarray(L0_attn_wv, dtype=np.float32))
    bv = np.asarray(L0_attn_bv, dtype=np.float32)
    wp = np.ascontiguousarray(np.asarray(L0_ffn_w_pos, dtype=np.float32))
    wn = np.ascontiguousarray(np.asarray(L0_ffn_w_neg, dtype=np.float32))

    nc = build(cfg)
    in_maps = [make_inputs(cfg, c, x, wq, bq, wk, bk, wv, bv, wp, wn)
               for c in range(cfg.R)]
    res = run_bass_kernel_spmd(nc, in_maps, core_ids=list(range(cfg.R)))
    out = np.concatenate(
        [np.asarray(res.results[c]["out_t"]).T for c in range(cfg.R)], axis=0)
    return out.reshape(1, cfg.S, cfg.E).astype(np.float32)


# version 09 (final)
# speedup vs baseline: 1.4206x; 1.0234x over previous
import math
from contextlib import ExitStack

import numpy as np
import ml_dtypes

import concourse.bass as bass
import concourse.mybir as mybir
from concourse.bass_utils import run_bass_kernel_spmd

F32 = mybir.dt.float32
BF16 = mybir.dt.bfloat16
AF = mybir.ActivationFunctionType
ALU = mybir.AluOpType
BF = ml_dtypes.bfloat16

D = 128
PD = 128


class Cfg:
    def __init__(self, E, H, KVH, S, R, T):
        self.E, self.H, self.KVH, self.S, self.R, self.T = E, H, KVH, S, R, T
        self.REP = H // KVH
        self.FC = E // PD
        self.KC = S // PD
        self.QB = min(512, T)
        self.NQB = T // self.QB
        self.PW = 2 * self.QB
        self.KVDIM = KVH * D
        self.SBT = min(1024, S)
        self.NSB = S // self.SBT
        self.NPAIR = self.REP // 2
        assert self.REP == 4 and E % PD == 0 and S % PD == 0 and T % self.QB == 0
        self.scale = 1.0 / math.sqrt(float(D))
        self.expbias = -20.0


FULL = Cfg(E=4096, H=32, KVH=8, S=8192, R=8, T=1024)

TICK = {"vector": "dve", "gpsimd": "pool", "scalar": "act", "tensor": "pe"}
CHAINED = {"vector", "gpsimd", "scalar"}


class Prog:
    ENGS = ("sync", "scalar", "vector", "tensor", "gpsimd")

    def __init__(self, nc, stack):
        self.nc = nc
        self.stack = stack
        self.q = {k: [] for k in self.ENGS}
        self.handles = {}
        self.cnt = {}
        self.lastw = {}
        self.enabled = True

    def cut(self, label, stop_after):
        if stop_after is not None and label == stop_after:
            self.enabled = False

    def sem(self, name):
        if name not in self.handles:
            self.handles[name] = self.stack.enter_context(self.nc.semaphore(name))
            self.cnt.setdefault(name, 0)
        return self.handles[name]

    def wait(self, eng, name, val):
        if val is None or val <= 0:
            return
        if self.lastw.get((eng, name), 0) >= val:
            return
        assert val <= self.cnt.get(name, 0), (
            f"wait({eng},{name},{val}) > scheduled {self.cnt.get(name, 0)}")
        self.lastw[(eng, name)] = val
        h = self.sem(name)
        if self.enabled:
            self.q[eng].append(lambda e, h=h, v=val: e.wait_ge(h, v))

    def do(self, eng, fn, waits=(), sem=None, inc=1):
        if eng in CHAINED:
            t = TICK[eng]
            if sem is None:
                sem = t
            if sem == t and self.cnt.get(t, 0) > 0:
                self.wait(eng, t, self.cnt[t])
        for wv in waits:
            if wv is None or wv[1] is None:
                continue
            self.wait(eng, wv[0], wv[1])
        if sem is None:
            if self.enabled:
                self.q[eng].append(fn)
            return None
        h = self.sem(sem)
        self.cnt[sem] += inc
        tok = self.cnt[sem]
        if self.enabled:
            self.q[eng].append(lambda e, h=h, fn=fn, inc=inc: fn(e).then_inc(h, inc))
        return tok

    def barrier(self):
        return [(s, self.cnt[s]) for s in ("pe", "act", "dve", "pool")
                if self.cnt.get(s, 0) > 0]

    def finalize(self, block):
        for eng in self.ENGS:
            fns = self.q[eng]

            def runner(e, fns=fns):
                for f in fns:
                    f(e)

            getattr(block, eng)(runner)


def build(cfg: Cfg, stop_after=None, dump=None):
    nc = bass.Bass("TRN2", target_bir_lowering=False)
    E, H, KVH, S, T = cfg.E, cfg.H, cfg.KVH, cfg.S, cfg.T
    FC, KC, QB, NQB, PW = cfg.FC, cfg.KC, cfg.QB, cfg.NQB, cfg.PW
    KVDIM, SBT, NSB, REP, NPAIR = cfg.KVDIM, cfg.SBT, cfg.NSB, cfg.REP, cfg.NPAIR

    x_in = nc.dram_tensor("x", [S, E], F32, kind="ExternalInput").ap()
    wq_in = nc.dram_tensor("wq", [E, E], F32, kind="ExternalInput").ap()
    wk_in = nc.dram_tensor("wk", [E, KVDIM], F32, kind="ExternalInput").ap()
    wv_in = nc.dram_tensor("wv", [E, KVDIM], F32, kind="ExternalInput").ap()
    wpos_in = nc.dram_tensor("wpos", [E, E], F32, kind="ExternalInput").ap()
    wneg_in = nc.dram_tensor("wneg", [E, E], F32, kind="ExternalInput").ap()
    bqT_in = nc.dram_tensor("bqT", [PD, FC], F32, kind="ExternalInput").ap()
    bkT_in = nc.dram_tensor("bkT", [PD, KVH], F32, kind="ExternalInput").ap()
    bv_in = nc.dram_tensor("bv", [1, KVDIM], F32, kind="ExternalInput").ap()
    cosq_in = nc.dram_tensor("cosq", [PD, T], F32, kind="ExternalInput").ap()
    sinq_in = nc.dram_tensor("sinq", [PD, T], F32, kind="ExternalInput").ap()
    cosk_in = nc.dram_tensor("cosk", [PD, S], F32, kind="ExternalInput").ap()
    sink_in = nc.dram_tensor("sink", [PD, S], F32, kind="ExternalInput").ap()
    perm_in = nc.dram_tensor("perm", [PD, PD], F32, kind="ExternalInput").ap()
    ident_in = nc.dram_tensor("ident", [PD, PD], BF16, kind="ExternalInput").ap()
    out_t = nc.dram_tensor("out_t", [E, T], F32, kind="ExternalOutput").ap()

    xT_dram = nc.dram_tensor("xT_dram", [E, S], BF16)
    kT_dram = nc.dram_tensor("kT_dram", [KVDIM, S], BF16)
    v_dram = nc.dram_tensor("v_dram", [S, KVDIM], BF16)
    xT_r = xT_dram.ap().rearrange("(fc p) t -> p fc t", p=PD)
    v_w = v_dram.ap().rearrange("(b p) kv -> p b kv", p=PD)

    with ExitStack() as top:
        P = Prog(nc, top)
        e = top.enter_context

        consts = e(nc.sbuf_tensor("consts", [PD, 8], F32))
        CONSTS = {}
        for v in (0.0, 1.0, cfg.scale, cfg.expbias, 1e-5, -0.5):
            v = float(v)
            if v not in CONSTS:
                i = len(CONSTS)
                CONSTS[v] = i
                nc.const_aps.register(F32, v, consts[:, i:i + 1])

        ones1f = e(nc.sbuf_tensor("ones1f", [1, PD], F32))
        ones128b = e(nc.sbuf_tensor("ones128b", [PD, 1], BF16))
        ones32b = e(nc.sbuf_tensor("ones32b", [32, 1], BF16))
        identb = e(nc.sbuf_tensor("identb", [PD, PD], BF16))
        bqT = e(nc.sbuf_tensor("bqT_s", [PD, FC], F32))
        bkT = e(nc.sbuf_tensor("bkT_s", [PD, KVH], F32))
        bv_row = e(nc.sbuf_tensor("bv_row", [1, KVDIM], F32))
        bvb = e(nc.sbuf_tensor("bvb", [PD, KVDIM], F32))
        permf = e(nc.sbuf_tensor("permf", [PD, PD], F32))

        for dst, src in ((identb, ident_in), (bqT, bqT_in), (bkT, bkT_in),
                         (bv_row, bv_in), (permf, perm_in)):
            P.do("sync", lambda s, d=dst, sp=src: s.dma_start(out=d[:], in_=sp),
                 sem="dtab", inc=16)
        NTAB = P.cnt["dtab"]

        for v, i in CONSTS.items():
            P.do("vector", lambda en, ap=consts[:, i:i + 1], v=v: en.memset(ap, v))
        P.do("vector", lambda en: en.memset(ones1f[:], 1.0))
        P.do("vector", lambda en: en.memset(ones128b[:], 1.0))
        tok_setup = P.do("vector", lambda en: en.memset(ones32b[:], 1.0))


        with nc.psum_tensor("ps_bv", [PD, 512], F32) as ps_bv:
            Wv = min(512, KVDIM)
            tok_prev = None
            for i in range(KVDIM // Wv):
                sl = slice(i * Wv, (i + 1) * Wv)
                tok_mm = P.do(
                    "tensor",
                    lambda t, sl=sl: t.matmul(ps_bv[:, 0:Wv], ones1f[:],
                                              bv_row[:, sl], start=True, stop=True),
                    waits=[("dtab", NTAB), ("dve", tok_setup),
                           ("dve", tok_prev) if tok_prev else None],
                    sem="pe")
                tok_prev = P.do(
                    "vector",
                    lambda v, sl=sl: v.tensor_copy(bvb[:, sl], ps_bv[:, 0:Wv]),
                    waits=[("pe", tok_mm)])
            tok_bvb = tok_prev

        P.cut("setup", stop_after)
        NTB = S // PD
        TGW = 4 * PD
        gdA = P.barrier()
        with ExitStack() as pa:
            ea = pa.enter_context
            xraw = [ea(nc.sbuf_tensor(f"xraw{i}", [PD, E], BF16)) for i in range(2)]
            stage = [ea(nc.sbuf_tensor(f"xstg{i}", [PD, FC, TGW], BF16)) for i in range(2)]
            ps_tr = [ea(nc.psum_tensor(f"ps_trA{i}", [PD, 1024], BF16)) for i in range(2)]

            tok_xr_use = {0: None, 1: None}
            tok_stg_st = {0: None, 1: None}
            tok_ptr_rd = {0: None, 1: None}
            ev_act_last = {0: None, 1: None}
            ev_dve_last = {0: None, 1: None}

            for tg in range(NTB // 4):
                gpar = tg % 2
                last_ev = None
                for ti in range(4):
                    tb = tg * 4 + ti
                    par = tb % 2
                    tok_ld = P.do(
                        "gpsimd",
                        lambda g, tb=tb, par=par: g.dma_start(
                            out=xraw[par][:], in_=x_in[tb * PD:(tb + 1) * PD, :]),
                        waits=[("pe", tok_xr_use[par]) if tok_xr_use[par] else None],
                        sem=f"dx{par}", inc=16)
                    for fc in range(FC):
                        pp = fc % 2
                        tok_tr = P.do(
                            "tensor",
                            lambda t, par=par, fc=fc, pp=pp: t.transpose(
                                ps_tr[pp][:, 0:PD],
                                xraw[par][:, fc * PD:(fc + 1) * PD], identb[:]),
                            waits=[(f"dx{par}", tok_ld), ("dtab", NTAB),
                                   tok_ptr_rd[pp]]
                                  + (gdA if tb == 0 else []),
                            sem="pe")
                        eng_ev = "scalar" if (fc % 2 == 0) else "vector"
                        def _mkcopy(gpar=gpar, fc=fc, pp=pp, ti=ti, eng_ev=eng_ev):
                            if eng_ev == "scalar":
                                return lambda s: s.copy(
                                    stage[gpar][:, fc, ti * PD:(ti + 1) * PD],
                                    ps_tr[pp][:, 0:PD])
                            return lambda v: v.tensor_copy(
                                stage[gpar][:, fc, ti * PD:(ti + 1) * PD],
                                ps_tr[pp][:, 0:PD])
                        tok_ev = P.do(
                            eng_ev,
                            _mkcopy(),
                            waits=[("pe", tok_tr),
                                   (f"dso{gpar}", tok_stg_st[gpar])
                                   if tok_stg_st[gpar] else None])
                        if eng_ev == "scalar":
                            tok_ptr_rd[pp] = ("act", tok_ev)
                            ev_act_last[gpar] = tok_ev
                        else:
                            tok_ptr_rd[pp] = ("dve", tok_ev)
                            ev_dve_last[gpar] = tok_ev
                        last_ev = tok_ptr_rd[pp]
                    tok_xr_use[par] = P.cnt["pe"]
                nfc = max(1, FC // 4)
                for j in range(0, FC, nfc):
                    tok_st = P.do(
                        "sync",
                        lambda s, tg=tg, gpar=gpar, j=j, nfc=nfc: s.dma_start(
                            out=xT_r[:, j:j + nfc, tg * TGW:(tg + 1) * TGW],
                            in_=stage[gpar][:, j:j + nfc, :]),
                        waits=[last_ev,
                               ("act", ev_act_last[gpar]) if ev_act_last[gpar] else None,
                               ("dve", ev_dve_last[gpar]) if ev_dve_last[gpar] else None],
                        sem=f"dso{gpar}", inc=16)
                tok_stg_st[gpar] = tok_st
            NDSO0 = P.cnt.get("dso0", 0)
            NDSO1 = P.cnt.get("dso1", 0)

        P.cut("A", stop_after)
        def proj_pass(kind):
            gdB = P.barrier()
            with ExitStack() as pb:
                eb = pb.enter_context
                if kind == "K":
                    wb = eb(nc.sbuf_tensor("wkb", [PD, FC, KVDIM], BF16))
                    w_src = wk_in.rearrange("(fc p) kv -> p fc kv", p=PD)
                    cosk = eb(nc.sbuf_tensor("cosk_s", [PD, SBT], F32))
                    sink = eb(nc.sbuf_tensor("sink_s", [PD, SBT], F32))
                    kraw = [eb(nc.sbuf_tensor(f"kraw{i}", [PD, 512], F32)) for i in range(2)]
                    rsc = [eb(nc.sbuf_tensor(f"rsc{i}", [PD, 512], F32)) for i in range(2)]
                    ps_sw = [eb(nc.psum_tensor(f"ps_swK{i}", [PD, 512], F32))
                             for i in range(2)]
                    kst = [eb(nc.sbuf_tensor(f"kst{i}", [PD, SBT], BF16)) for i in range(2)]

                else:
                    wb = eb(nc.sbuf_tensor("wvb", [PD, FC, KVDIM], BF16))
                    w_src = wv_in.rearrange("(fc p) kv -> p fc kv", p=PD)
                    TB_G = min(8, SBT // PD)
                    vst = [eb(nc.sbuf_tensor(f"vst{i}", [PD, TB_G, KVDIM], BF16))
                           for i in range(2)]
                xsup = eb(nc.sbuf_tensor(f"xsup{kind}", [PD, FC, SBT], BF16))
                psb = [eb(nc.psum_tensor(f"psB{kind}{i}", [PD, 512], F32))
                       for i in range(4)]
                ntab_now = P.cnt["dtab"]

                wsem = f"dw{kind}"
                step = max(1, FC // 4)
                for fc2 in range(0, FC, step):
                    P.do("gpsimd", lambda g, fc2=fc2: g.dma_start(
                        out=wb[:, fc2:fc2 + step, :], in_=w_src[:, fc2:fc2 + step, :]),
                        waits=gdB + [("dks0", P.cnt.get("dks0", 0)),
                                     ("dks1", P.cnt.get("dks1", 0))],
                        sem=wsem, inc=16)
                nw = P.cnt[wsem]

                psb_free = [None] * 4
                tok_stage_st = {0: None, 1: None}
                st_sem = "dks" if kind == "K" else "dvs"
                xsup_pe_done = None
                kraw_rd = {0: None, 1: None}
                sw_free = {0: None, 1: None}
                ck_rope_done = None

                for sb in range(NSB):
                    ntg = (sb + 1) * SBT // (4 * PD)
                    d0 = 16 * 4 * ((ntg + 1) // 2)
                    d1 = 16 * 4 * (ntg // 2)
                    nl = max(1, FC // 8)
                    for j in range(0, FC, nl):
                        P.do("sync", lambda s, sb=sb, j=j, nl=nl: s.dma_start(
                            out=xsup[:, j:j + nl, :],
                            in_=xT_r[:, j:j + nl, sb * SBT:(sb + 1) * SBT]),
                            waits=gdB + [("dso0", min(d0, NDSO0)),
                                   ("dso1", min(d1, NDSO1)),
                                   ("pe", xsup_pe_done) if xsup_pe_done else None],
                            sem=f"dxs{kind}", inc=16)
                    tok_xs = P.cnt[f"dxs{kind}"]

                    if kind == "K":
                        P.do("sync", lambda s, sb=sb: s.dma_start(
                            out=cosk[:], in_=cosk_in[:, sb * SBT:(sb + 1) * SBT]),
                            waits=gdB + [("dso0", NDSO0), ("dso1", NDSO1),
                                         ("dve", ck_rope_done) if ck_rope_done else None],
                            sem="dck", inc=16)
                        P.do("sync", lambda s, sb=sb: s.dma_start(
                            out=sink[:], in_=sink_in[:, sb * SBT:(sb + 1) * SBT]),
                            sem="dck", inc=16)
                        tok_ck = P.cnt["dck"]
                        TQK = SBT // 512
                        for g in range(KVH):
                            spar = (sb * KVH + g) % 2
                            last_rope = None
                            for tqp in range(0, max(2, TQK), 2):
                                bks = [tqp % 4, (tqp % 4) + 1][:min(2, TQK)]
                                for fc in range(FC):
                                    for jj, bk in enumerate(bks):
                                        P.do("tensor", lambda t, g=g, fc=fc, tq=tqp + jj, bk=bk: t.matmul(
                                            psb[bk][:],
                                            wb[:, fc, g * PD:(g + 1) * PD],
                                            xsup[:, fc, tq * 512:(tq + 1) * 512],
                                            start=(fc == 0), stop=(fc == FC - 1)),
                                            waits=[("dxs" + kind, tok_xs), (wsem, nw),
                                                   psb_free[bk] if fc == 0 else None],
                                            sem="pe")
                                tok_mm = P.cnt["pe"]
                                for jj, bk in enumerate(bks):
                                    tq = tqp + jj
                                    rp = tq % 2
                                    tok_ev = P.do("scalar", lambda s, bk=bk, g=g, rp=rp: s.add(
                                        kraw[rp][:], psb[bk][:], bkT[:, g:g + 1]),
                                        waits=[("pe", tok_mm), ("dtab", ntab_now)]
                                              + (kraw_rd[rp] or []))
                                    psb_free[bk] = ("act", tok_ev)
                                    cs = slice(tq * 512, (tq + 1) * 512)
                                    dst = kst[spar][:, tq * 512:(tq + 1) * 512]
                                    sp = tq % 2
                                    tok_sw = P.do("tensor", lambda t, rp=rp, sp=sp: t.matmul(
                                        ps_sw[sp][:], permf[:], kraw[rp][:],
                                        start=True, stop=True, skip_group_check=True),
                                        waits=[("act", tok_ev), ("dtab", NTAB),
                                               sw_free[sp]],
                                        sem="pe")
                                    w0 = [("act", tok_ev), ("dck", tok_ck),
                                          (st_sem + str(spar), tok_stage_st[spar])
                                          if (tok_stage_st[spar] and tq == 0) else None]
                                    tp1 = P.do("vector", lambda v, rp=rp, cs=cs: v.tensor_mul(
                                        rsc[0][:], kraw[rp][:], cosk[:, cs]),
                                        waits=w0)
                                    tp2 = P.do("vector", lambda v, sp=sp, cs=cs: v.tensor_mul(
                                        rsc[1][:], ps_sw[sp][:], sink[:, cs]),
                                        waits=[("pe", tok_sw)])
                                    sw_free[sp] = ("dve", tp2)
                                    to1 = P.do("vector", lambda v, dst=dst: v.tensor_add(
                                        dst[:], rsc[0][:], rsc[1][:]))
                                    kraw_rd[rp] = [("pe", tok_sw), ("dve", tp1)]
                                    last_rope = to1
                                    ck_rope_done = to1
                            tok_st = P.do("sync", lambda s, g=g, sb=sb, spar=spar: s.dma_start(
                                out=kT_dram.ap()[g * PD:(g + 1) * PD,
                                                 sb * SBT:(sb + 1) * SBT],
                                in_=kst[spar][:]),
                                waits=[("dve", last_rope)],
                                sem=st_sem + str(spar), inc=16)
                            tok_stage_st[spar] = tok_st
                    else:
                        NTBS = SBT // PD
                        TB_G = min(8, NTBS)
                        NDV = max(1, KVDIM // 512)
                        Wv = min(512, KVDIM)
                        for tb8 in range(0, NTBS, TB_G):
                            par = (sb * (NTBS // TB_G) + tb8 // TB_G) % 2
                            last_vev = None
                            for tb in range(tb8, tb8 + TB_G):
                                banks = [(tb % 2) * 2 + i for i in range(NDV)]
                                for fc in range(FC):
                                    for dvh in range(NDV):
                                        bk = banks[dvh]
                                        P.do("tensor", lambda t, tb=tb, fc=fc, dvh=dvh, bk=bk: t.matmul(
                                            psb[bk][:, 0:Wv],
                                            xsup[:, fc, tb * PD:(tb + 1) * PD],
                                            wb[:, fc, dvh * 512:dvh * 512 + Wv],
                                            start=(fc == 0), stop=(fc == FC - 1)),
                                            waits=[("dxs" + kind, tok_xs), (wsem, nw),
                                                   psb_free[bk] if fc == 0 else None,
                                                   (st_sem + str(par), tok_stage_st[par])
                                                   if (fc == 0 and tb == tb8 and tok_stage_st[par]) else None],
                                            sem="pe")
                                tok_mm = P.cnt["pe"]
                                for dvh in range(NDV):
                                    bk = banks[dvh]
                                    tok_ev = P.do("vector", lambda v, tb=tb, tb8=tb8, dvh=dvh, bk=bk, par=par: v.tensor_add(
                                        vst[par][:, tb - tb8, dvh * 512:dvh * 512 + Wv],
                                        psb[bk][:, 0:Wv],
                                        bvb[:, dvh * 512:dvh * 512 + Wv]),
                                        waits=[("pe", tok_mm), ("dve", tok_bvb)])
                                    psb_free[bk] = ("dve", tok_ev)
                                    last_vev = tok_ev
                            tok_st = P.do("sync", lambda s, sb=sb, tb8=tb8, par=par: s.dma_start(
                                out=v_w[:, sb * NTBS + tb8:sb * NTBS + tb8 + TB_G, :],
                                in_=vst[par][:]),
                                waits=[("dve", last_vev)],
                                sem=st_sem + str(par), inc=16)
                            tok_stage_st[par] = tok_st
                    xsup_pe_done = P.cnt["pe"]

        proj_pass("K")
        proj_pass("V")
        NKS0, NKS1 = P.cnt.get("dks0", 0), P.cnt.get("dks1", 0)
        NVS0, NVS1 = P.cnt.get("dvs0", 0), P.cnt.get("dvs1", 0)

        if dump in ("kT", "v", "xT"):
            if dump == "kT":
                o = out_t.rearrange("(a b) t -> a (b t)", b=8192 // T)
                P.do("gpsimd", lambda g, o=o: g.dma_start(
                    out=o, in_=kT_dram.ap()[0:KVDIM // 2, :]),
                    waits=[("dks0", P.cnt.get("dks0", 0)), ("dks1", P.cnt.get("dks1", 0))],
                    sem="ddump", inc=16)
            elif dump == "v":
                P.do("gpsimd", lambda g: g.dma_start(
                    out=out_t, in_=v_dram.ap()[0:E, :]),
                    waits=[("dvs0", P.cnt.get("dvs0", 0)), ("dvs1", P.cnt.get("dvs1", 0))],
                    sem="ddump", inc=16)
            else:
                P.do("gpsimd", lambda g: g.dma_start(
                    out=out_t, in_=xT_dram.ap()[:, 0:T]),
                    waits=[("dso0", NDSO0), ("dso1", NDSO1)],
                    sem="ddump", inc=16)
            P.do("gpsimd", lambda g: g.nop(), waits=[("ddump", 16)])
            P.cut("DUMP", "DUMP")
        P.cut("B", stop_after)
        sT = e(nc.sbuf_tensor("sT", [PD, FC, T], BF16))
        gdT = P.barrier()
        nl = max(1, FC // 4)
        for j in range(0, FC, nl):
            P.do("sync", lambda s, j=j, nl=nl: s.dma_start(
                out=sT[:, j:j + nl, :], in_=xT_r[:, j:j + nl, 0:T]),
                waits=gdT + [("dso0", NDSO0), ("dso1", NDSO1),
                             ("dks0", NKS0), ("dks1", NKS1),
                             ("dvs0", NVS0), ("dvs1", NVS1)],
                sem="dsT", inc=16)
        NDST = P.cnt["dsT"]

        pcd = ExitStack()
        qT = pcd.enter_context(nc.sbuf_tensor("qT", [PD, H, T], BF16))
        cosq = pcd.enter_context(nc.sbuf_tensor("cosq_s", [PD, T], F32))
        sinq = pcd.enter_context(nc.sbuf_tensor("sinq_s", [PD, T], F32))
        gdC = P.barrier()
        P.do("sync", lambda s: s.dma_start(out=cosq[:], in_=cosq_in),
             waits=gdC + [("dks0", NKS0), ("dks1", NKS1),
                          ("dvs0", NVS0), ("dvs1", NVS1)],
             sem="dcq", inc=16)
        P.do("sync", lambda s: s.dma_start(out=sinq[:], in_=sinq_in),
             sem="dcq", inc=16)
        NCQ = P.cnt["dcq"]
        with ExitStack() as pc:
            ec = pc.enter_context
            HB = 2
            wqb = [ec(nc.sbuf_tensor(f"wqb{i}", [PD, FC, HB * PD], BF16)) for i in range(2)]
            qraw = [ec(nc.sbuf_tensor(f"qraw{i}", [PD, T], F32)) for i in range(2)]
            rscq = [ec(nc.sbuf_tensor(f"rscq{i}", [PD, T], F32)) for i in range(2)]
            psc = [ec(nc.psum_tensor(f"psC{i}", [PD, 512], F32)) for i in range(4)]
            ps_swq = [ec(nc.psum_tensor(f"ps_swQ{i}", [PD, 512], F32)) for i in range(2)]
            wq_r = wq_in.rearrange("(fc p) o -> p fc o", p=PD)

            tok_wq = {0: None, 1: None}
            tok_wq_use = {0: None, 1: None}
            psc_free = [None] * 4
            qraw_free = {0: None, 1: None}
            swq_free = {0: None, 1: None}
            qsw_last = {0: None, 1: None}

            for blki in range(H // HB):
                par = blki % 2
                nfc = max(1, FC // 4)
                for j in range(0, FC, nfc):
                    tok_wq[par] = P.do("gpsimd", lambda g, blki=blki, par=par, j=j, nfc=nfc: g.dma_start(
                        out=wqb[par][:, j:j + nfc, :],
                        in_=wq_r[:, j:j + nfc, blki * HB * PD:(blki + 1) * HB * PD]),
                        waits=[("pe", tok_wq_use[par]) if tok_wq_use[par] else None]
                              + (gdC + [("dks0", NKS0), ("dks1", NKS1),
                                        ("dvs0", NVS0), ("dvs1", NVS1)]
                                 if blki < 2 else []),
                        sem=f"dwq{par}", inc=16)
                for hh in range(HB):
                    h = blki * HB + hh
                    hp = h % 2
                    tok_ev = None
                    for qb in range(NQB):
                        bk = (h % 2) * 2 + (qb % 2)
                        for fc in range(FC):
                            P.do("tensor", lambda t, fc=fc, hh=hh, qb=qb, par=par, bk=bk: t.matmul(
                                psc[bk][:, 0:QB],
                                wqb[par][:, fc, hh * PD:(hh + 1) * PD],
                                sT[:, fc, qb * QB:(qb + 1) * QB],
                                start=(fc == 0), stop=(fc == FC - 1)),
                                waits=[(f"dwq{par}", tok_wq[par]), ("dsT", NDST),
                                       psc_free[bk] if fc == 0 else None,
                                       ("dve", qraw_free[hp])
                                       if (fc == 0 and qb == 0 and qraw_free[hp]) else None,
                                       qsw_last[hp] if (fc == 0 and qb == 0) else None],
                                sem="pe")
                        tok_mm = P.cnt["pe"]
                        tok_ev = P.do("scalar", lambda s, h=h, qb=qb, hp=hp, bk=bk: s.add(
                            qraw[hp][:, qb * QB:(qb + 1) * QB], psc[bk][:, 0:QB],
                            bqT[:, h:h + 1]),
                            waits=[("pe", tok_mm)])
                        psc_free[bk] = ("act", tok_ev)
                    dst = qT[:, h, :]
                    tok_sw = {}
                    for qb in range(NQB):
                        sp = qb % 2
                        tok_sw[qb] = P.do("tensor", lambda t, hp=hp, qb=qb, sp=sp: t.matmul(
                            ps_swq[sp][:, 0:QB], permf[:],
                            qraw[hp][:, qb * QB:(qb + 1) * QB],
                            start=True, stop=True, skip_group_check=True),
                            waits=[("act", tok_ev), ("dtab", NTAB),
                                   swq_free[sp]],
                            sem="pe")
                    P.do("vector", lambda v, hp=hp: v.tensor_mul(
                        rscq[0][:], qraw[hp][:], cosq[:]),
                        waits=[("act", tok_ev), ("dcq", NCQ)])
                    for qb in range(NQB):
                        sp = qb % 2
                        qsl = slice(qb * QB, (qb + 1) * QB)
                        t2 = P.do("vector", lambda v, sp=sp, qsl=qsl: v.tensor_mul(
                            rscq[1][:, qsl], ps_swq[sp][:, 0:QB], sinq[:, qsl]),
                            waits=[("pe", tok_sw[qb])])
                        swq_free[sp] = ("dve", t2)
                    tm4 = P.do("vector", lambda v, dst=dst: v.tensor_add(
                        dst[:], rscq[0][:], rscq[1][:]))
                    qraw_free[hp] = tm4
                    qsw_last[hp] = ("pe", tok_sw[NQB - 1])
                tok_wq_use[par] = P.cnt["pe"]
            tok_qT_dve = P.cnt["dve"]

        if dump == "qT":
            o = out_t.rearrange("(h p) t -> p h t", p=PD)
            P.do("gpsimd", lambda g, o=o: g.dma_start(out=o, in_=qT[:]),
                 waits=[("dve", P.cnt["dve"])], sem="ddump", inc=16)
            P.do("gpsimd", lambda g: g.nop(), waits=[("ddump", 16)])
            P.cut("DUMP", "DUMP")
        P.cut("C", stop_after)
        gdD = P.barrier()
        with ExitStack() as pd:
            ed = pd.enter_context
            kS = ed(nc.sbuf_tensor("kS", [PD, S], BF16))
            vS = ed(nc.sbuf_tensor("vS", [PD, KC, D], BF16))
            pT = [ed(nc.sbuf_tensor(f"pT{i}", [PD, PW], BF16)) for i in range(3)]
            acc = [ed(nc.sbuf_tensor(f"acc{i}", [PD, PW], F32)) for i in range(2)]
            accb = ed(nc.sbuf_tensor("accb", [PD, PW], BF16))
            rl = ed(nc.sbuf_tensor("rl", [1, QB], F32))
            bcN = ed(nc.sbuf_tensor("bcN", [PD, QB], F32))
            tmpf = ed(nc.sbuf_tensor("tmpf", [PD, QB], F32))
            tmpb = ed(nc.sbuf_tensor("tmpb", [PD, QB], BF16))
            expb_t = ed(nc.sbuf_tensor("expb_t", [PD, 1], F32))
            exps_t = ed(nc.sbuf_tensor("exps_t", [PD, 1], F32))
            sc = [ed(nc.psum_tensor(f"scD{i}", [PD, PW], F32)) for i in range(2)]
            oPs2 = [ed(nc.psum_tensor(f"oPs{i}", [PD, 512], F32)) for i in range(2)]
            lpsb = ed(nc.psum_tensor("lps", [PD, 512], F32))
            bcpb = ed(nc.psum_tensor("bcp", [PD, 512], F32))

            P.do("vector", lambda v: v.memset(expb_t[:], cfg.expbias))
            tok_expc = P.do("vector", lambda v: v.memset(exps_t[:], cfg.scale))
            tok_kv_pe_done = None
            sc_free = {0: None, 1: None}
            pT_free = [None, None, None]
            oPs_free = None
            prev_tree = {"accv": None, "accp": None}

            for g in range(KVH):
                P.do("sync", lambda s, g=g: s.dma_start(
                    out=kS[:], in_=kT_dram.ap()[g * PD:(g + 1) * PD, :]),
                    waits=gdD + [("dks0", NKS0), ("dks1", NKS1),
                           ("dvs0", NVS0), ("dvs1", NVS1),
                           ("pe", tok_kv_pe_done) if tok_kv_pe_done else None],
                    sem="dkv", inc=16)
                nkcl = max(1, KC // 8)
                for j in range(0, KC, nkcl):
                    P.do("sync", lambda s, g=g, j=j, nkcl=nkcl: s.dma_start(
                        out=vS[:, j:j + nkcl, :],
                        in_=v_w[:, j:j + nkcl, g * PD:(g + 1) * PD]),
                        sem="dkv", inc=16)
                tok_kv = P.cnt["dkv"]
                if g == 0:
                    P.do("sync", lambda s: s.nop(), waits=[("dkv", tok_kv)])
                    if dump == "vS":
                        o = out_t.rearrange("(a p) t -> p a t", p=PD)
                        vv = vS.ap().rearrange("p (c8 c) d -> p c8 (c d)", c8=8)
                        P.do("gpsimd", lambda g2, o=o, vv=vv: g2.dma_start(
                            out=o[:, 0:8, :], in_=vv),
                            waits=[("dkv", tok_kv)], sem="ddump", inc=16)
                        P.do("gpsimd", lambda g2: g2.nop(), waits=[("ddump", 16)])
                        P.cut("DUMP", "DUMP")
                    P.cut("D0", stop_after)

                for qb in range(NQB):
                    qsl = slice(qb * QB, (qb + 1) * QB)
                    for pairi in range(NPAIR):
                        h0 = g * REP + pairi * 2
                        h1 = h0 + 1
                        first_pair = (g == 0 and qb == 0 and pairi == 0)
                        tok_z0 = P.do("vector", lambda en: en.memset(acc[0][:], 0.0),
                                      waits=[("dve", prev_tree["accv"])
                                             if prev_tree["accv"] else None]
                                            + (gdD if first_pair else []))
                        if g == 0 and qb == 0 and pairi == 0:
                            P.cut("DM0", stop_after)
                        tok_exp = {}
                        tok_accop = {}
                        for kc in range(KC):
                            if g == 0 and qb == 0 and pairi == 0 and kc in (1, 2, 4, 16):
                                P.cut(f"DK{kc}", stop_after)
                            if (g == 0 and qb == 0 and pairi == 0 and kc == 16
                                    and stop_after in ("DNOPV", "DQK", "DQE")):
                                P.cut(stop_after, stop_after)
                            par = kc % 2
                            pi = kc % 3
                            P.do("tensor", lambda t, kc=kc, h0=h0, par=par, qsl=qsl: t.matmul(
                                sc[par][:, 0:QB], kS[:, kc * PD:(kc + 1) * PD],
                                qT[:, h0, qsl], start=True, stop=True,
                                skip_group_check=True),
                                waits=[("dkv", tok_kv), ("dve", tok_qT_dve),
                                       sc_free[par]],
                                sem="pe")
                            tok_qk = P.do("tensor", lambda t, kc=kc, h1=h1, par=par, qsl=qsl: t.matmul(
                                sc[par][:, QB:PW], kS[:, kc * PD:(kc + 1) * PD],
                                qT[:, h1, qsl], start=True, stop=True,
                                skip_group_check=True),
                                sem="pe")
                            if kc >= 1 and stop_after not in ("DNOPV", "DQK", "DQE"):
                                kcp = kc - 1
                                pip = kcp % 3
                                P.do("tensor", lambda t, kcp=kcp, pip=pip: t.matmul(
                                    oPs2[0][:, 0:QB], vS[:, kcp, :], pT[pip][:, 0:QB],
                                    start=(kcp == 0), stop=False,
                                    skip_group_check=True),
                                    waits=[("act", tok_exp[kcp]),
                                           oPs_free if kcp == 0 else None],
                                    sem="pe")
                                P.do("tensor", lambda t, kcp=kcp, pip=pip: t.matmul(
                                    oPs2[1][:, 0:QB], vS[:, kcp, :], pT[pip][:, QB:PW],
                                    start=(kcp == 0), stop=False,
                                    skip_group_check=True),
                                    sem="pe")
                            if g == 0 and qb == 0 and pairi == 0 and kc == 0:
                                if stop_after == "DQ1W":
                                    P.do("sync", lambda s: s.nop(),
                                         waits=[("pe", tok_qk)])
                                    P.cut("DQ1W", stop_after)
                                P.cut("DQ1", stop_after)
                            if stop_after == "DQK":
                                sc_free[par] = ("pe", tok_qk)
                                continue
                            tok_exp[kc] = P.do("scalar", lambda s, par=par, pi=pi: s.activation(
                                pT[pi][:], sc[par][:], AF.Exp,
                                bias=expb_t[:], scale=exps_t[:]),
                                waits=[("pe", tok_qk), pT_free[pi],
                                       ("dve", tok_expc)])
                            sc_free[par] = ("act", tok_exp[kc])
                            if g == 0 and qb == 0 and pairi == 0 and kc == 0:
                                P.cut("DE1", stop_after)
                            if stop_after == "DQE":
                                pT_free[pi] = ("act", tok_exp[kc])
                                continue
                            tok_accop[kc] = P.do("vector", lambda en, pi=pi: en.tensor_add(
                                acc[0][:], acc[0][:], pT[pi][:]),
                                waits=[("act", tok_exp[kc]),
                                       ("dve", tok_z0) if kc == 0 else None])
                            pT_free[pi] = ("dve", tok_accop[kc])
                        if g == 0 and qb == 0 and pairi == 0:
                            P.cut("D1", stop_after)
                        kcp = KC - 1
                        pip = kcp % 3
                        P.do("tensor", lambda t, kcp=kcp, pip=pip: t.matmul(
                            oPs2[0][:, 0:QB], vS[:, kcp, :], pT[pip][:, 0:QB],
                            start=(kcp == 0), stop=True, skip_group_check=True),
                            waits=[("act", tok_exp.get(kcp)),
                                   oPs_free if kcp == 0 else None],
                            sem="pe")
                        tok_pvl = P.do("tensor", lambda t, kcp=kcp, pip=pip: t.matmul(
                            oPs2[1][:, 0:QB], vS[:, kcp, :], pT[pip][:, QB:PW],
                            start=(kcp == 0), stop=True, skip_group_check=True),
                            sem="pe")

                        tok_t3 = P.do("vector", lambda v: v.tensor_copy(
                            accb[:], acc[0][:]),
                            waits=[("dve", tok_accop.get(KC - 1))])
                        prev_tree["accv"] = tok_t3
                        last_cp = None
                        for hh, h in enumerate((h0, h1)):
                            hsl = slice(hh * QB, (hh + 1) * QB)
                            tok_lp = P.do("tensor", lambda t, hsl=hsl: t.matmul(
                                lpsb[0:1, 0:QB], ones128b[:], accb[:, hsl],
                                start=True, stop=True, skip_group_check=True),
                                waits=[("dve", tok_t3)], sem="pe")
                            tok_rl = P.do("vector", lambda v: v.reciprocal(
                                rl[:], lpsb[0:1, 0:QB]),
                                waits=[("pe", tok_lp)])
                            tok_bc = P.do("tensor", lambda t: t.matmul(
                                bcpb[:, 0:QB], ones1f[:], rl[:],
                                start=True, stop=True, skip_group_check=True),
                                waits=[("dve", tok_rl)], sem="pe")
                            tok_bcc = P.do("vector", lambda v: v.tensor_copy(
                                bcN[:], bcpb[:, 0:QB]),
                                waits=[("pe", tok_bc)])
                            P.do("vector", lambda v, hh=hh: v.tensor_mul(
                                tmpf[:], oPs2[hh][:, 0:QB], bcN[:]),
                                waits=[("pe", tok_pvl)])
                            P.do("vector", lambda v, h=h, qsl=qsl: v.tensor_add(
                                tmpb[:], tmpf[:], sT[:, h, qsl]))
                            last_cp = P.do("vector", lambda v, h=h, qsl=qsl: v.tensor_copy(
                                sT[:, h, qsl], tmpb[:]))
                        oPs_free = ("dve", last_cp)
                        if g == 0 and qb == 0 and pairi == 0 and dump == "lpath":
                            t_e1 = P.do("vector", lambda v: v.tensor_copy(
                                tmpf[:], oPs2[0][:, 0:QB]))
                            P.do("gpsimd", lambda g2: g2.dma_start(
                                out=out_t[0:PD, 0:QB], in_=tmpf[:]),
                                waits=[("dve", t_e1)], sem="ddump", inc=16)
                            P.do("gpsimd", lambda g2: g2.dma_start(
                                out=out_t[PD:2 * PD, 0:PW], in_=accb[:]),
                                waits=[("dve", t_e1)], sem="ddump", inc=16)
                            P.do("gpsimd", lambda g2: g2.dma_start(
                                out=out_t[256:257, 0:QB], in_=rl[:]),
                                waits=[("dve", t_e1)], sem="ddump", inc=16)
                            P.do("gpsimd", lambda g2: g2.dma_start(
                                out=out_t[384:384 + PD, 0:QB], in_=bcN[:]),
                                waits=[("dve", t_e1)], sem="ddump", inc=16)
                            P.do("gpsimd", lambda g2: g2.nop(), waits=[("ddump", 64)])
                            P.cut("DUMP", "DUMP")
                        if g == 0 and qb == 0 and pairi == 0:
                            P.cut("D2", stop_after)
                tok_kv_pe_done = P.cnt["pe"]
                if g == 0 and dump == "vSbusy":
                    o = out_t.rearrange("(a p) t -> p a t", p=PD)
                    vv = vS.ap().rearrange("p (c8 c) d -> p c8 (c d)", c8=8)
                    P.do("gpsimd", lambda g2, o=o, vv=vv: g2.dma_start(
                        out=o[:, 0:8, :], in_=vv),
                        waits=[("pe", tok_kv_pe_done), ("dve", P.cnt["dve"])],
                        sem="ddump", inc=16)
                    P.do("gpsimd", lambda g2: g2.nop(), waits=[("ddump", 16)])
                    P.cut("DUMP", "DUMP")
        pcd.close()

        if dump == "sT":
            o = out_t.rearrange("(h p) t -> p h t", p=PD)
            P.do("gpsimd", lambda g, o=o: g.dma_start(out=o, in_=sT[:]),
                 waits=[("dve", P.cnt["dve"])], sem="ddump", inc=16)
            P.do("gpsimd", lambda g: g.nop(), waits=[("ddump", 16)])
            P.cut("DUMP", "DUMP")
        P.cut("D", stop_after)
        lnt = e(nc.sbuf_tensor("lntmp", [PD, QB], F32))

        def emit_ln(src_chunks, apply_out, tag, entry_list):
            with ExitStack() as pl:
                el = pl.enter_context
                sq = [el(nc.sbuf_tensor(f"sq{tag}{i}", [PD, QB], BF16)) for i in range(2)]
                rows = el(nc.sbuf_tensor(f"rows{tag}", [1, 6 * QB], F32))
                a_sb = el(nc.sbuf_tensor(f"asb{tag}", [PD, QB], F32))
                b_sb = el(nc.sbuf_tensor(f"bsb{tag}", [PD, QB], F32))
                eps_t = el(nc.sbuf_tensor(f"eps{tag}", [1, 1], F32))
                nh_t = el(nc.sbuf_tensor(f"nh{tag}", [1, 1], F32))
                ps1 = el(nc.psum_tensor(f"ps1{tag}", [PD, 512], F32))
                ps2 = el(nc.psum_tensor(f"ps2{tag}", [PD, 512], F32))
                psb2 = el(nc.psum_tensor(f"psbb{tag}", [PD, 512], F32))

                mrow = rows[:, 0:QB]
                msrow = rows[:, QB:2 * QB]
                m2row = rows[:, 2 * QB:3 * QB]
                varrow = rows[:, 3 * QB:4 * QB]
                arow = rows[:, 4 * QB:5 * QB]
                brow = rows[:, 5 * QB:6 * QB]
                sq_free = {0: None, 1: None}
                P.do("vector", lambda v: v.memset(eps_t[:], 1e-5))
                tok_lnc = P.do("vector", lambda v: v.memset(nh_t[:], -0.5))

                for qb in range(NQB):
                    qsl = slice(qb * QB, (qb + 1) * QB)
                    tok_s1 = tok_s2 = None
                    for fc in range(FC):
                        sp = fc % 2
                        src = src_chunks(fc)
                        tok_sq = P.do("vector", lambda v, src=src, sp=sp, qsl=qsl: v.tensor_mul(
                            sq[sp][:], src[:, qsl], src[:, qsl]),
                            waits=entry_list + [sq_free[sp]])
                        tok_s1 = P.do("tensor", lambda t, src=src, qsl=qsl, fc=fc: t.matmul(
                            ps1[0:1, 0:QB], ones128b[:], src[:, qsl],
                            start=(fc == 0), stop=(fc == FC - 1),
                            skip_group_check=True),
                            waits=entry_list + [("dve", tok_setup)], sem="pe")
                        tok_s2 = P.do("tensor", lambda t, sp=sp, fc=fc: t.matmul(
                            ps2[0:1, 0:QB], ones128b[:], sq[sp][:],
                            start=(fc == 0), stop=(fc == FC - 1),
                            skip_group_check=True),
                            waits=[("dve", tok_sq)], sem="pe")
                        sq_free[sp] = ("pe", tok_s2)
                    inv = 1.0 / float(E)
                    P.do("vector", lambda v, inv=inv: v.tensor_scalar_mul(
                        mrow, ps1[0:1, 0:QB], inv), waits=[("pe", tok_s1)])
                    P.do("vector", lambda v, inv=inv: v.tensor_scalar_mul(
                        msrow, ps2[0:1, 0:QB], inv), waits=[("pe", tok_s2)])
                    P.do("vector", lambda v: v.tensor_mul(m2row, mrow, mrow))
                    t_var = P.do("vector", lambda v: v.tensor_sub(varrow, msrow, m2row))
                    t_ln = P.do("scalar", lambda s: s.activation(
                        arow, varrow, AF.Ln, bias=eps_t[0:1, :], scale=1.0),
                        waits=[("dve", t_var), ("dve", tok_lnc)])
                    t_rs = P.do("scalar", lambda s: s.activation(
                        arow, arow, AF.Exp, bias=0.0, scale=nh_t[0:1, :]))
                    t_mr = P.do("vector", lambda v: v.tensor_mul(brow, mrow, arow),
                                waits=[("act", t_rs)])
                    t_nb = P.do("vector", lambda v: v.tensor_scalar_mul(brow, brow, -1.0))
                    t_bca = P.do("tensor", lambda t: t.matmul(
                        psb2[:, 0:QB], ones1f[:], arow, start=True, stop=True,
                        skip_group_check=True),
                        waits=[("dve", t_nb)], sem="pe")
                    t_ca = P.do("vector", lambda v: v.tensor_copy(a_sb[:], psb2[:, 0:QB]),
                                waits=[("pe", t_bca)])
                    t_bcb = P.do("tensor", lambda t: t.matmul(
                        psb2[:, 0:QB], ones1f[:], brow, start=True, stop=True,
                        skip_group_check=True),
                        waits=[("dve", t_ca)], sem="pe")
                    t_cb = P.do("vector", lambda v: v.tensor_copy(b_sb[:], psb2[:, 0:QB]),
                                waits=[("pe", t_bcb)])
                    for fc in range(FC):
                        apply_out(fc, qb, a_sb, b_sb, [("dve", t_cb)])

        def ln1_apply(fc, qb, a_sb, b_sb, waits):
            qsl = slice(qb * QB, (qb + 1) * QB)
            P.do("vector", lambda v, fc=fc, qsl=qsl: v.tensor_mul(
                lnt[:], sT[:, fc, qsl], a_sb[:]), waits=waits)
            P.do("vector", lambda v, fc=fc, qsl=qsl: v.tensor_add(
                sT[:, fc, qsl], lnt[:], b_sb[:]))

        emit_ln(lambda fc: sT[:, fc, :], ln1_apply, "1", P.barrier())
        tok_ln1 = P.cnt["dve"]

        if dump == "sTe":
            o = out_t.rearrange("(h p) t -> p h t", p=PD)
            P.do("gpsimd", lambda g, o=o: g.dma_start(out=o, in_=sT[:]),
                 waits=[("dve", P.cnt["dve"])], sem="ddump", inc=16)
            P.do("gpsimd", lambda g: g.nop(), waits=[("ddump", 16)])
            P.cut("DUMP", "DUMP")
        P.cut("E", stop_after)
        v2T = e(nc.sbuf_tensor("v2T", [PD, FC, T], BF16))
        gdF = P.barrier()
        with ExitStack() as pf:
            ef = pf.enter_context
            wpb = [ef(nc.sbuf_tensor(f"wpb{i}", [PD, E], BF16)) for i in range(2)]
            wnb = [ef(nc.sbuf_tensor(f"wnb{i}", [PD, E], BF16)) for i in range(2)]
            bin2 = ef(nc.sbuf_tensor("bin2", [PD, E], BF16))
            wd = ef(nc.sbuf_tensor("wd", [PD, E], BF16))
            wdT = [ef(nc.sbuf_tensor(f"wdT{i}", [PD, FC, PD], BF16)) for i in range(2)]
            psf = [ef(nc.psum_tensor(f"psF{i}", [PD, 512], F32)) for i in range(2)]
            pst = [ef(nc.psum_tensor(f"psFT{i}", [PD, 1024], BF16)) for i in range(2)]

            tok_w_use = {0: None, 1: None}
            tok_wd_use_any = None
            tok_wdT_use = {0: None, 1: None}
            tok_pst_rd = {0: None, 1: None}
            psf_free = {0: None, 1: None}

            for ob in range(FC):
                par = ob % 2
                t_lp = P.do("gpsimd", lambda g, ob=ob, par=par: g.dma_start(
                    out=wpb[par][:], in_=wpos_in[ob * PD:(ob + 1) * PD, :]),
                    waits=[("dve", tok_w_use[par]) if tok_w_use[par] else None]
                          + (gdF if ob < 2 else []),
                    sem=f"dfp{par}", inc=16)
                t_lnn = P.do("gpsimd", lambda g, ob=ob, par=par: g.dma_start(
                    out=wnb[par][:], in_=wneg_in[ob * PD:(ob + 1) * PD, :]),
                    sem=f"dfn{par}", inc=16)
                P.do("vector", lambda v, par=par: v.tensor_single_scalar(
                    bin2[:], wnb[par][:], 0.0, ALU.is_gt),
                    waits=[(f"dfn{par}", t_lnn)])
                P.do("vector", lambda v, par=par: v.tensor_single_scalar(
                    wd[:], wpb[par][:], 0.0, ALU.is_gt),
                    waits=[(f"dfp{par}", t_lp),
                           ("pe", tok_wd_use_any) if tok_wd_use_any else None])
                t_wd = P.do("vector", lambda v: v.tensor_sub(
                    wd[:], wd[:], bin2[:]))
                tok_w_use[par] = t_wd
                last_ev = None
                for fc in range(FC):
                    pp = fc % 2
                    t_tr = P.do("tensor", lambda t, fc=fc, pp=pp: t.transpose(
                        pst[pp][:, 0:PD], wd[:, fc * PD:(fc + 1) * PD], identb[:]),
                        waits=[("dve", t_wd),
                               ("dve", tok_pst_rd[pp]) if tok_pst_rd[pp] else None,
                               ("pe", tok_wdT_use[par])
                               if (fc == 0 and tok_wdT_use[par]) else None],
                        sem="pe")
                    t_evt = P.do("vector", lambda v, par=par, fc=fc, pp=pp: v.tensor_copy(
                        wdT[par][:, fc, :], pst[pp][:, 0:PD]),
                        waits=[("pe", t_tr)])
                    tok_pst_rd[pp] = t_evt
                    last_ev = t_evt
                tok_wd_use_any = P.cnt["pe"]
                for fc in range(FC):
                    for qb in range(NQB):
                        P.do("tensor", lambda t, par=par, fc=fc, qb=qb: t.matmul(
                            psf[qb % 2][:, 0:QB], wdT[par][:, fc, :],
                            sT[:, fc, qb * QB:(qb + 1) * QB],
                            start=(fc == 0), stop=(fc == FC - 1),
                            skip_group_check=True),
                            waits=[("dve", last_ev), ("dve", tok_ln1),
                                   psf_free[qb % 2] if fc == 0 else None],
                            sem="pe")
                tok_mm = P.cnt["pe"]
                tok_wdT_use[par] = tok_mm
                for qb in range(NQB):
                    qsl = slice(qb * QB, (qb + 1) * QB)
                    t_ev = P.do("vector", lambda v, ob=ob, qb=qb, qsl=qsl: v.tensor_add(
                        v2T[:, ob, qsl], psf[qb % 2][:, 0:QB], sT[:, ob, qsl]),
                        waits=[("pe", tok_mm)])
                    psf_free[qb % 2] = ("dve", t_ev)

        if dump == "v2T":
            o = out_t.rearrange("(h p) t -> p h t", p=PD)
            P.do("gpsimd", lambda g, o=o: g.dma_start(out=o, in_=v2T[:]),
                 waits=[("dve", P.cnt["dve"])], sem="ddump", inc=16)
            P.do("gpsimd", lambda g: g.nop(), waits=[("ddump", 16)])
            P.cut("DUMP", "DUMP")
        P.cut("F", stop_after)
        outstage = [e(nc.sbuf_tensor(f"outs{i}", [PD, QB], F32)) for i in range(2)]
        out_st = {0: None, 1: None}
        _ln2_n = [0]

        def ln2_apply(fc, qb, a_sb, b_sb, waits):
            par = _ln2_n[0] % 2
            _ln2_n[0] += 1
            qsl = slice(qb * QB, (qb + 1) * QB)
            w = list(waits)
            if out_st[par] is not None:
                w.append((f"dob{par}", out_st[par]))
            P.do("vector", lambda v, fc=fc, qsl=qsl: v.tensor_mul(
                lnt[:], v2T[:, fc, qsl], a_sb[:]), waits=w)
            t2 = P.do("vector", lambda v, par=par: v.tensor_add(
                outstage[par][:], lnt[:], b_sb[:]))
            out_st[par] = P.do("sync", lambda s, fc=fc, qsl=qsl, par=par: s.dma_start(
                out=out_t[fc * PD:(fc + 1) * PD, qsl], in_=outstage[par][:]),
                waits=[("dve", t2)], sem=f"dob{par}", inc=16)

        emit_ln(lambda fc: v2T[:, fc, :], ln2_apply, "2", P.barrier())

        P.do("sync", lambda s: s.nop(),
             waits=[("dob0", P.cnt.get("dob0", 0)), ("dob1", P.cnt.get("dob1", 0))])

        blk = e(nc.Block())
        P.finalize(blk)
    return nc



def _rope_tables(positions):
    j = np.arange(0, D, 2, dtype=np.float32) / np.float32(D)
    inv_freq = (np.float32(1.0) / (np.float32(10000.0) ** j)).astype(np.float32)
    ang = positions.astype(np.float32)[:, None] * inv_freq[None, :]
    cos = np.cos(ang).T.astype(np.float32)
    sin = np.sin(ang).T.astype(np.float32)
    c2 = np.concatenate([cos, cos], axis=0)
    s2m = np.concatenate([-sin, sin], axis=0)
    return np.ascontiguousarray(c2), np.ascontiguousarray(s2m)


def make_inputs(cfg, core, x, wq, bq, wk, bk, wv, bv, w_pos, w_neg):
    S, T = cfg.S, cfg.T
    t0 = core * T
    perm = np.concatenate([np.arange(t0, t0 + T), np.arange(0, t0),
                           np.arange(t0 + T, S)])
    xr = np.ascontiguousarray(x[perm])
    cosk, sink = _rope_tables(perm.astype(np.float32))
    return {
        "x": xr,
        "wq": wq, "wk": wk, "wv": wv,
        "wpos": w_pos, "wneg": w_neg,
        "bqT": np.ascontiguousarray(bq.reshape(cfg.FC, PD).T),
        "bkT": np.ascontiguousarray(bk.reshape(cfg.KVH, PD).T),
        "bv": np.ascontiguousarray(bv.reshape(1, cfg.KVDIM)),
        "cosq": np.ascontiguousarray(cosk[:, 0:T]),
        "sinq": np.ascontiguousarray(sink[:, 0:T]),
        "cosk": cosk, "sink": sink,
        "ident": np.eye(PD, dtype=BF),
        "perm": np.ascontiguousarray(
            np.eye(PD, dtype=np.float32)[:, list(range(64, PD)) + list(range(64))]),
    }


def kernel(L0_attn_x, L0_attn_wq, L0_attn_bq, L0_attn_wk, L0_attn_bk,
           L0_attn_wv, L0_attn_bv, L0_ffn_w_pos, L0_ffn_w_neg):
    cfg = FULL
    x = np.ascontiguousarray(np.asarray(L0_attn_x, dtype=np.float32).reshape(cfg.S, cfg.E))
    wq = np.ascontiguousarray(np.asarray(L0_attn_wq, dtype=np.float32))
    bq = np.asarray(L0_attn_bq, dtype=np.float32)
    wk = np.ascontiguousarray(np.asarray(L0_attn_wk, dtype=np.float32))
    bk = np.asarray(L0_attn_bk, dtype=np.float32)
    wv = np.ascontiguousarray(np.asarray(L0_attn_wv, dtype=np.float32))
    bv = np.asarray(L0_attn_bv, dtype=np.float32)
    wp = np.ascontiguousarray(np.asarray(L0_ffn_w_pos, dtype=np.float32))
    wn = np.ascontiguousarray(np.asarray(L0_ffn_w_neg, dtype=np.float32))

    nc = build(cfg)
    in_maps = [make_inputs(cfg, c, x, wq, bq, wk, bk, wv, bv, wp, wn)
               for c in range(cfg.R)]
    res = run_bass_kernel_spmd(nc, in_maps, core_ids=list(range(cfg.R)))
    out = np.concatenate(
        [np.asarray(res.results[c]["out_t"]).T for c in range(cfg.R)], axis=0)
    return out.reshape(1, cfg.S, cfg.E).astype(np.float32)
